# version 09
# speedup vs baseline: 1.3138x; 1.0954x over previous
import contextlib
import functools
import numpy as np
import ml_dtypes

import concourse.bass as bass
import concourse.mybir as mybir
from concourse.bass_utils import run_bass_kernel_spmd

F32 = mybir.dt.float32
BF16 = mybir.dt.bfloat16
AF = mybir.ActivationFunctionType
ALU = mybir.AluOpType
AX = mybir.AxisListType
BF = ml_dtypes.bfloat16

NCORES = 8
D = 1024
NC_D = 8
GAMMA = 0.96875


ENGS = ("sync", "gpsimd", "vector", "scalar", "tensor")


def _is_psum(ap):
    t = getattr(ap, "tensor", None)
    return t is not None and type(t).__name__ == "PSumTensorHandle"


class _Op:
    __slots__ = ("eng", "dma", "fn", "r", "w", "idx", "deps", "store", "cc")

    def __init__(self, eng, dma, fn, r, w, idx, deps, store, cc=False):
        self.eng, self.dma, self.fn = eng, dma, fn
        self.r, self.w, self.idx, self.deps, self.store = r, w, idx, deps, store
        self.cc = cc


def _nm(ap):
    if isinstance(ap, str):
        return ap
    n = getattr(ap, "name", None)
    if callable(n):
        n = n()
    return n


class KB:

    def __init__(self, nc, es):
        self.nc = nc
        self.es = es
        self.ops = []
        self._fdum = es.enter_context(nc.sbuf_tensor("fdum", [1, 8], F32))
        self._floor = None
        self._barn = 0
        self.last_eng_op = {}
        self.suppress = set()
        self.last_w = {}
        self.readers = {}
        self.sems = {}
        for key in (("sync", True), ("gpsimd", True), ("gpsimd", False),
                    ("vector", False), ("scalar", False), ("tensor", False),
                    ("scalar", True), ("gpsimd", "cc")):
            self.sems[key] = es.enter_context(
                nc.semaphore(f"ch_{key[0]}_{key[1]}"))

    def e(self, eng, fn, r=(), w=(), dma=False, store=False, cc=False):
        idx = len(self.ops)
        deps = set()
        sup = self.suppress
        rn = [x for x in (_nm(a) for a in r) if x not in sup]
        wn = [x for x in (_nm(a) for a in w) if x not in sup]
        for a in r:
            if not isinstance(a, str) and _is_psum(a):
                n = _nm(a)
                if n not in wn:
                    wn = wn + [n]
        for n in rn:
            lw = self.last_w.get(n)
            if lw is not None:
                deps.add(lw)
        for n in wn:
            lw = self.last_w.get(n)
            if lw is not None:
                deps.add(lw)
            for t in self.readers.get(n, {}).values():
                deps.add(t)
        if self._floor is not None and idx != self._floor:
            deps.add(self._floor)
        op = _Op(eng, dma, fn, rn, wn, idx, deps, store, cc)
        self.ops.append(op)
        self.last_eng_op[eng] = idx
        for n in rn:
            self.readers.setdefault(n, {})[eng] = idx
        for n in wn:
            self.last_w[n] = idx
            self.readers[n] = {}
        return idx

    def fence(self, r=(), w=()):
        return self.e("vector", lambda E: E.memset(self._fdum[:], 0.0),
                      r=list(r) + ["fdum"], w=list(w) + ["fdum"])

    def barrier(self):
        deps = set(self.last_eng_op.values())
        idx = self.e("vector", lambda E: E.memset(self._fdum[:], 0.0),
                     r=["fdum"], w=["fdum"])
        self.ops[idx].deps |= deps
        self._floor = idx

    def dma(self, queue, out, in_, xr=(), xw=(), **kw):
        return self.e(queue, lambda E: E.dma_start(out=out, in_=in_, **kw),
                      r=[in_] + list(xr), w=[out] + list(xw), dma=True,
                      store=not hasattr(out.tensor, "base_partition"))

    def mm(self, out, lhsT, rhs, start=True, stop=True, xr=(), xw=()):
        return self.e("tensor",
                      lambda E: E.matmul(out, lhsT, rhs, start=start, stop=stop),
                      r=[lhsT, rhs] + list(xr), w=[out] + list(xw))

    def trans(self, out, in_, ident, xr=(), xw=()):
        return self.e("tensor", lambda E: E.transpose(out, in_, ident),
                      r=[in_, ident] + list(xr), w=[out] + list(xw))

    def act(self, out, in_, func, bias=0.0, scale=1.0, accum_out=None,
            xr=(), xw=()):
        ins = [in_]
        outs = [out]
        if isinstance(bias, bass.AP) or hasattr(bias, "tensor"):
            ins = ins + [bias]
        if isinstance(scale, bass.AP) or hasattr(scale, "tensor"):
            ins = ins + [scale]
        if accum_out is not None:
            outs = outs + [accum_out]
        return self.e("scalar",
                      lambda E: E.activation(out=out, in_=in_, func=func,
                                             bias=bias, scale=scale,
                                             accum_out=accum_out),
                      r=ins + list(xr), w=outs + list(xw))

    def v(self, meth, out, *ins, xr=(), xw=(), **kw):
        def fn(E):
            return getattr(E, meth)(out, *ins, **kw)
        rd = [a for a in ins if hasattr(a, "tensor")] + list(xr)
        rd += [a for a in kw.values() if hasattr(a, "tensor")]
        wr = [out] + list(xw)
        if hasattr(kw.get("accum_out"), "tensor"):
            wr.append(kw["accum_out"])
        return self.e("vector", fn, r=rd, w=wr)

    def g(self, meth, out, *ins, xr=(), xw=(), **kw):
        def fn(E):
            return getattr(E, meth)(out, *ins, **kw)
        rd = [a for a in ins if hasattr(a, "tensor")] + list(xr)
        return self.e("gpsimd", fn, r=rd, w=[out] + list(xw))

    def finalize(self):
        ops = self.ops
        for e in ("sync", "gpsimd", "scalar"):
            key = (e, "f")
            self.sems[key] = self.es.enter_context(nc_sem(self.nc, f"fn_{e}"))
        dep_marked = set()
        for c in ops:
            for p in c.deps:
                if ops[p].eng == "tensor" and c.eng == "tensor":
                    continue
                dep_marked.add(p)
        val = {}
        fence = {}
        cnt = {}
        dma_cum = {}
        for o in ops:
            if o.dma:
                kd = (o.eng, "cc") if o.cc else (o.eng, True)
                dma_cum[kd] = dma_cum.get(kd, 0) + 16
                cnt[kd] = dma_cum[kd]
                if o.idx in dep_marked:
                    kf = (o.eng, "f")
                    cnt[kf] = cnt.get(kf, 0) + 1
                    val[o.idx] = (kf, cnt[kf])
                    fence[o.idx] = (kd, dma_cum[kd], cnt[kf])
            elif o.idx in dep_marked:
                k = (o.eng, False)
                cnt[k] = cnt.get(k, 0) + 1
                val[o.idx] = (k, cnt[k])
        clocks = {e: {} for e in ENGS}
        snaps = {}
        waits = {}
        for c in ops:
            cl = clocks[c.eng]
            wl = []
            for p in sorted(c.deps):
                if ops[p].eng == "tensor" and c.eng == "tensor":
                    continue
                k, v = val[p]
                if cl.get(k, 0) >= v:
                    continue
                wl.append((k, v))
                cl[k] = v
                for k2, v2 in snaps[p].items():
                    if cl.get(k2, 0) < v2:
                        cl[k2] = v2
            waits[c.idx] = wl
            if c.idx in val:
                k, v = val[c.idx]
                sn = dict(cl)
                sn[k] = max(sn.get(k, 0), v)
                if c.dma:
                    kdc_ = fence[c.idx][0]
                    sn[kdc_] = max(sn.get(kdc_, 0), fence[c.idx][1])
                snaps[c.idx] = sn
        streams = {e: [] for e in ENGS}
        for o in ops:
            streams[o.eng].append(o)
        sems = self.sems
        final_cnt = dict(dma_cum)

        with self.nc.Block() as blk:
            def make_body(E):
                def body(engobj):
                    for o in streams[E]:
                        for (k, v) in waits[o.idx]:
                            engobj.wait_ge(sems[k], v)
                        inst = o.fn(engobj)
                        if o.dma:
                            kd = (E, "cc") if o.cc else (E, True)
                            inst.then_inc(sems[kd], 16)
                            if o.idx in fence:
                                kd2, dcum, fc = fence[o.idx]
                                engobj.wait_ge(sems[kd2], dcum)
                                engobj.sem_inc(sems[(E, "f")], 1)
                        elif o.idx in val:
                            inst.then_inc(sems[(E, False)], 1)
                    for kd3, v3 in final_cnt.items():
                        if kd3[0] == E:
                            engobj.wait_ge(sems[kd3], v3)
                return body
            for E in ENGS:
                if streams[E]:
                    getattr(blk, E)(make_body(E))


def nc_sem(nc, name):
    return nc.semaphore(name)




class FusedCtx:
    def __init__(self, nc, es, kb, reg):
        self.nc, self.es, self.kb, self.reg = nc, es, kb, reg


def _ctx_or_new(ctx):
    if ctx is not None:
        return ctx.nc, ctx.es, ctx.kb, ctx.reg, False
    nc = bass.Bass("TRN2", target_bir_lowering=False)
    return nc, None, None, None, True


_REG_INPUTS = {}


def _dio(nc, reg, name, shape, dtype, kind):
    if reg is not None:
        if name in reg:
            return reg[name]
        ap = nc.dram_tensor(name, shape, dtype, kind=kind).ap()
        reg[name] = ap
        if kind == "ExternalInput":
            _REG_INPUTS.setdefault(id(reg), set()).add(name)
        return ap
    return nc.dram_tensor(name, shape, dtype, kind=kind).ap()



def fmT(sb, dc=None, t0=None, t1=None):
    a = sb[:]
    if dc is not None:
        a = a[:, dc]
        if t0 is not None:
            a = a[:, t0:t1]
    return a


def emit_ln(kb, C, x3, xn3, TOK, pref, xbc, sqc, bc, tmpc, psA, psB,
            xrole=(), bcrole=(), rows_ap=None, rrole=()):
    nc, es = kb.nc, kb.es
    W = min(512, TOK)
    NT = TOK // W
    T = TOK
    rows = (rows_ap if rows_ap is not None else
            es.enter_context(nc.sbuf_tensor(pref + "rows", [1, 3 * TOK], F32))[:])
    A = rows[:, 0:T]
    B = rows[:, T:2 * T]
    Cm = rows[:, 2 * T:3 * T]
    for tc in range(NT):
        t0, t1 = tc * W, (tc + 1) * W
        for dc in range(NC_D):
            xb = xbc[:, (dc % 2) * W:(dc % 2) * W + W]
            sq = sqc[:, (dc % 2) * W:(dc % 2) * W + W]
            kb.v("tensor_copy", xb, x3[:, dc, t0:t1], xr=list(xrole))
            kb.act(sq, x3[:, dc, t0:t1], AF.Square, xr=list(xrole))
            kb.mm(psA[0:1, 0:W], C["ones_bf_col"], xb,
                  start=(dc == 0), stop=(dc == NC_D - 1))
            kb.mm(psB[0:1, 0:W], C["ones_bf_col"], sq,
                  start=(dc == 0), stop=(dc == NC_D - 1))
        kb.v("tensor_copy", A[:, t0:t1], psA[0:1, 0:W], xw=list(rrole))
        kb.v("tensor_copy", B[:, t0:t1], psB[0:1, 0:W], xw=list(rrole))
    RR = list(rrole)
    kb.v("tensor_scalar", A, A, 1.0 / D, None, op0=ALU.mult, xr=RR, xw=RR)
    kb.v("tensor_mul", Cm, A, A, xr=RR, xw=RR)
    kb.v("scalar_tensor_tensor", B, B, 1.0 / D, Cm,
         op0=ALU.mult, op1=ALU.subtract, xr=RR, xw=RR)
    kb.act(B, B, AF.Ln, bias=C["eps"], xr=RR, xw=RR)
    kb.act(B, B, AF.Exp, scale=-0.5, xr=RR, xw=RR)
    kb.v("tensor_mul", Cm, A, B, xr=RR, xw=RR)
    for tc in range(NT):
        t0, t1 = tc * W, (tc + 1) * W
        kb.mm(psA[:, 0:W], C["ones_f32_row"], B[:, t0:t1], xr=RR)
        kb.v("tensor_copy", bc[:, t0:t1], psA[:, 0:W], xw=list(bcrole))
        kb.mm(psB[:, 0:W], C["ones_f32_row"], Cm[:, t0:t1], xr=RR)
        kb.v("tensor_copy", bc[:, T + t0:T + t1], psB[:, 0:W],
             xw=list(bcrole))
    for dc in range(NC_D):
        for tc in range(NT):
            t0, t1 = tc * W, (tc + 1) * W
            kb.v("tensor_mul", tmpc[:, 0:W], x3[:, dc, t0:t1], bc[:, t0:t1],
                 xr=list(xrole) + list(bcrole))
            kb.v("tensor_sub", xn3[:, dc, t0:t1], tmpc[:, 0:W],
                 bc[:, T + t0:T + t1], xr=list(bcrole))



def build_A(TOK, ctx=None):
    nc, es0, kb0, reg, standalone = _ctx_or_new(ctx)
    es = contextlib.ExitStack()
    x_in = _dio(nc, reg, "x", [D, TOK], F32, "ExternalInput")
    ones_bf_col = _dio(nc, reg, "ones_bf_col", [128, 1], F32, "ExternalInput")
    ones_f32_row = _dio(nc, reg, "ones_f32_row", [1, 128], F32, "ExternalInput")
    eps_in = _dio(nc, reg, "eps", [1, 1], F32, "ExternalInput")
    xn_out = _dio(nc, reg, "xn", [D, TOK], BF16, "ExternalOutput")
    with es:
        if standalone:
            kb = KB(nc, es)
        else:
            kb = kb0
            kb.es = es
        x_sb = es.enter_context(nc.sbuf_tensor("A_x_sb", [128, NC_D, TOK], F32))
        xn_sb = es.enter_context(nc.sbuf_tensor("A_xn_sb", [128, NC_D, TOK], BF16))
        cbf = es.enter_context(nc.sbuf_tensor("A_cbf", [128, 1], BF16))
        crow = es.enter_context(nc.sbuf_tensor("A_crow", [1, 128], F32))
        ceps = es.enter_context(nc.sbuf_tensor("A_ceps", [1, 1], F32))
        C = {"ones_bf_col": cbf[:], "ones_f32_row": crow[:], "eps": ceps[:]}
        xbc = es.enter_context(nc.sbuf_tensor("A_xbc", [128, 1024], BF16))
        sqc = es.enter_context(nc.sbuf_tensor("A_sqc", [128, 1024], BF16))
        bcb = es.enter_context(nc.sbuf_tensor("A_bcb", [128, 2 * TOK], F32))
        tmpc = es.enter_context(nc.sbuf_tensor("A_tmpc", [128, TOK], F32))
        psA = es.enter_context(nc.psum_tensor("A_psA", [128, 512], F32))
        psB = es.enter_context(nc.psum_tensor("A_psB", [128, 512], F32))
        kb.dma("sync", x_sb[:], x_in.rearrange("(c p) t -> p c t", p=128))
        kb.dma("gpsimd", cbf[:], ones_bf_col)
        kb.dma("sync", crow[:], ones_f32_row)
        kb.dma("sync", ceps[:], eps_in)
        emit_ln(kb, C, x_sb[:], xn_sb[:], TOK, "ln0", xbc[:], sqc[:], bcb[:], tmpc[:], psA[:], psB[:])
        kb.dma("sync", xn_out.rearrange("(c p) t -> p c t", p=128), xn_sb[:])
        if standalone:
            kb.finalize()
    return nc



def _const_inputs():
    return {
        "ones_bf_col": np.ones((128, 1), np.float32),
        "ones_f32_row": np.ones((1, 128), np.float32),
        "eps": np.full((1, 1), 1e-5, np.float32),
    }






def build_B(TOK, ctx=None):
    TB = 4 * TOK
    W = 512
    nc, es0, kb0, reg, standalone = _ctx_or_new(ctx)
    xn_in = None if (reg is not None and "xnall" in reg) else \
        _dio(nc, reg, "xnf", [D, TB], BF16, "ExternalInput")
    wqk_in = _dio(nc, reg, "wqk", [D, 512], F32, "ExternalInput")
    wv_in = _dio(nc, reg, "wv", [D, 2048], F32, "ExternalInput")
    bqk_in = _dio(nc, reg, "bqk", [1, 512], F32, "ExternalInput")
    bv_in = _dio(nc, reg, "bv", [1, 2048], F32, "ExternalInput")
    ones_in = _dio(nc, reg, "onesr", [1, 512], F32, "ExternalInput")
    qk_out = _dio(nc, reg, "qk", [512, TB], BF16, "ExternalOutput")
    v_out = _dio(nc, reg, "v", [TB, 2048], BF16, "ExternalOutput")
    es = contextlib.ExitStack()
    with es:
        if standalone:
            kb = KB(nc, es)
        else:
            kb = kb0
            kb.es = es
        xnf = es.enter_context(nc.sbuf_tensor("B_xnf_sb", [128, NC_D, TB], BF16))
        wqk = es.enter_context(nc.sbuf_tensor("B_wqk_sb", [128, NC_D, 512], BF16))
        wv = es.enter_context(nc.sbuf_tensor("B_wv_sb", [128, NC_D, 2048], BF16))
        bqk = es.enter_context(nc.sbuf_tensor("B_bqk_sb", [1, 512], BF16))
        bv = es.enter_context(nc.sbuf_tensor("B_bv_sb", [1, 2048], BF16))
        onesr = es.enter_context(nc.sbuf_tensor("B_onesr_sb", [1, 512], BF16))
        stq = es.enter_context(nc.sbuf_tensor("B_stq", [128, 512], BF16))
        stgf = es.enter_context(nc.sbuf_tensor("B_stgf", [128, 512], F32))
        stv = es.enter_context(nc.sbuf_tensor("B_stv", [128, 2048], BF16))
        psq = [es.enter_context(nc.psum_tensor(f"B_psq{i}", [128, 512], F32))
               for i in range(2)]
        psv = [es.enter_context(nc.psum_tensor(f"B_psv{i}", [128, 512], F32))
               for i in range(4)]
        if xn_in is None:
            for r in range(4):
                kb.dma("sync", xnf[:, :, r * TOK:(r + 1) * TOK],
                       reg["xnall"][r].rearrange("(c p) t -> p c t", p=128))
        else:
            kb.dma("sync", xnf[:], xn_in.rearrange("(c p) t -> p c t", p=128))
        kb.dma("gpsimd", wqk[:], wqk_in.rearrange("(c p) f -> p c f", p=128))
        kb.dma("gpsimd", wv[:], wv_in.rearrange("(c p) f -> p c f", p=128))
        kb.dma("gpsimd", bqk[:], bqk_in)
        kb.dma("gpsimd", bv[:], bv_in)
        kb.dma("gpsimd", onesr[:], ones_in)
        for cc in range(4):
            for tc in range(TB // W):
                t0, t1 = tc * W, (tc + 1) * W
                ps = psq[tc % 2]
                for dc in range(NC_D):
                    kb.mm(ps[:], wqk[:, dc, cc * 128:(cc + 1) * 128],
                          xnf[:, dc, t0:t1], start=(dc == 0), stop=False)
                kb.mm(ps[:], bqk[:, cc * 128:(cc + 1) * 128], onesr[:],
                      start=False, stop=True)
                kb.act(stgf[:], ps[:], AF.Copy)
                kb.v("scalar_tensor_tensor", stq[:], stgf[:], 0.01, stgf[:],
                     op0=ALU.mult, op1=ALU.max)
                kb.dma("sync", qk_out[cc * 128:(cc + 1) * 128, t0:t1], stq[:])
        for tt in range(TB // 128):
            for vc in range(4):
                f0, f1 = vc * W, (vc + 1) * W
                ps = psv[vc]
                for dc in range(NC_D):
                    kb.mm(ps[:], xnf[:, dc, tt * 128:(tt + 1) * 128],
                          wv[:, dc, f0:f1], start=(dc == 0), stop=False)
                kb.mm(ps[:], onesr[:, 0:128], bv[:, f0:f1],
                      start=False, stop=True)
                kb.act(stgf[:], ps[:], AF.Copy)
                kb.v("scalar_tensor_tensor", stv[:, f0:f1], stgf[:], 0.01,
                     stgf[:], op0=ALU.mult, op1=ALU.max)
            kb.dma("sync", v_out[tt * 128:(tt + 1) * 128, :], stv[:])
        if standalone:
            kb.finalize()
    return nc



def build_C(TOK, ctx=None):
    TB = 4 * TOK
    KT = TB // 128
    QW = min(512, TOK)
    NQ = TB // QW
    nc, es0, kb0, reg, standalone = _ctx_or_new(ctx)
    qk_in = _dio(nc, reg, "qk", [512, TB], BF16, "ExternalInput")
    v_in = _dio(nc, reg, "v", [TB, 2048], BF16, "ExternalInput")
    id_in = _dio(nc, reg, "ident", [128, 128], F32, "ExternalInput")
    ones_in = _dio(nc, reg, "onesr", [1, 512], F32, "ExternalInput")
    onesc_in = _dio(nc, reg, "onesc", [128, 1], F32, "ExternalInput")
    out = _dio(nc, reg, "att", [2048, TB], BF16, "ExternalOutput")
    fused_rs = reg is not None and "rs_in" in reg
    if fused_rs:
        wos_in = _dio(nc, reg, "woTs", [2048, D], F32, "ExternalInput")
    es = contextlib.ExitStack()
    with es:
        if standalone:
            kb = KB(nc, es)
        else:
            kb = kb0
            kb.es = es
        q_sb = es.enter_context(nc.sbuf_tensor("C_q_sb", [64, TB], BF16))
        k_sb = es.enter_context(nc.sbuf_tensor("C_k_sb", [64, TB], BF16))
        v_sb = es.enter_context(nc.sbuf_tensor("C_v_sb", [128, KT, 512], BF16))
        S_sb = es.enter_context(nc.sbuf_tensor("C_S_sb", [128, KT, QW], BF16))
        ident = es.enter_context(nc.sbuf_tensor("C_ident_sb", [128, 128], BF16))
        onesr = es.enter_context(nc.sbuf_tensor("C_onesr_sb", [1, 512], BF16))
        onesc = es.enter_context(nc.sbuf_tensor("C_onesc_sb", [128, 1], BF16))
        onesrf = es.enter_context(nc.sbuf_tensor("C_onesrf", [1, 512], F32))
        macc = es.enter_context(nc.sbuf_tensor("C_macc", [128, QW], F32))
        macb = es.enter_context(nc.sbuf_tensor("C_macb", [128, QW], BF16))
        mq = es.enter_context(nc.sbuf_tensor("C_mq", [128, QW // 128], BF16))
        mrow = es.enter_context(nc.sbuf_tensor("C_mrow", [1, QW], BF16))
        mb_sb = es.enter_context(nc.sbuf_tensor("C_mb_sb", [128, QW], BF16))
        drow = es.enter_context(nc.sbuf_tensor("C_drow", [1, QW], F32))
        rrow = es.enter_context(nc.sbuf_tensor("C_rrow", [1, QW], F32))
        rb_sb = es.enter_context(nc.sbuf_tensor("C_rb_sb", [128, QW], F32))
        ostg = es.enter_context(nc.sbuf_tensor("C_ostg", [128, 4, QW], BF16))
        ps_s = [es.enter_context(nc.psum_tensor(f"C_ps_s{i}", [128, QW], F32))
                for i in range(2)]
        ps_av = [es.enter_context(nc.psum_tensor(f"C_ps_av{i}", [128, QW], F32))
                 for i in range(4)]
        ps_d = es.enter_context(nc.psum_tensor("C_ps_d", [1, QW], F32))
        ps_t = es.enter_context(nc.psum_tensor("C_ps_t", [128, 128], BF16))
        kb.dma("gpsimd", ident[:], id_in)
        kb.dma("gpsimd", onesr[:], ones_in)
        kb.dma("gpsimd", onesc[:], onesc_in)
        kb.dma("sync", onesrf[:], ones_in)
        kb.suppress.add("C_S_sb")
        GSZ = max(1, KT // 8)
        for h in range(4):
            kb.dma("sync", q_sb[:], qk_in[h * 64:(h + 1) * 64, :])
            kb.dma("sync", k_sb[:], qk_in[256 + h * 64:256 + (h + 1) * 64, :])
            kb.dma("sync", v_sb[:],
                   v_in[:, h * 512:(h + 1) * 512].rearrange(
                       "(n p) c -> p n c", p=128))
            for i in range(NQ + 1):
                lo = i < NQ
                hi = i > 0
                q0 = i * QW
                for kt in range(KT):
                    if hi:
                        for dv in range(4):
                            kb.mm(ps_av[dv][:],
                                  v_sb[:, kt, dv * 128:(dv + 1) * 128],
                                  S_sb[:, kt, :],
                                  start=(kt == 0), stop=(kt == KT - 1),
                                  xr=[f"CS{kt}"])
                        kb.mm(ps_d[:], onesc[:], S_sb[:, kt, :],
                              start=(kt == 0), stop=(kt == KT - 1),
                              xr=[f"CS{kt}"])
                    if lo:
                        ps = ps_s[kt % 2]
                        kb.mm(ps[:], k_sb[:, kt * 128:(kt + 1) * 128],
                              q_sb[:, q0:q0 + QW])
                        kb.act(S_sb[:, kt, :], ps[:], AF.Copy, scale=0.125,
                               xw=[f"CS{kt}"])
                        if kt == 0:
                            kb.v("tensor_copy", macc[:], S_sb[:, kt, :],
                                 xr=[f"CS{kt}"])
                        else:
                            kb.v("tensor_max", macc[:], macc[:],
                                 S_sb[:, kt, :], xr=[f"CS{kt}"])
                if hi:
                    kb.v("tensor_copy", drow[:], ps_d[:])
                    kb.v("reciprocal", rrow[:], drow[:])
                if lo:
                    kb.v("tensor_copy", macb[:], macc[:])
                    for qb in range(QW // 128):
                        kb.trans(ps_t[:], macb[:, qb * 128:(qb + 1) * 128],
                                 ident[:])
                        kb.v("tensor_reduce",
                             mq[:, qb:qb + 1].rearrange("p (a b) -> p a b",
                                                        a=1),
                             ps_t[:].rearrange("p (a b) -> p a b", a=1),
                             AX.X, ALU.max)
                    for qb in range(QW // 128):
                        kb.trans(ps_t[0:1, 0:128], mq[:, qb:qb + 1], ident[:])
                        kb.v("tensor_copy", mrow[:, qb * 128:(qb + 1) * 128],
                             ps_t[0:1, 0:128])
                    kb.mm(ps_s[0][:], onesr[:, 0:128], mrow[:])
                    kb.v("tensor_copy", mb_sb[:], ps_s[0][:])
                if hi:
                    kb.mm(ps_s[1][:], onesrf[:, 0:128], rrow[:])
                    kb.v("tensor_copy", rb_sb[:], ps_s[1][:])
                    for dv in range(4):
                        kb.v("tensor_mul", ostg[:, dv, :], ps_av[dv][:],
                             rb_sb[:])
                        kb.dma("sync",
                               out[h * 512 + dv * 128:h * 512 + (dv + 1) * 128,
                                   q0 - QW:q0], ostg[:, dv, :])
                if lo:
                    for g0 in range(0, KT, GSZ):
                        roles = [f"CS{k}" for k in range(g0, g0 + GSZ)]
                        kb.v("tensor_sub", S_sb[:, g0:g0 + GSZ, :],
                             S_sb[:, g0:g0 + GSZ, :],
                             mb_sb[:].rearrange(
                                 "p (a q) -> p a q", a=1).broadcast_to(
                                 [128, GSZ, QW]),
                             xr=roles, xw=roles)
                        kb.act(S_sb[:, g0:g0 + GSZ, :].rearrange(
                                   "p k q -> p (k q)"),
                               S_sb[:, g0:g0 + GSZ, :].rearrange(
                                   "p k q -> p (k q)"), AF.Exp,
                               xr=roles, xw=roles)
        if fused_rs:
            _sf = S_sb[:].rearrange("p c f -> p (c f)")
            wos = _sf[:, 0:16384].rearrange("p (c f) -> p c f", c=16)
            acb = _sf[:, 16384:18432].rearrange("p (c f) -> p c f", c=16)
            pst = _sf[:, 18432:19456]
            pp2 = [ps_av[0], ps_av[1]]
            ALLS = [f"CS{k}" for k in range(KT)]
            kb.dma("gpsimd", wos[:],
                   wos_in.rearrange("(c p) f -> p c f", p=128),
                   xr=ALLS, xw=ALLS)
            for tt in range(TB // 128):
                kb.dma("sync", acb[:],
                       out[:, tt * 128:(tt + 1) * 128].rearrange(
                           "(c p) t -> p c t", p=128),
                       xr=ALLS, xw=ALLS)
                for half in range(1024 // QW):
                    ps = pp2[half % 2]
                    for vchunk in range(16):
                        kb.mm(ps[:], acb[:, vchunk, :],
                              wos[:, vchunk, half * QW:(half + 1) * QW],
                              start=(vchunk == 0), stop=(vchunk == 15),
                              xr=ALLS)
                    kb.v("tensor_copy", pst[:, half * QW:(half + 1) * QW],
                         ps[:], xr=ALLS, xw=ALLS)
                kb.dma("sync", reg["rs_in"][tt * 128:(tt + 1) * 128, :],
                       pst[:], xr=ALLS)
        if standalone:
            kb.finalize()
    return nc



def build_D(TOK, ctx=None):
    W = min(512, TOK)
    NTH = TOK // W
    NTT = TOK // 128
    VC = 64
    FC = 16
    nc, es0, kb0, reg, standalone = _ctx_or_new(ctx)
    fused_rs = reg is not None and "rs_out" in reg
    o_in = None if fused_rs else \
        _dio(nc, reg, "o_all", [8192, TOK], BF16, "ExternalInput")
    wo_in = None if (reg is not None and "rs_out" in reg) else \
        _dio(nc, reg, "woT", [8192, D], F32, "ExternalInput")
    bo_in = _dio(nc, reg, "bo", [128, NC_D], F32, "ExternalInput")
    x_in = _dio(nc, reg, "x", [D, TOK], F32, "ExternalInput")
    xn_in = _dio(nc, reg, "xn", [D, TOK], BF16, "ExternalInput")
    wg_in = _dio(nc, reg, "wgT", [D, 8], F32, "ExternalInput")
    w1_in = _dio(nc, reg, "w1T", [8, D, 2048], F32, "ExternalInput")
    w2_in = _dio(nc, reg, "w2T", [8, D, 2048], F32, "ExternalInput")
    w3_in = _dio(nc, reg, "w3T", [8, 2048, D], F32, "ExternalInput")
    id_in = _dio(nc, reg, "ident", [128, 128], F32, "ExternalInput")
    sel_in = _dio(nc, reg, "selmat", [8, 1024], F32, "ExternalInput")
    ones_bf_col = _dio(nc, reg, "ones_bf_col", [128, 1], F32, "ExternalInput")
    ones_f32_row = _dio(nc, reg, "ones_f32_row", [1, 128], F32, "ExternalInput")
    eps_in = _dio(nc, reg, "eps", [1, 1], F32, "ExternalInput")
    x1_out = _dio(nc, reg, "x1", [D, TOK], F32, "ExternalOutput")
    xn1_out = _dio(nc, reg, "xn1", [D, TOK], BF16, "ExternalOutput")
    es = contextlib.ExitStack()
    with es:
        if standalone:
            kb = KB(nc, es)
        else:
            kb = kb0
            kb.es = es
        A1 = es.enter_context(nc.sbuf_tensor("D_A1", [128, 32768], BF16))
        xn = es.enter_context(nc.sbuf_tensor("D_xn_sb", [128, NC_D, TOK], BF16))
        s_fm = es.enter_context(nc.sbuf_tensor("D_s_fm", [128, NC_D, TOK], BF16))
        A6 = es.enter_context(nc.sbuf_tensor("D_A6", [128, 8192], BF16))
        oacc = es.enter_context(nc.sbuf_tensor("D_oacc", [128, NC_D, W], F32))
        xe = es.enter_context(nc.sbuf_tensor("D_xe_sb", [128, NC_D, W], BF16))
        lg = es.enter_context(nc.sbuf_tensor("D_lg", [128, TOK // 128, 8], F32))
        gsc = es.enter_context(nc.sbuf_tensor("D_gsc", [128, 64], F32))
        gbf = es.enter_context(nc.sbuf_tensor("D_gbf", [128, 8], BF16))
        gfm = es.enter_context(nc.sbuf_tensor("D_gfm", [8, TOK], BF16))
        selmD = es.enter_context(nc.sbuf_tensor("D_selmD", [8, 1024], BF16))
        gb = es.enter_context(nc.sbuf_tensor("D_gb", [128, W], BF16))
        xbc = es.enter_context(nc.sbuf_tensor("D_xbc", [128, 1024], BF16))
        sqc = es.enter_context(nc.sbuf_tensor("D_sqc", [128, 1024], BF16))

        ptm = es.enter_context(nc.sbuf_tensor("D_ptm", [128, 512], BF16))
        ttm = es.enter_context(nc.sbuf_tensor("D_ttm", [128, 128], F32))
        sil = es.enter_context(nc.sbuf_tensor("D_sil", [128, W], BF16))
        bo_sb = es.enter_context(nc.sbuf_tensor("D_bo_sb", [128, NC_D, 1], F32))
        ident = es.enter_context(nc.sbuf_tensor("D_ident_sb", [128, 128], BF16))
        cbf = es.enter_context(nc.sbuf_tensor("D_cbf", [128, 1], BF16))
        crow = es.enter_context(nc.sbuf_tensor("D_crow", [1, 128], F32))
        crbf = es.enter_context(nc.sbuf_tensor("D_crbf", [1, 128], BF16))
        ceps = es.enter_context(nc.sbuf_tensor("D_ceps", [1, 1], F32))
        wg_sb = es.enter_context(nc.sbuf_tensor("D_wg_sb", [128, NC_D, 8], BF16))
        C = {"ones_bf_col": cbf[:], "ones_f32_row": crow[:], "eps": ceps[:]}
        pp = [es.enter_context(nc.psum_tensor(f"D_pp{i}", [128, 512], F32))
              for i in range(2)]
        pt = es.enter_context(nc.psum_tensor("D_pt", [128, 128], BF16))
        ph = [es.enter_context(nc.psum_tensor(f"D_ph{i}", [128, W], F32))
              for i in range(2)]
        po = [es.enter_context(nc.psum_tensor(f"D_po{i}", [128, W], F32))
              for i in range(2)]
        pg = es.enter_context(nc.psum_tensor("D_pg", [128, 512], F32))
        kb.dma("gpsimd", cbf[:], ones_bf_col)
        kb.dma("sync", crow[:], ones_f32_row)
        kb.dma("gpsimd", crbf[:], ones_f32_row)
        kb.dma("sync", ceps[:], eps_in)
        kb.dma("gpsimd", ident[:], id_in)
        kb.dma("sync", bo_sb[:].rearrange("p c o -> p (c o)"), bo_in)
        kb.dma("sync", xn[:], xn_in.rearrange("(c p) t -> p c t", p=128))
        kb.dma("gpsimd", wg_sb[:], wg_in.rearrange("(c p) e -> p c e", p=128))
        kb.dma("gpsimd", selmD[:], sel_in)

        for tt in range(NTT):
            for dc in range(NC_D):
                kb.mm(pg[:, 0:8], xn[:, dc, tt * 128:(tt + 1) * 128],
                      wg_sb[:, dc, :], start=(dc == 0), stop=(dc == NC_D - 1))
            kb.v("tensor_copy", lg[:, tt, :], pg[:, 0:8])
        G = gsc
        for tt in range(NTT):
            l = lg[:, tt, :]
            m1 = G[:, 0:1]; eq1 = G[:, 8:16]; tmp = G[:, 16:24]
            m2 = G[:, 1:2]; eq2 = G[:, 24:32]; num = G[:, 32:40]
            den = G[:, 2:3]; sel = G[:, 40:48]; gam = G[:, 48:56]
            e21 = G[:, 3:4]
            kb.v("tensor_reduce", m1.rearrange("p (a b) -> p a b", a=1),
                 l.rearrange("p (a b) -> p a b", a=1), AX.X, ALU.max)
            kb.v("tensor_tensor", eq1, l, m1.broadcast_to([128, 8]), op=ALU.is_equal)
            kb.v("scalar_tensor_tensor", tmp, eq1, -1e30, l, op0=ALU.mult, op1=ALU.add)
            kb.v("tensor_reduce", m2.rearrange("p (a b) -> p a b", a=1),
                 tmp.rearrange("p (a b) -> p a b", a=1), AX.X, ALU.max)
            kb.v("tensor_tensor", eq2, tmp, m2.broadcast_to([128, 8]), op=ALU.is_equal)
            kb.v("tensor_sub", e21, m2, m1)
            kb.act(e21, e21, AF.Exp)
            kb.v("tensor_scalar", den, e21, 1.0, None, op0=ALU.add)
            kb.v("reciprocal", den, den)
            kb.v("tensor_sub", num, l, m1.broadcast_to([128, 8]))
            kb.act(num, num, AF.Exp)
            kb.v("tensor_add", sel, eq1, eq2)
            kb.v("tensor_mul", gam, num, sel)
            kb.v("tensor_scalar", gam, gam, den, None, op0=ALU.mult)
            kb.v("tensor_copy", gbf[:], gam)
            kb.trans(pt[0:8, 0:128], gbf[:], ident[:])
            kb.v("tensor_copy", gfm[0:8, tt * 128:(tt + 1) * 128],
                 pt[0:8, 0:128])

        if fused_rs:
            prj = A6[:].bitcast(F32)[:, 0:2048].bitcast(BF16)
            for tt in range(NTT):
                kb.dma("sync", prj[:, 0:1024],
                       reg["rs_out"][tt * 128:(tt + 1) * 128, :])
                for dc in range(NC_D):
                    kb.trans(pt[:], prj[:, dc * 128:(dc + 1) * 128], ident[:])
                    kb.v("tensor_scalar", ttm[:], pt[:], bo_sb[:, dc, :], None,
                         op0=ALU.add)
                    kb.v("scalar_tensor_tensor", ttm[:], ttm[:], 0.01, ttm[:],
                         op0=ALU.mult, op1=ALU.max)
                    kb.v("tensor_add", s_fm[:, dc, tt * 128:(tt + 1) * 128],
                         ttm[:], xn[:, dc, tt * 128:(tt + 1) * 128])
        else:
            woh = A1[:].rearrange("p (v f) -> p v f", v=64)
            ocol = A6[:].rearrange("p (v f) -> p v f", v=64)
            for dh in range(2):
                kb.dma("gpsimd", woh,
                       wo_in[:, dh * 512:(dh + 1) * 512].rearrange(
                           "(v p) f -> p v f", p=128))
                for tt in range(NTT):
                    kb.dma("sync", ocol,
                           o_in[:, tt * 128:(tt + 1) * 128].rearrange(
                               "(v p) t -> p v t", p=128))
                    ps = pp[tt % 2]
                    for vc in range(VC):
                        kb.mm(ps[:], ocol[:, vc, :], woh[:, vc, :],
                              start=(vc == 0), stop=(vc == VC - 1))
                    kb.v("tensor_copy", ptm[:], ps[:])
                    for q in range(4):
                        dc = dh * 4 + q
                        kb.trans(pt[:], ptm[:, q * 128:(q + 1) * 128], ident[:])
                        kb.v("tensor_scalar", ttm[:], pt[:], bo_sb[:, dc, :],
                             None, op0=ALU.add)
                        kb.v("scalar_tensor_tensor", ttm[:], ttm[:], 0.01,
                             ttm[:], op0=ALU.mult, op1=ALU.max)
                        kb.v("tensor_add",
                             s_fm[:, dc, tt * 128:(tt + 1) * 128],
                             ttm[:], xn[:, dc, tt * 128:(tt + 1) * 128])

        wslot = [A1[:, 0:16384].rearrange("p (c f) -> p c f", c=NC_D),
                 A1[:, 16384:32768].rearrange("p (c f) -> p c f", c=NC_D)]
        wslot3 = [A1[:, 0:16384].rearrange("p (c f) -> p c f", c=FC),
                  A1[:, 16384:32768].rearrange("p (c f) -> p c f", c=FC)]
        h1b = A6[:].rearrange("p (c f) -> p c f", c=FC)
        for th in range(NTH):
            t0, t1 = th * W, (th + 1) * W
            first = True
            for e in range(8):
                sl = e % 2
                kb.mm(pg[:, 0:W], selmD[:, e * 128:(e + 1) * 128],
                      gfm[:, t0:t1])
                kb.v("tensor_copy", gb[:, 0:W], pg[:, 0:W])
                for dc in range(NC_D):
                    kb.v("tensor_mul", xe[:, dc, :], xn[:, dc, t0:t1], gb[:, 0:W])
                kb.dma("gpsimd", wslot[sl],
                       w1_in[e].rearrange("(c p) f -> p c f", p=128))
                for fc in range(FC):
                    ps = ph[fc % 2]
                    for dc in range(NC_D):
                        kb.mm(ps[:, 0:W], wslot[sl][:, dc, fc * 128:(fc + 1) * 128],
                              xn[:, dc, t0:t1], start=(dc == 0), stop=(dc == NC_D - 1))
                    kb.act(h1b[:, fc, 0:W], ps[:, 0:W], AF.Silu)
                kb.dma("gpsimd", wslot[1 - sl],
                       w2_in[e].rearrange("(c p) f -> p c f", p=128))
                for fc in range(FC):
                    ps = ph[fc % 2]
                    for dc in range(NC_D):
                        kb.mm(ps[:, 0:W], wslot[1 - sl][:, dc, fc * 128:(fc + 1) * 128],
                              xe[:, dc, :], start=(dc == 0), stop=(dc == NC_D - 1))
                    kb.v("tensor_mul", h1b[:, fc, 0:W], h1b[:, fc, 0:W], ps[:, 0:W])
                kb.dma("gpsimd", wslot3[sl],
                       w3_in[e].rearrange("(c p) f -> p c f", p=128))
                for dc in range(NC_D):
                    ps = po[dc % 2]
                    for fc in range(FC):
                        kb.mm(ps[:, 0:W], wslot3[sl][:, fc, dc * 128:(dc + 1) * 128],
                              h1b[:, fc, 0:W], start=(fc == 0), stop=(fc == FC - 1))
                    if first:
                        kb.v("tensor_copy", oacc[:, dc, :], ps[:, 0:W])
                    else:
                        kb.v("tensor_add", oacc[:, dc, :], oacc[:, dc, :], ps[:, 0:W])
                first = False
            for dc in range(NC_D):
                kb.act(sil[:, 0:W], oacc[:, dc, :], AF.Silu)
                kb.v("tensor_mul", s_fm[:, dc, t0:t1], s_fm[:, dc, t0:t1],
                     sil[:, 0:W])

        x1v = A1[:].bitcast(F32)[:, 0:NC_D * TOK].rearrange(
            "p (c t) -> p c t", c=NC_D)
        a6f = A6[:].bitcast(F32)
        xch = a6f[:, 0:TOK]
        tmpc = a6f[:, TOK:2 * TOK]
        for dc in range(NC_D):
            kb.dma("sync", xch, x_in[dc * 128:(dc + 1) * 128, :])
            kb.v("tensor_add", x1v[:, dc], xch, s_fm[:, dc])
            kb.dma("sync", x1_out[dc * 128:(dc + 1) * 128, :], x1v[:, dc])
        bcbv = oacc[:].rearrange("p c t -> p (c t)")
        emit_ln(kb, C, x1v, xn[:], TOK, "ln1", xbc[:], sqc[:], bcbv, tmpc,
                pp[0][:], pp[1][:])
        kb.dma("sync", xn1_out.rearrange("(c p) t -> p c t", p=128), xn[:])
        if standalone:
            kb.finalize()
    return nc



def _heads_cols(hg):
    q = np.concatenate([np.arange(h * 64, (h + 1) * 64) for h in range(hg, hg + 4)])
    k = 1024 + q
    v = np.concatenate([2048 + np.arange(h * 512, (h + 1) * 512)
                        for h in range(hg, hg + 4)])
    return q, k, v


def run_L0(inputs, TOK, runner):
    CONST = _const_inputs()
    ident = np.eye(128, dtype=np.float32)
    ones512 = np.ones((1, 512), np.float32)
    x = np.asarray(inputs["L0_attn_representations"])
    S = x.shape[1]
    assert S == 4 * TOK
    maps = []
    for c in range(NCORES):
        b, p = c // 4, c % 4
        xs = np.ascontiguousarray(x[b, p * TOK:(p + 1) * TOK, :].T)
        maps.append({"x": xs, **CONST})
    resA = runner(build_A(TOK), maps)
    xn_sh = [np.asarray(r["xn"]) for r in resA]
    WcT = np.ascontiguousarray(np.asarray(inputs["L0_attn_Wc"]).T)
    bc = np.asarray(inputs["L0_attn_bc"])
    maps = []
    for c in range(NCORES):
        b, p = c // 4, c % 4
        qc, kc, vc = _heads_cols(4 * p)
        xnf = np.concatenate([xn_sh[b * 4 + j] for j in range(4)], axis=1)
        maps.append({
            "xnf": np.ascontiguousarray(xnf),
            "wqk": np.ascontiguousarray(WcT[:, np.concatenate([qc, kc])]),
            "wv": np.ascontiguousarray(WcT[:, vc]),
            "bqk": bc[np.concatenate([qc, kc])][None, :].copy(),
            "bv": bc[vc][None, :].copy(),
            "onesr": ones512,
        })
    resB = runner(build_B(TOK), maps)
    maps = []
    for c in range(NCORES):
        maps.append({
            "qk": np.asarray(resB[c]["qk"]),
            "v": np.asarray(resB[c]["v"]),
            "ident": ident, "onesr": ones512,
            "onesc": np.ones((128, 1), np.float32),
        })
    resC = runner(build_C(TOK), maps)
    WoT = np.ascontiguousarray(np.asarray(inputs["L0_attn_Wo"]).T)
    bo = np.asarray(inputs["L0_attn_bo"]).reshape(NC_D, 128).T.copy()
    WgT = np.ascontiguousarray(np.asarray(inputs["L0_ffn_Wg"]).T)
    W1T = np.ascontiguousarray(np.asarray(inputs["L0_ffn_W1"]).transpose(0, 2, 1))
    W2T = np.ascontiguousarray(np.asarray(inputs["L0_ffn_W2"]).transpose(0, 2, 1))
    W3T = np.ascontiguousarray(np.asarray(inputs["L0_ffn_W3"]).transpose(0, 2, 1))
    maps = []
    for c in range(NCORES):
        b, p = c // 4, c % 4
        rng = slice(p * TOK, (p + 1) * TOK)
        o_all = np.concatenate(
            [np.asarray(resC[b * 4 + j]["att"])[:, rng] for j in range(4)], axis=0)
        xs = np.ascontiguousarray(x[b, rng, :].T)
        selmat = np.zeros((8, 1024), np.float32)
        for t8 in range(8):
            selmat[t8, t8 * 128:(t8 + 1) * 128] = 1.0
        maps.append({
            "o_all": np.ascontiguousarray(o_all),
            "woT": WoT, "bo": bo, "x": xs, "xn": xn_sh[c],
            "wgT": WgT, "w1T": W1T, "w2T": W2T, "w3T": W3T,
            "ident": ident, "selmat": selmat, **CONST,
        })
    resD = runner(build_D(TOK), maps)
    x1 = [np.asarray(r["x1"]) for r in resD]
    xn1 = [np.asarray(r["xn1"]) for r in resD]
    return x1, xn1


LR_LAM = 0
LR_LM = 2048
LR_CUR = 4096
LR_T1 = 6144
LR_T2 = 8192
LR_LC = 10240


def emit_row_cmul(kb, L, o_off, a_off, b_off, t_off):
    t1, t2 = L[:, t_off:t_off + 1024], L[:, t_off + 1024:t_off + 2048]
    are, aim = L[:, a_off:a_off + 1024], L[:, a_off + 1024:a_off + 2048]
    bre, bim = L[:, b_off:b_off + 1024], L[:, b_off + 1024:b_off + 2048]
    ore, oim = L[:, o_off:o_off + 1024], L[:, o_off + 1024:o_off + 2048]
    u1 = L[:, t_off + 2048:t_off + 3072]
    u2 = L[:, t_off + 3072:t_off + 4096]
    kb.v("tensor_mul", t1, are, bre)
    kb.v("tensor_mul", t2, aim, bim)
    kb.v("tensor_mul", u1, are, bim)
    kb.v("tensor_mul", u2, aim, bre)
    kb.v("tensor_sub", ore, t1, t2)
    kb.v("tensor_add", oim, u1, u2)


def emit_lambda_tables(kb, C, lrows, P, PP, stg, stq, tq, pbc, psA, psB, SCR):
    L = lrows
    om = L[:, LR_T1:LR_T1 + 1024]
    den = L[:, LR_T1 + 1024:LR_T1 + 2048]
    rec = L[:, LR_T2:LR_T2 + 1024]
    num = L[:, LR_T2 + 1024:LR_T2 + 2048]
    kb.v("tensor_mul", den, om, om)
    kb.v("tensor_scalar", den, den, 1.0, None, op0=ALU.add)
    kb.v("reciprocal", rec, den)
    kb.v("tensor_scalar", num, den, -1.0, 2.0, op0=ALU.mult, op1=ALU.add)
    kb.v("tensor_mul", L[:, LR_LAM:LR_LAM + 1024], num, rec)
    kb.v("tensor_mul", L[:, LR_LAM + 1024:LR_LAM + 2048], om, rec)
    kb.v("tensor_scalar", L[:, LR_LAM + 1024:LR_LAM + 2048],
         L[:, LR_LAM + 1024:LR_LAM + 2048], -2.0, None, op0=ALU.mult)
    kb.v("memset", P[0:1, 0:1024], 1.0)
    kb.v("memset", P[0:1, 1024:2048], 0.0)
    kb.v("tensor_copy", L[:, LR_LM:LR_LM + 2048], L[:, LR_LAM:LR_LAM + 2048])
    m = 1
    while m < 128:
        for q in range(4):
            kb.mm(psA[0:m, 0:512], C["ones_f32_row"][:, 0:m],
                  L[:, LR_LM + q * 512:LR_LM + (q + 1) * 512])
            kb.v("tensor_copy", stg[0:m, q * 512:(q + 1) * 512],
                 psA[0:m, 0:512], xr=[SCR], xw=[SCR])
        kb.v("tensor_mul", tq[0:m, 0:1024], P[0:m, 0:1024], stg[0:m, 0:1024],
             xr=[SCR], xw=[SCR])
        kb.v("tensor_mul", tq[0:m, 1024:2048], P[0:m, 1024:2048],
             stg[0:m, 1024:2048], xr=[SCR], xw=[SCR])
        kb.v("tensor_sub", stq[0:m, 0:1024], tq[0:m, 0:1024],
             tq[0:m, 1024:2048], xr=[SCR], xw=[SCR])
        kb.v("tensor_mul", tq[0:m, 0:1024], P[0:m, 0:1024],
             stg[0:m, 1024:2048], xr=[SCR], xw=[SCR])
        kb.v("tensor_mul", tq[0:m, 1024:2048], P[0:m, 1024:2048],
             stg[0:m, 0:1024], xr=[SCR], xw=[SCR])
        kb.v("tensor_add", stq[0:m, 1024:2048], tq[0:m, 0:1024],
             tq[0:m, 1024:2048], xr=[SCR], xw=[SCR])
        kb.dma("gpsimd", P[m:2 * m, :], stq[0:m, :], xr=[SCR])
        emit_row_cmul(kb, L, LR_LM, LR_LM, LR_LM, LR_T1)
        m *= 2
    for q in range(4):
        kb.mm(psB[:, 0:512], C["ones_f32_row"],
              L[:, LR_LAM + q * 512:LR_LAM + (q + 1) * 512])
        kb.v("tensor_copy", pbc[:, q * 512:(q + 1) * 512], psB[:, 0:512],
             xr=[SCR], xw=[SCR])
    kb.v("tensor_mul", tq[:, 0:1024], P[:, 0:1024], pbc[:, 0:1024],
         xr=[SCR], xw=[SCR])
    kb.v("tensor_mul", tq[:, 1024:2048], P[:, 1024:2048], pbc[:, 1024:2048],
         xr=[SCR], xw=[SCR])
    kb.v("tensor_sub", PP[:, 0:1024], tq[:, 0:1024], tq[:, 1024:2048],
         xr=[SCR])
    kb.v("tensor_mul", tq[:, 0:1024], P[:, 0:1024], pbc[:, 1024:2048],
         xr=[SCR], xw=[SCR])
    kb.v("tensor_mul", tq[:, 1024:2048], P[:, 1024:2048], pbc[:, 0:1024],
         xr=[SCR], xw=[SCR])
    kb.v("tensor_add", PP[:, 1024:2048], tq[:, 0:1024], tq[:, 1024:2048],
         xr=[SCR])
    kb.dma("gpsimd", L[:, LR_LC:LR_LC + 2048], PP[127:128, :])


def emit_omega(kb, lrows, adiff, AR, are_sin, sin_in, psA, SIN_ROLE):
    for half in range(2):
        kb.dma("sync", are_sin,
               sin_in[:, half * 512:(half + 1) * 512].rearrange(
                   "(c p) f -> p c f", p=128), xw=[SIN_ROLE])
        for nch in range(NC_D):
            kb.mm(psA[0:1, 0:512], adiff[:, nch:nch + 1],
                  are_sin[:, nch, :] if True else None,
                  start=(nch == 0), stop=(nch == NC_D - 1), xr=[SIN_ROLE])
        kb.v("tensor_copy",
             lrows[:, LR_T1 + half * 512:LR_T1 + (half + 1) * 512],
             psA[0:1, 0:512])



def build_E(TOK, ctx=None):
    NTT = TOK // 128
    nc, es0, kb0, reg, standalone = _ctx_or_new(ctx)
    xn_in = _dio(nc, reg, "xn1", [D, TOK], BF16, "ExternalInput")
    apos_in = _dio(nc, reg, "apos", [128, NC_D], F32, "ExternalInput")
    arev_in = _dio(nc, reg, "arev", [128, NC_D], F32, "ExternalInput")
    bw_in = _dio(nc, reg, "B_w", [1024, 1024], F32, "ExternalInput")
    cos_in = _dio(nc, reg, "cosF", [1024, 1024], F32, "ExternalInput")
    sin_in = _dio(nc, reg, "sinF", [1024, 1024], F32, "ExternalInput")
    ut_in = _dio(nc, reg, "ut128", [128, 128], F32, "ExternalInput")
    ones_f32_row = _dio(nc, reg, "ones_f32_row", [1, 128], F32, "ExternalInput")
    zc_out = _dio(nc, reg, "zcum", [TOK, 2048], BF16, "ExternalOutput")
    cl_out = _dio(nc, reg, "cloc", [1, NTT * 2048], F32, "ExternalOutput")
    h_out = _dio(nc, reg, "hrow", [1, 2048], F32, "ExternalOutput")
    es = contextlib.ExitStack()
    with es:
        if standalone:
            kb = KB(nc, es)
        else:
            kb = kb0
            kb.es = es
        ARE = es.enter_context(nc.sbuf_tensor("E_ARE", [128, 8192], F32))
        kb.suppress.add("E_ARE")
        lrows = es.enter_context(nc.sbuf_tensor("E_lrows", [1, 12288], F32))
        P = es.enter_context(nc.sbuf_tensor("E_P", [128, 2048], F32))
        PP = es.enter_context(nc.sbuf_tensor("E_PP", [128, 2048], F32))
        acolp = es.enter_context(nc.sbuf_tensor("E_acolp", [128, NC_D], F32))
        acolr = es.enter_context(nc.sbuf_tensor("E_acolr", [128, NC_D], F32))
        adiff = es.enter_context(nc.sbuf_tensor("E_adiff", [128, NC_D], F32))
        Bh = es.enter_context(nc.sbuf_tensor("E_Bh", [128, NC_D, 2048], BF16))
        utb = es.enter_context(nc.sbuf_tensor("E_utb", [128, 128], BF16))
        crow = es.enter_context(nc.sbuf_tensor("E_crow", [1, 128], F32))
        zbf = es.enter_context(nc.sbuf_tensor("E_zbf", [128, 2048], BF16))
        zcs = es.enter_context(nc.sbuf_tensor("E_zcs", [128, 2048], BF16))
        zlast = es.enter_context(nc.sbuf_tensor("E_zlast", [1, 2048], BF16))
        p127 = es.enter_context(nc.sbuf_tensor("E_p127", [1, 2048], F32))
        tch = es.enter_context(nc.sbuf_tensor("E_tch", [128, 1024], F32))
        C = {"ones_f32_row": crow[:]}
        pb = [es.enter_context(nc.psum_tensor(f"E_pb{i}", [128, 512], F32))
              for i in range(4)]
        pz = [es.enter_context(nc.psum_tensor(f"E_pz{i}", [128, 512], F32))
              for i in range(2)]
        are_sin = ARE[:, 0:4096].rearrange("p (c f) -> p c f", c=NC_D)
        stg = ARE[:, 4096:6144][0:64, :]
        stq = ARE[:, 6144:8192][0:64, :]
        tqv = ARE[:, 0:2048]
        pbc = ARE[:, 2048:4096]
        areb = ARE[:].bitcast(BF16)
        bwv = areb[:, 0:8192].rearrange("p (c f) -> p c f", c=NC_D)
        trig = areb[:, 8192:16384].rearrange("p (c f) -> p c f", c=NC_D)
        xnv = areb[:, 0:NC_D * TOK].rearrange("p (c t) -> p c t", c=NC_D)

        kb.dma("sync", crow[:], ones_f32_row)
        kb.dma("gpsimd", acolp[:], apos_in)
        kb.dma("gpsimd", acolr[:], arev_in)
        kb.dma("gpsimd", utb[:], ut_in)
        kb.v("tensor_sub", adiff[:], acolr[:], acolp[:])
        emit_omega(kb, lrows[:], adiff[:], ARE, are_sin, sin_in, pb[0][:],
                   "ARE:sin")
        kb.fence(r=["ARE:sin"], w=["ARE:scr"])
        emit_lambda_tables(kb, C, lrows[:], P[:], PP[:], stg, stq, tqv, pbc,
                           pb[0][:], pb[1][:], "ARE:scr")
        kb.dma("gpsimd", p127[:], P[127:128, :])
        kb.fence(r=["ARE:scr"], w=["ARE:bw", "ARE:trig"])
        kb.dma("gpsimd", bwv, bw_in.rearrange("(c p) f -> p c f", p=128),
               xw=["ARE:bw"])
        for part in range(2):
            tin = cos_in if part == 0 else sin_in
            sgn = 1.0 if part == 0 else -1.0
            kb.dma("gpsimd", trig, tin.rearrange("(c p) f -> p c f", p=128),
                   xw=["ARE:trig"])
            for dc in range(NC_D):
                for q in range(2):
                    ps = pb[2 + q]
                    for nch in range(NC_D):
                        kb.mm(ps[:], bwv[:, nch, dc * 128:(dc + 1) * 128],
                              trig[:, nch, q * 512:(q + 1) * 512],
                              start=(nch == 0), stop=(nch == NC_D - 1),
                              xr=["ARE:bw", "ARE:trig"])
                    kb.v("tensor_scalar",
                         Bh[:, dc, part * 1024 + q * 512:part * 1024 + (q + 1) * 512],
                         ps[:], sgn, None, op0=ALU.mult)
        kb.fence(r=["ARE:bw", "ARE:trig"], w=["ARE:xn"])
        kb.dma("sync", xnv, xn_in.rearrange("(c p) t -> p c t", p=128),
               xw=["ARE:xn"])
        kb.v("memset", lrows[:, LR_CUR:LR_CUR + 2048], 0.0)
        L = lrows[:]
        for t in range(NTT):
            for q4 in range(4):
                ps = pb[q4]
                for dc in range(NC_D):
                    kb.mm(ps[:], xnv[:, dc, t * 128:(t + 1) * 128],
                          Bh[:, dc, q4 * 512:(q4 + 1) * 512],
                          start=(dc == 0), stop=(dc == NC_D - 1),
                          xr=["ARE:xn"])
            for q in range(2):
                kb.v("tensor_mul", tch[:, 0:512],
                     P[:, q * 512:(q + 1) * 512], pb[q][:])
                kb.v("tensor_mul", tch[:, 512:1024],
                     P[:, 1024 + q * 512:1024 + (q + 1) * 512], pb[2 + q][:])
                kb.v("tensor_add", zbf[:, q * 512:(q + 1) * 512],
                     tch[:, 0:512], tch[:, 512:1024])
            for q in range(2):
                kb.v("tensor_mul", tch[:, 0:512],
                     P[:, q * 512:(q + 1) * 512], pb[2 + q][:])
                kb.v("tensor_mul", tch[:, 512:1024],
                     P[:, 1024 + q * 512:1024 + (q + 1) * 512], pb[q][:])
                kb.v("tensor_sub", zbf[:, 1024 + q * 512:1024 + (q + 1) * 512],
                     tch[:, 0:512], tch[:, 512:1024])
            for q4 in range(4):
                ps = pz[q4 % 2]
                kb.mm(ps[:], utb[:], zbf[:, q4 * 512:(q4 + 1) * 512])
                kb.v("tensor_copy", zcs[:, q4 * 512:(q4 + 1) * 512], ps[:])
            kb.dma("sync", zc_out[t * 128:(t + 1) * 128, :], zcs[:])
            kb.dma("sync", cl_out[:, t * 2048:(t + 1) * 2048],
                   L[:, LR_CUR:LR_CUR + 2048])
            kb.dma("gpsimd", zlast[:], zcs[127:128, :])
            zl = zlast[:]
            kb.v("tensor_mul", L[:, LR_T2:LR_T2 + 1024], p127[:, 0:1024],
                 zl[:, 0:1024])
            kb.v("tensor_mul", L[:, LR_T2 + 1024:LR_T2 + 2048],
                 p127[:, 1024:2048], zl[:, 1024:2048])
            kb.v("tensor_sub", L[:, LR_T1:LR_T1 + 1024],
                 L[:, LR_T2:LR_T2 + 1024], L[:, LR_T2 + 1024:LR_T2 + 2048])
            kb.v("tensor_mul", L[:, LR_T2:LR_T2 + 1024], p127[:, 0:1024],
                 zl[:, 1024:2048])
            kb.v("tensor_mul", L[:, LR_T2 + 1024:LR_T2 + 2048],
                 p127[:, 1024:2048], zl[:, 0:1024])
            kb.v("tensor_add", L[:, LR_T1 + 1024:LR_T1 + 2048],
                 L[:, LR_T2:LR_T2 + 1024], L[:, LR_T2 + 1024:LR_T2 + 2048])
            emit_row_cmul(kb, L, LR_CUR, LR_LC, LR_CUR, 0)
            kb.v("tensor_add", L[:, LR_CUR:LR_CUR + 2048],
                 L[:, LR_CUR:LR_CUR + 2048], L[:, LR_T1:LR_T1 + 2048])
        kb.dma("sync", h_out, lrows[:, LR_CUR:LR_CUR + 2048])
        if standalone:
            kb.finalize()
    return nc



def build_F(TOK, ctx=None):
    NTT = TOK // 128
    W = min(512, TOK)
    NTH = TOK // W
    NTC = NTT // 4 if NTT >= 4 else 1
    TPC = min(4, NTT)
    HH = 1408
    NHC = 11
    nc, es0, kb0, reg, standalone = _ctx_or_new(ctx)
    zc_in = _dio(nc, reg, "zcum", [TOK, 2048], BF16, "ExternalInput")
    cl_in = _dio(nc, reg, "cloc", [1, NTT * 2048], F32, "ExternalInput")
    hs_in = None if (reg is not None and "hall" in reg) else \
        _dio(nc, reg, "Hsel", [1, 8192], F32, "ExternalInput")
    selw_in = _dio(nc, reg, "selw", [4, 4], F32, "ExternalInput") \
        if hs_in is None else None
    apos_in = _dio(nc, reg, "apos", [128, NC_D], F32, "ExternalInput")
    arev_in = _dio(nc, reg, "arev", [128, NC_D], F32, "ExternalInput")
    cos_in = _dio(nc, reg, "cosF", [1024, 1024], F32, "ExternalInput")
    sin_in = _dio(nc, reg, "sinF", [1024, 1024], F32, "ExternalInput")
    cwt_in = _dio(nc, reg, "C_wT", [1024, 1024], F32, "ExternalInput")
    dcol_in = _dio(nc, reg, "Dcol", [128, NC_D], F32, "ExternalInput")
    xn_in = _dio(nc, reg, "xn1", [D, TOK], BF16, "ExternalInput")
    x_in = _dio(nc, reg, "x1", [D, TOK], F32, "ExternalInput")
    gw_in = _dio(nc, reg, "gwT", [D, 8], F32, "ExternalInput")
    eb_in = _dio(nc, reg, "ebias", [1, 8], F32, "ExternalInput")
    w1_in = _dio(nc, reg, "w1", [8, D, 2816], F32, "ExternalInput")
    w2_in = _dio(nc, reg, "w2", [8, 2816, D], F32, "ExternalInput")
    w3_in = _dio(nc, reg, "w3", [8, D, 2816], F32, "ExternalInput")
    id_in = _dio(nc, reg, "ident", [128, 128], F32, "ExternalInput")
    ones_bf_col = _dio(nc, reg, "ones_bf_col", [128, 1], F32, "ExternalInput")
    ones_f32_row = _dio(nc, reg, "ones_f32_row", [1, 128], F32, "ExternalInput")
    eps_in = _dio(nc, reg, "eps", [1, 1], F32, "ExternalInput")
    s1_dram = nc.dram_tensor("s1_scr", [D, TOK], BF16, kind="Internal").ap()
    g_dram = nc.dram_tensor("g_scr", [D, TOK], BF16, kind="Internal").ap()
    x2_out = _dio(nc, reg, "x2", [D, TOK], F32, "ExternalOutput")
    xn2_out = _dio(nc, reg, "xn2", [D, TOK], BF16, "ExternalOutput")
    es = contextlib.ExitStack()
    with es:
        if standalone:
            kb = KB(nc, es)
        else:
            kb = kb0
            kb.es = es
        ARE = es.enter_context(nc.sbuf_tensor("F_ARE", [128, 20480], F32))
        kb.suppress.add("F_ARE")
        lrows = es.enter_context(nc.sbuf_tensor("F_lrows", [1, 12288], F32))
        P = es.enter_context(nc.sbuf_tensor("F_P", [128, 2048], F32))
        PP = es.enter_context(nc.sbuf_tensor("F_PP", [128, 2048], F32))
        xn = es.enter_context(nc.sbuf_tensor("F_xn_sb", [128, NC_D, TOK], BF16))
        tch = es.enter_context(nc.sbuf_tensor("F_tch", [128, 1024], F32))
        acolp = es.enter_context(nc.sbuf_tensor("F_acolp", [128, NC_D], F32))
        acolr = es.enter_context(nc.sbuf_tensor("F_acolr", [128, NC_D], F32))
        adiff = es.enter_context(nc.sbuf_tensor("F_adiff", [128, NC_D], F32))
        dcol = es.enter_context(nc.sbuf_tensor("F_dcol", [128, NC_D, 1], F32))
        crow = es.enter_context(nc.sbuf_tensor("F_crow", [1, 128], F32))
        crbf = es.enter_context(nc.sbuf_tensor("F_crbf", [1, 128], BF16))
        cbf = es.enter_context(nc.sbuf_tensor("F_cbf", [128, 1], BF16))
        ceps = es.enter_context(nc.sbuf_tensor("F_ceps", [1, 1], F32))
        ident = es.enter_context(nc.sbuf_tensor("F_ident_sb", [128, 128], BF16))
        clrow = es.enter_context(nc.sbuf_tensor("F_clrow", [1, 2048], F32))
        ctot = es.enter_context(nc.sbuf_tensor("F_ctot", [1, 2048], F32))
        wg_sb = es.enter_context(nc.sbuf_tensor("F_wg_sb", [128, NC_D, 8], BF16))
        ebb = es.enter_context(nc.sbuf_tensor("F_ebb", [128, 8], F32))
        lg = es.enter_context(nc.sbuf_tensor("F_lg", [128, NTT, 8], F32))
        gtm = es.enter_context(nc.sbuf_tensor("F_gtm", [128, NTT, 8], F32))
        gsc = es.enter_context(nc.sbuf_tensor("F_gsc", [128, 64], F32))
        gbf = es.enter_context(nc.sbuf_tensor("F_gbf", [128, 1], BF16))
        grow = es.enter_context(nc.sbuf_tensor("F_grow", [1, W], BF16))
        gb = es.enter_context(nc.sbuf_tensor("F_gb", [128, W], BF16))
        silb = es.enter_context(nc.sbuf_tensor("F_silb", [128, W], BF16))
        xbc = es.enter_context(nc.sbuf_tensor("F_xbc", [128, 1024], BF16))
        sqc = es.enter_context(nc.sbuf_tensor("F_sqc", [128, 1024], BF16))
        tmpc = es.enter_context(nc.sbuf_tensor("F_tmpc", [128, 512], F32))
        C = {"ones_bf_col": cbf[:], "ones_f32_row": crow[:], "eps": ceps[:]}
        pb = [es.enter_context(nc.psum_tensor(f"F_pb{i}", [128, 512], F32))
              for i in range(4)]
        po = [es.enter_context(nc.psum_tensor(f"F_po{i}", [128, 512], F32))
              for i in range(2)]
        ptb = es.enter_context(nc.psum_tensor("F_ptb", [128, 128], BF16))
        A = ARE[:]
        Ab = A.bitcast(BF16)
        are_sin = A[:, 8192:12288].rearrange("p (c f) -> p c f", c=NC_D)
        stg = A[:, 12288:14336][0:64, :]
        stq = A[:, 14336:16384][0:64, :]
        tqv = A[:, 16384:18432]
        pbc = A[:, 18432:20480]
        chre = Ab[:, 0:8192].rearrange("p (c f) -> p c f", c=NC_D)
        chim = Ab[:, 8192:16384].rearrange("p (c f) -> p c f", c=NC_D)
        trig = Ab[:, 16384:24576].rearrange("p (c f) -> p c f", c=NC_D)
        cwtb = Ab[:, 24576:32768].rearrange("p (c f) -> p c f", c=NC_D)
        hsel = A[:, 8192:16384][0:1, :]
        l2048 = A[:, 16384:18432][0:1, :]
        growE = A[:, 18432:20480][0:1, :]
        hfm = Ab[:, 16384:24576].rearrange("p (c f) -> p c f", c=16)
        zt = Ab[:, 24576:26624]
        ht = Ab[:, 26624:28672]
        s1st = Ab[:, 28672:28672 + NC_D * W].rearrange(
            "p (c f) -> p c f", c=NC_D)
        PW = min(1024, TOK)
        NP = TOK // PW
        NW = PW // W
        wslot2 = Ab[:, 0:11264].rearrange("p (c f) -> p c f", c=NHC)
        h1v = Ab[:, 11264:11264 + NHC * PW].rearrange(
            "p (c f) -> p c f", c=NHC * NW)
        oaccv = Ab[:, 22528:22528 + NC_D * TOK].rearrange(
            "p (c f) -> p c f", c=NC_D)
        gbp = Ab[:, 38912:38912 + NW * W].rearrange(
            "p (c f) -> p c f", c=NW)
        w1s = P[:].bitcast(BF16)
        w3s = PP[:].bitcast(BF16)
        x2v = A[:, 0:NC_D * TOK].rearrange("p (c t) -> p c t", c=NC_D)
        bcb = A[:, 16384:16384 + 2 * TOK]
        x1ch = A[:, 16384:16384 + TOK]
        s1ch = A[:, 16384 + TOK:16384 + TOK + TOK // 2].bitcast(BF16)
        gch = A[:, 16384 + TOK + TOK // 2:16384 + 2 * TOK].bitcast(BF16)

        kb.dma("sync", crow[:], ones_f32_row)
        kb.dma("gpsimd", crbf[:], ones_f32_row)
        kb.dma("gpsimd", cbf[:], ones_bf_col)
        kb.dma("sync", ceps[:], eps_in)
        kb.dma("gpsimd", ident[:], id_in)
        kb.dma("sync", dcol[:].rearrange("p c o -> p (c o)"), dcol_in)
        kb.dma("sync", xn[:], xn_in.rearrange("(c p) t -> p c t", p=128))
        kb.dma("gpsimd", acolp[:], apos_in)
        kb.dma("gpsimd", acolr[:], arev_in)
        kb.dma("gpsimd", wg_sb[:], gw_in.rearrange("(c p) e -> p c e", p=128))
        kb.v("tensor_sub", adiff[:], acolr[:], acolp[:])
        emit_omega(kb, lrows[:], adiff[:], ARE, are_sin, sin_in, pb[0][:],
                   "AF:sin")
        kb.fence(r=["AF:sin"], w=["AF:scr"])
        emit_lambda_tables(kb, C, lrows[:], P[:], PP[:], stg, stq, tqv, pbc,
                           pb[0][:], pb[1][:], "AF:scr")
        kb.fence(r=["AF:scr"], w=["AF:trig", "AF:cwt", "AF:chat"])
        kb.dma("gpsimd", cwtb, cwt_in.rearrange("(c p) f -> p c f", p=128),
               xw=["AF:cwt"])
        for part in range(2):
            tin = cos_in if part == 0 else sin_in
            dst = chre if part == 0 else chim
            sgn = (1.0 / 1024.0) if part == 0 else (-1.0 / 1024.0)
            kb.dma("gpsimd", trig, tin.rearrange("(c p) f -> p c f", p=128),
                   xw=["AF:trig"])
            for fc in range(NC_D):
                for q in range(2):
                    ps = pb[2 + q]
                    for nch in range(NC_D):
                        kb.mm(ps[:], trig[:, nch, fc * 128:(fc + 1) * 128],
                              cwtb[:, nch, q * 512:(q + 1) * 512],
                              start=(nch == 0), stop=(nch == NC_D - 1),
                              xr=["AF:trig", "AF:cwt"])
                    kb.v("tensor_scalar", dst[:, fc, q * 512:(q + 1) * 512],
                         ps[:], sgn, None, op0=ALU.mult, xw=["AF:chat"])
        kb.fence(r=["AF:trig", "AF:cwt"], w=["AF:erow"])
        if hs_in is None:
            hallsb = A[:, 16384:18432][0:4, :]
            selwsb = A[:, 18432:18436][0:4, :]
            kb.dma("sync", hallsb, reg["hall"][0:4, :], xw=["AF:erow"])
            kb.dma("sync", selwsb, selw_in, xw=["AF:erow"])
            for k in range(4):
                for q in range(4):
                    kb.mm(pb[0][0:1, 0:512], selwsb[:, k:k + 1],
                          hallsb[:, q * 512:(q + 1) * 512], xr=["AF:erow"])
                    kb.v("tensor_copy",
                         hsel[:, k * 2048 + q * 512:k * 2048 + (q + 1) * 512],
                         pb[0][0:1, 0:512], xw=["AF:erow"])
        else:
            kb.dma("sync", hsel, hs_in, xw=["AF:erow"])
        L = lrows[:]
        kb.v("tensor_copy", L[:, LR_LM:LR_LM + 2048], L[:, LR_LC:LR_LC + 2048])
        for _ in range(int(np.log2(NTT))):
            emit_row_cmul(kb, L, LR_LM, LR_LM, LR_LM, LR_T1)
        kb.v("tensor_copy", l2048, L[:, LR_LM:LR_LM + 2048], xw=["AF:erow"])
        kb.v("memset", L[:, LR_CUR:LR_CUR + 2048], 0.0)
        kb.v("memset", L[:, LR_LM:LR_LM + 1024], 1.0)
        kb.v("memset", L[:, LR_LM + 1024:LR_LM + 2048], 0.0)
        for k in range(4):
            hk = hsel[:, k * 2048:(k + 1) * 2048]
            kb.v("tensor_mul", L[:, LR_T1:LR_T1 + 1024],
                 L[:, LR_LM:LR_LM + 1024], hk[:, 0:1024], xr=["AF:erow"])
            kb.v("tensor_mul", L[:, LR_T1 + 1024:LR_T1 + 2048],
                 L[:, LR_LM + 1024:LR_LM + 2048], hk[:, 1024:2048],
                 xr=["AF:erow"])
            kb.v("tensor_sub", L[:, LR_T2:LR_T2 + 1024],
                 L[:, LR_T1:LR_T1 + 1024], L[:, LR_T1 + 1024:LR_T1 + 2048])
            kb.v("tensor_mul", L[:, LR_T1:LR_T1 + 1024],
                 L[:, LR_LM:LR_LM + 1024], hk[:, 1024:2048], xr=["AF:erow"])
            kb.v("tensor_mul", L[:, LR_T1 + 1024:LR_T1 + 2048],
                 L[:, LR_LM + 1024:LR_LM + 2048], hk[:, 0:1024], xr=["AF:erow"])
            kb.v("tensor_add", L[:, LR_T2 + 1024:LR_T2 + 2048],
                 L[:, LR_T1:LR_T1 + 1024], L[:, LR_T1 + 1024:LR_T1 + 2048])
            kb.v("tensor_add", L[:, LR_CUR:LR_CUR + 2048],
                 L[:, LR_CUR:LR_CUR + 2048], L[:, LR_T2:LR_T2 + 2048])
            if k < 3:
                kb.v("tensor_copy", L[:, LR_LAM:LR_LAM + 2048], l2048,
                     xr=["AF:erow"])
                emit_row_cmul(kb, L, LR_LM, LR_LM, LR_LAM, LR_T1)
        kb.fence(r=["AF:erow", "AF:chat"], w=["AF:hfm", "AF:zt", "AF:s1"])
        for tc in range(NTC):
            for ti in range(TPC):
                t = tc * TPC + ti
                kb.dma("sync", zt, zc_in[t * 128:(t + 1) * 128, :],
                       xw=["AF:zt"])
                kb.dma("sync", clrow[:], cl_in[:, t * 2048:(t + 1) * 2048])
                kb.v("tensor_add", ctot[:], clrow[:], L[:, LR_CUR:LR_CUR + 2048])
                emit_row_cmul(kb, L, LR_CUR, LR_LC, LR_CUR, LR_T1)
                for q in range(4):
                    kb.mm(pb[q][:], crow[:], ctot[:, q * 512:(q + 1) * 512])
                for q in range(2):
                    zre = zt[:, q * 512:(q + 1) * 512]
                    zim = zt[:, 1024 + q * 512:1024 + (q + 1) * 512]
                    pre = P[:, q * 512:(q + 1) * 512]
                    pim = P[:, 1024 + q * 512:1024 + (q + 1) * 512]
                    ppre = PP[:, q * 512:(q + 1) * 512]
                    ppim = PP[:, 1024 + q * 512:1024 + (q + 1) * 512]
                    cbre = pb[q][:]
                    cbim = pb[2 + q][:]
                    t1 = tch[:, 0:512]
                    t2 = tch[:, 512:1024]
                    kb.v("tensor_mul", t1, pre, zre, xr=["AF:zt"])
                    kb.v("tensor_mul", t2, pim, zim, xr=["AF:zt"])
                    kb.v("tensor_sub", t1, t1, t2)
                    kb.v("tensor_mul", t2, ppre, cbre)
                    kb.v("tensor_add", t1, t1, t2)
                    kb.v("tensor_mul", t2, ppim, cbim)
                    kb.v("tensor_sub", ht[:, q * 512:(q + 1) * 512], t1, t2,
                         xw=["AF:zt"])
                    kb.v("tensor_mul", t1, pre, zim, xr=["AF:zt"])
                    kb.v("tensor_mul", t2, pim, zre, xr=["AF:zt"])
                    kb.v("tensor_add", t1, t1, t2)
                    kb.v("tensor_mul", t2, ppre, cbim)
                    kb.v("tensor_add", t1, t1, t2)
                    kb.v("tensor_mul", t2, ppim, cbre)
                    kb.v("tensor_add",
                         ht[:, 1024 + q * 512:1024 + (q + 1) * 512], t1, t2,
                         xw=["AF:zt"])
                for fch in range(16):
                    kb.trans(ptb[:], ht[:, fch * 128:(fch + 1) * 128],
                             ident[:], xr=["AF:zt"])
                    kb.v("tensor_copy", hfm[:, fch, ti * 128:(ti + 1) * 128],
                         ptb[:], xw=["AF:hfm"])
            for dc in range(NC_D):
                ps = po[dc % 2]
                for fch in range(16):
                    lhs = (chre if fch < 8 else chim)[:, fch % 8,
                                                      dc * 128:(dc + 1) * 128]
                    kb.mm(ps[:, 0:W], lhs, hfm[:, fch, 0:W],
                          start=(fch == 0), stop=(fch == 15),
                          xr=["AF:hfm", "AF:chat"])
                kb.v("scalar_tensor_tensor", s1st[:, dc, 0:W],
                     xn[:, dc, tc * W:tc * W + W], dcol[:, dc, :], ps[:, 0:W],
                     op0=ALU.mult, op1=ALU.add, xw=["AF:s1"])
            kb.dma("sync",
                   s1_dram.rearrange("(c p) t -> p c t", p=128)[:, :,
                       tc * W:tc * W + W],
                   s1st[:, :, 0:W], xr=["AF:s1"])
        kb.dma("sync", ebb[0:1, :], eb_in)
        kb.mm(pb[0][:, 0:8], crow[:], ebb[0:1, :])
        kb.v("tensor_copy", ebb[:], pb[0][:, 0:8])
        for tt in range(NTT):
            for dc in range(NC_D):
                kb.mm(pb[1][:, 0:8], xn[:, dc, tt * 128:(tt + 1) * 128],
                      wg_sb[:, dc, :], start=(dc == 0), stop=(dc == NC_D - 1))
            kb.v("tensor_copy", lg[:, tt, :], pb[1][:, 0:8])
        G = gsc
        for tt in range(NTT):
            l = lg[:, tt, :]
            m1 = G[:, 0:1]; sc = G[:, 8:16]; tmp = G[:, 16:24]
            m2 = G[:, 1:2]; eq1 = G[:, 24:32]; eq2 = G[:, 32:40]
            ssum = G[:, 2:3]; bia = G[:, 40:48]
            kb.v("tensor_reduce", m1.rearrange("p (a b) -> p a b", a=1),
                 l.rearrange("p (a b) -> p a b", a=1), AX.X, ALU.max)
            kb.v("tensor_sub", sc, l, m1.broadcast_to([128, 8]))
            kb.act(sc, sc, AF.Exp)
            kb.v("tensor_reduce", ssum.rearrange("p (a b) -> p a b", a=1),
                 sc.rearrange("p (a b) -> p a b", a=1), AX.X, ALU.add)
            kb.v("reciprocal", ssum, ssum)
            kb.v("tensor_scalar", sc, sc, ssum, None, op0=ALU.mult)
            kb.v("tensor_add", bia, sc, ebb[:])
            kb.v("tensor_reduce", m1.rearrange("p (a b) -> p a b", a=1),
                 bia.rearrange("p (a b) -> p a b", a=1), AX.X, ALU.max)
            kb.v("tensor_tensor", eq1, bia, m1.broadcast_to([128, 8]),
                 op=ALU.is_equal)
            kb.v("scalar_tensor_tensor", tmp, eq1, -1e30, bia,
                 op0=ALU.mult, op1=ALU.add)
            kb.v("tensor_reduce", m2.rearrange("p (a b) -> p a b", a=1),
                 tmp.rearrange("p (a b) -> p a b", a=1), AX.X, ALU.max)
            kb.v("tensor_tensor", eq2, tmp, m2.broadcast_to([128, 8]),
                 op=ALU.is_equal)
            kb.v("tensor_add", eq1, eq1, eq2)
            kb.v("tensor_mul", gtm[:, tt, :], sc, eq1)
        kb.fence(r=["AF:sin", "AF:scr", "AF:trig", "AF:cwt", "AF:chat",
                    "AF:erow", "AF:hfm", "AF:zt", "AF:s1"],
                 w=["AF:w", "AF:h1", "AF:oacc", "AF:gb"])
        kb.fence(w=["F_P", "F_PP", "AF:w1c0", "AF:w1c1", "AF:w3c0",
                    "AF:w3c1"])
        kb.suppress.update({"F_P", "F_PP"})
        tmpcb = tmpc[:].bitcast(BF16)
        for e in range(8):
            for hh in range(2):
                c0 = hh * HH
                kb.dma("gpsimd", wslot2[:, 0:NHC, :],
                       w2_in[e][c0:c0 + HH, :].rearrange(
                           "(c p) f -> p c f", p=128), xw=["AF:w"])
                for p in range(NP):
                    for th2 in range(NW):
                        t0 = p * PW + th2 * W
                        for ti in range(W // 128):
                            tl = (t0 + ti * 128) // 128
                            kb.v("tensor_copy", gbf[:], gtm[:, tl, e:e + 1])
                            kb.trans(ptb[0:1, 0:128], gbf[:], ident[:])
                            kb.v("tensor_copy",
                                 grow[:, ti * 128:(ti + 1) * 128],
                                 ptb[0:1, 0:128])
                        kb.mm(pb[3][:, 0:W], crbf[:], grow[:])
                        kb.v("tensor_copy", gbp[:, th2, :], pb[3][:, 0:W],
                             xw=["AF:gb"])
                    for hc in range(NHC):
                        bf = hc % 2
                        kb.dma("gpsimd",
                               w1s[:, bf * 1024:bf * 1024 + 1024].rearrange(
                                   "p (c f) -> p c f", c=NC_D),
                               w1_in[e][:, c0 + hc * 128:c0 + hc * 128 + 128]
                               .rearrange("(c p) f -> p c f", p=128),
                               xw=[f"AF:w1c{bf}"])
                        kb.dma("gpsimd",
                               w3s[:, bf * 1024:bf * 1024 + 1024].rearrange(
                                   "p (c f) -> p c f", c=NC_D),
                               w3_in[e][:, c0 + hc * 128:c0 + hc * 128 + 128]
                               .rearrange("(c p) f -> p c f", p=128),
                               xw=[f"AF:w3c{bf}"])
                        for th2 in range(NW):
                            t0 = p * PW + th2 * W
                            idx = hc * NW + th2
                            psA = pb[(2 * idx) % 4]
                            psB = pb[(2 * idx + 1) % 4]
                            for dc in range(NC_D):
                                kb.mm(psA[:, 0:W],
                                      w1s[:, bf * 1024 + dc * 128:
                                          bf * 1024 + dc * 128 + 128],
                                      xn[:, dc, t0:t0 + W], start=(dc == 0),
                                      stop=(dc == NC_D - 1),
                                      xr=[f"AF:w1c{bf}"])
                            for dc in range(NC_D):
                                kb.mm(psB[:, 0:W],
                                      w3s[:, bf * 1024 + dc * 128:
                                          bf * 1024 + dc * 128 + 128],
                                      xn[:, dc, t0:t0 + W], start=(dc == 0),
                                      stop=(dc == NC_D - 1),
                                      xr=[f"AF:w3c{bf}"])
                            kb.v("scalar_tensor_tensor", tmpcb[:, 0:W],
                                 psA[:, 0:W], 1.0, gbp[:, th2, :],
                                 op0=ALU.mult, op1=ALU.mult, xr=["AF:gb"])
                            kb.act(tmpcb[:, 0:W], tmpcb[:, 0:W], AF.Silu)
                            kb.v("scalar_tensor_tensor", tmpcb[:, 512:512 + W],
                                 psB[:, 0:W], 1.0, gbp[:, th2, :],
                                 op0=ALU.mult, op1=ALU.mult, xr=["AF:gb"])
                            kb.v("tensor_mul", h1v[:, idx, :],
                                 tmpcb[:, 0:W], tmpcb[:, 512:512 + W],
                                 xw=["AF:h1"])
                    for th2 in range(NW):
                        t0 = p * PW + th2 * W
                        for dc in range(NC_D):
                            ps = po[dc % 2]
                            for hc in range(NHC):
                                kb.mm(ps[:, 0:W],
                                      wslot2[:, hc, dc * 128:(dc + 1) * 128],
                                      h1v[:, hc * NW + th2, :],
                                      start=(hc == 0), stop=(hc == NHC - 1),
                                      xr=["AF:w", "AF:h1"])
                            if e == 0 and hh == 0:
                                kb.v("tensor_copy",
                                     oaccv[:, dc, t0:t0 + W], ps[:, 0:W],
                                     xw=["AF:oacc"])
                            else:
                                kb.v("tensor_add", oaccv[:, dc, t0:t0 + W],
                                     oaccv[:, dc, t0:t0 + W], ps[:, 0:W],
                                     xr=["AF:oacc"], xw=["AF:oacc"])
        for th in range(NTH):
            t0 = th * W
            for dc in range(NC_D):
                kb.act(silb[:, 0:W], oaccv[:, dc, t0:t0 + W], AF.Silu,
                       xr=["AF:oacc"])
                kb.dma("sync",
                       g_dram.rearrange("(c p) t -> p c t", p=128)[:, :,
                           t0:t0 + W][:, dc],
                       silb[:, 0:W])
        kb.fence(r=["AF:w", "AF:h1", "AF:oacc", "AF:gb"],
                 w=["AF:x2", "AF:chk"])
        for dc in range(NC_D):
            kb.dma("sync", x1ch[:, 0:TOK], x_in[dc * 128:(dc + 1) * 128, :],
                   xw=["AF:chk"])
            kb.dma("sync", s1ch[:, 0:TOK], s1_dram[dc * 128:(dc + 1) * 128, :],
                   xw=["AF:chk"])
            kb.dma("sync", gch[:, 0:TOK], g_dram[dc * 128:(dc + 1) * 128, :],
                   xw=["AF:chk"])
            kb.v("tensor_mul", x2v[:, dc], s1ch[:, 0:TOK], gch[:, 0:TOK],
                 xr=["AF:chk"], xw=["AF:x2"])
            kb.v("tensor_add", x2v[:, dc], x2v[:, dc], x1ch[:, 0:TOK],
                 xr=["AF:chk", "AF:x2"], xw=["AF:x2"])
            kb.dma("sync", x2_out[dc * 128:(dc + 1) * 128, :], x2v[:, dc],
                   xr=["AF:x2"])
        kb.fence(r=["AF:chk", "AF:oacc", "AF:gb"], w=["AF:bcb"])
        emit_ln(kb, C, x2v, xn[:], TOK, "ln2", xbc[:], sqc[:],
                bcb, tmpc[:], pb[0][:], pb[1][:], xrole=["AF:x2"],
                bcrole=["AF:bcb"], rows_ap=lrows[:, 0:3 * TOK])
        kb.dma("sync", xn2_out.rearrange("(c p) t -> p c t", p=128), xn[:])
        if standalone:
            kb.finalize()
    return nc


def _dft_consts():
    jj = np.arange(1024)
    ang = 2.0 * np.pi * np.outer(jj, jj) / 1024.0
    return (np.cos(ang).astype(np.float32), np.sin(ang).astype(np.float32))


def run_L1(inputs, TOK, runner, x1_sh, xn1_sh):
    CONST = _const_inputs()
    NTT = TOK // 128
    COS, SIN = _dft_consts()
    a_params = np.asarray(inputs["L1_attn_a_params"])
    apos = np.zeros(1024, np.float32); apos[1:512] = a_params[:511]
    arev = np.zeros(1024, np.float32); arev[513:] = a_params[:511][::-1]
    apos_h = apos.reshape(NC_D, 128).T.copy()
    arev_h = arev.reshape(NC_D, 128).T.copy()
    B_w = np.ascontiguousarray(np.asarray(inputs["L1_attn_B_w"]))
    maps = []
    for c in range(NCORES):
        maps.append({
            "xn1": xn1_sh[c], "apos": apos_h, "arev": arev_h, "B_w": B_w,
            "cosF": COS, "sinF": SIN,
            "ut128": np.triu(np.ones((128, 128), np.float32)),
            "ones_f32_row": CONST["ones_f32_row"],
        })
    resE = runner(build_E(TOK), maps)
    Hs = [np.asarray(r["hrow"]).reshape(2048) for r in resE]
    C_wT = np.ascontiguousarray(np.asarray(inputs["L1_attn_C_w"]).T)
    Dv = np.asarray(inputs["L1_attn_D"]).reshape(NC_D, 128).T.copy()
    gwT = np.ascontiguousarray(np.asarray(inputs["L1_ffn_gate_w"]).T)
    eb = np.asarray(inputs["L1_ffn_expert_bias"])[None, :].copy()
    w1 = np.ascontiguousarray(np.asarray(inputs["L1_ffn_w1"]))
    w2 = np.ascontiguousarray(np.asarray(inputs["L1_ffn_w2"]))
    w3 = np.ascontiguousarray(np.asarray(inputs["L1_ffn_w3"]))
    maps = []
    for c in range(NCORES):
        b, p = c // 4, c % 4
        hsel = np.zeros((1, 8192), np.float32)
        for k in range(4):
            i2 = p - 1 - k
            if i2 >= 0:
                hsel[0, k * 2048:(k + 1) * 2048] = Hs[b * 4 + i2]
        maps.append({
            "zcum": np.asarray(resE[c]["zcum"]),
            "cloc": np.asarray(resE[c]["cloc"]),
            "Hsel": hsel, "apos": apos_h, "arev": arev_h,
            "cosF": COS, "sinF": SIN, "C_wT": C_wT, "Dcol": Dv,
            "xn1": xn1_sh[c], "x1": x1_sh[c], "gwT": gwT, "ebias": eb,
            "w1": w1, "w2": w2, "w3": w3,
            "ident": np.eye(128, dtype=np.float32), **CONST,
        })
    resF = runner(build_F(TOK), maps)
    return ([np.asarray(r["x2"]) for r in resF],
            [np.asarray(r["xn2"]) for r in resF])



def build_G(TOK, ctx=None):
    W = min(512, TOK)
    NTH = TOK // W
    NTT = TOK // 128
    nc, es0, kb0, reg, standalone = _ctx_or_new(ctx)
    xn_in = _dio(nc, reg, "xn2", [D, TOK], BF16, "ExternalInput")
    x_in = _dio(nc, reg, "x2", [D, TOK], F32, "ExternalInput")
    wqkv_in = _dio(nc, reg, "WqkvT", [D, 3072], F32, "ExternalInput")
    wproj_in = _dio(nc, reg, "WprojT", [D, D], F32, "ExternalInput")
    bproj_in = _dio(nc, reg, "bprojc", [128, NC_D], F32, "ExternalInput")
    sig_in = _dio(nc, reg, "tile_sigs", [8, 32], F32, "ExternalInput")
    pe_in = _dio(nc, reg, "peT", [32, TOK], F32, "ExternalInput")
    wt_in = _dio(nc, reg, "WtT", [8, D, D], F32, "ExternalInput")
    bt_in = _dio(nc, reg, "bt", [8, D], F32, "ExternalInput")
    sel_in = _dio(nc, reg, "selmat", [8, 1024], F32, "ExternalInput")
    id_in = _dio(nc, reg, "ident", [128, 128], F32, "ExternalInput")
    ones_bf_col = _dio(nc, reg, "ones_bf_col", [128, 1], F32, "ExternalInput")
    ones_f32_row = _dio(nc, reg, "ones_f32_row", [1, 128], F32, "ExternalInput")
    eps_in = _dio(nc, reg, "eps", [1, 1], F32, "ExternalInput")
    x3_out = _dio(nc, reg, "x3", [D, TOK], F32, "ExternalOutput")
    xn3_out = _dio(nc, reg, "xn3", [D, TOK], BF16, "ExternalOutput")
    es = contextlib.ExitStack()
    with es:
        if standalone:
            kb = KB(nc, es)
        else:
            kb = kb0
            kb.es = es
        xn = es.enter_context(nc.sbuf_tensor("G_xn_sb", [128, NC_D, TOK], BF16))
        AG = es.enter_context(nc.sbuf_tensor("G_AG", [128, 16384], F32))
        kb.suppress.add("G_AG")
        wqkv = AG[:].bitcast(BF16)[:, 0:NC_D * 3072].rearrange(
            "p (c f) -> p c f", c=NC_D)
        wproj = AG[:].bitcast(BF16)[:, NC_D * 3072:NC_D * 3072 + NC_D * 1024
                                    ].rearrange("p (c f) -> p c f", c=NC_D)
        s2 = es.enter_context(nc.sbuf_tensor("G_s2", [128, NC_D, TOK], BF16))
        wts = es.enter_context(nc.sbuf_tensor("G_wts", [128, NC_D, 1024], BF16))
        AG2 = es.enter_context(nc.sbuf_tensor("G_AG2", [128, 6144], F32))
        kb.suppress.add("G_AG2")
        R2 = ["G2:a"]
        qkvb = AG2[:].bitcast(BF16)[:, 0:3072]
        aat = AG2[:, 1536:2560]
        aab = AG2[:].bitcast(BF16)[:, 5120:6144]
        otf = AG2[:].bitcast(BF16)[:, 6144:7168]
        ptm = AG2[:].bitcast(BF16)[:, 7168:8192]
        sc = es.enter_context(nc.sbuf_tensor("G_sc", [128, 16, 16], F32))
        srd = es.enter_context(nc.sbuf_tensor("G_srd", [128, 16], F32))
        sig = es.enter_context(nc.sbuf_tensor("G_sig", [8, 32], F32))
        sigs = es.enter_context(nc.sbuf_tensor("G_sigs", [8, 32], BF16))
        sigt = es.enter_context(nc.sbuf_tensor("G_sigt", [32, 8], BF16))
        pet = es.enter_context(nc.sbuf_tensor("G_pet", [32, TOK], BF16))
        msk = es.enter_context(nc.sbuf_tensor("G_msk", [128, NTT, 8], F32))
        mbf = es.enter_context(nc.sbuf_tensor("G_mbf", [128, 8], BF16))
        m8 = es.enter_context(nc.sbuf_tensor("G_m8", [8, TOK], BF16))
        selm = es.enter_context(nc.sbuf_tensor("G_selm", [8, 1024], BF16))
        btb = es.enter_context(nc.sbuf_tensor("G_btb", [8, 1024], BF16))
        xe = AG2[:].bitcast(BF16)[:, 8192:8192 + NC_D * W].rearrange(
            "p (c f) -> p c f", c=NC_D)
        gbv = es.enter_context(nc.sbuf_tensor("G_gbv", [128, W], BF16))
        silb = es.enter_context(nc.sbuf_tensor("G_silb", [128, W], BF16))
        bpc = es.enter_context(nc.sbuf_tensor("G_bpc", [128, NC_D, 1], F32))
        ident = es.enter_context(nc.sbuf_tensor("G_ident_sb", [128, 128], BF16))
        cbf = es.enter_context(nc.sbuf_tensor("G_cbf", [128, 1], BF16))
        crow = es.enter_context(nc.sbuf_tensor("G_crow", [1, 128], F32))
        ceps = es.enter_context(nc.sbuf_tensor("G_ceps", [1, 1], F32))
        xbc = es.enter_context(nc.sbuf_tensor("G_xbc", [128, 1024], BF16))
        sqc = es.enter_context(nc.sbuf_tensor("G_sqc", [128, 1024], BF16))
        tmpc = es.enter_context(nc.sbuf_tensor("G_tmpc", [128, 512], F32))
        bcb = es.enter_context(nc.sbuf_tensor("G_bcb", [128, 2 * TOK], F32))

        C = {"ones_bf_col": cbf[:], "ones_f32_row": crow[:], "eps": ceps[:]}
        pq = [es.enter_context(nc.psum_tensor(f"G_pq{i}", [128, 512], F32))
              for i in range(6)]
        ptb = es.enter_context(nc.psum_tensor("G_ptb", [128, 128], BF16))
        kb.dma("sync", crow[:], ones_f32_row)
        kb.dma("gpsimd", cbf[:], ones_bf_col)
        kb.dma("sync", ceps[:], eps_in)
        kb.dma("gpsimd", ident[:], id_in)
        kb.dma("sync", xn[:], xn_in.rearrange("(c p) t -> p c t", p=128))
        kb.dma("gpsimd", wqkv, wqkv_in.rearrange("(c p) f -> p c f", p=128),
               xw=["AG:wq"])
        kb.dma("gpsimd", wproj, wproj_in.rearrange("(c p) f -> p c f", p=128),
               xw=["AG:wp"])
        kb.dma("sync", bpc[:].rearrange("p c o -> p (c o)"), bproj_in)
        kb.dma("sync", sig[:], sig_in)
        kb.dma("gpsimd", pet[:], pe_in)
        kb.dma("gpsimd", selm[:], sel_in)
        kb.dma("gpsimd", btb[:], bt_in)
        for tt in range(NTT):
            t0 = tt * 128
            for q6 in range(6):
                ps = pq[q6]
                for dc in range(NC_D):
                    kb.mm(ps[:], xn[:, dc, t0:t0 + 128],
                          wqkv[:, dc, q6 * 512:(q6 + 1) * 512],
                          start=(dc == 0), stop=(dc == NC_D - 1),
                          xr=["AG:wq"])
                kb.v("tensor_copy", qkvb[:, q6 * 512:(q6 + 1) * 512], ps[:], xr=["G2:a"], xw=["G2:a"])
            kb.v("tensor_scalar",
                 qkvb.rearrange("p (h x) -> p h x", h=16)[:, :, 0:64],
                 qkvb.rearrange("p (h x) -> p h x", h=16)[:, :, 0:64],
                 0.125, None, op0=ALU.mult, xr=["G2:a"], xw=["G2:a"])
            for h in range(16):
                qh = qkvb[:, h * 192:h * 192 + 64]
                for g in range(16):
                    kg = qkvb[:, g * 192 + 64:g * 192 + 128]
                    kb.v("tensor_mul", aat[:, 0:64], qh, kg,
                         xr=["G2:a"], xw=["G2:a"])
                    kb.v("tensor_reduce", sc[:, h:h + 1, g:g + 1],
                         aat[:, 0:64].rearrange("p (a b) -> p a b", a=1),
                         AX.X, ALU.add, xr=["G2:a"])
            kb.v("tensor_reduce", srd[:].rearrange("p (a b) -> p a b", a=16),
                 sc[:], AX.X, ALU.max)
            kb.v("tensor_sub", sc[:], sc[:],
                 srd[:].rearrange("p (a b) -> p a b", a=16).broadcast_to(
                     [128, 16, 16]))
            kb.act(sc[:].rearrange("p a b -> p (a b)"),
                   sc[:].rearrange("p a b -> p (a b)"), AF.Exp)
            kb.v("tensor_reduce", srd[:].rearrange("p (a b) -> p a b", a=16),
                 sc[:], AX.X, ALU.add)
            kb.v("reciprocal", srd[:], srd[:])
            for h in range(16):
                kb.v("tensor_scalar", sc[:, h, :], sc[:, h, :],
                     srd[:, h:h + 1], None, op0=ALU.mult)
            for h in range(16):
                acc = aat[:, h * 64:(h + 1) * 64]
                vg = qkvb[:, 128:192]
                kb.v("tensor_scalar", acc, vg,
                     sc[:, h, 0:1].rearrange("p a -> p a"), None, op0=ALU.mult,
                     xr=["G2:a"], xw=["G2:a"])
                for g in range(1, 16):
                    vg = qkvb[:, g * 192 + 128:g * 192 + 192]
                    kb.v("scalar_tensor_tensor", acc, vg,
                         sc[:, h, g:g + 1].rearrange("p a -> p a"), acc,
                         op0=ALU.mult, op1=ALU.add,
                         xr=["G2:a"], xw=["G2:a"])
            kb.v("tensor_copy", aab[:], aat[:], xr=["G2:a"], xw=["G2:a"])
            for ic in range(NC_D):
                kb.trans(ptb[:], aab[:, ic * 128:(ic + 1) * 128], ident[:], xr=["G2:a"], xw=["G2:a"])
                kb.v("tensor_copy", otf[:, ic * 128:(ic + 1) * 128], ptb[:], xr=["G2:a"], xw=["G2:a"])
            for half in range(2):
                ps = pq[half]
                for ic in range(NC_D):
                    kb.mm(ps[:], otf[:, ic * 128:(ic + 1) * 128],
                          wproj[:, ic, half * 512:(half + 1) * 512],
                          start=(ic == 0), stop=(ic == NC_D - 1),
                          xr=["G2:a", "AG:wp"])
                kb.v("tensor_copy", ptm[:, half * 512:(half + 1) * 512], ps[:], xr=["G2:a"], xw=["G2:a"])
            for dc in range(NC_D):
                kb.trans(ptb[:], ptm[:, dc * 128:(dc + 1) * 128], ident[:], xr=["G2:a"], xw=["G2:a"])
                kb.v("tensor_scalar", s2[:, dc, t0:t0 + 128], ptb[:],
                     bpc[:, dc, :], None, op0=ALU.add)
        kb.act(sigs[:], sig[:], AF.Sign)
        kb.trans(ptb[0:32, 0:8], sigs[:], ident[0:8, 0:8])
        kb.v("tensor_copy", sigt[:, 0:8], ptb[0:32, 0:8])
        for tt in range(NTT):
            kb.mm(pq[2][:, 0:8], pet[:, tt * 128:(tt + 1) * 128], sigt[:, 0:8])
            kb.v("tensor_copy", msk[:, tt, 0:8], pq[2][:, 0:8])
            kb.v("tensor_reduce", srd[:, 0:1].rearrange("p (a b) -> p a b", a=1),
                 msk[:, tt, 0:8].rearrange("p (a b) -> p a b", a=1),
                 AX.X, ALU.max)
            kb.v("tensor_tensor", msk[:, tt, 0:8], msk[:, tt, 0:8],
                 srd[:, 0:1].broadcast_to([128, 8]), op=ALU.is_equal)
            kb.v("tensor_copy", mbf[:], msk[:, tt, 0:8])
            kb.trans(ptb[0:8, 0:128], mbf[:], ident[:])
            kb.v("tensor_copy", m8[0:8, tt * 128:(tt + 1) * 128],
                 ptb[0:8, 0:128])
        yacv = AG[:, 0:NC_D * W].rearrange("p (c f) -> p c f", c=NC_D)
        kb.fence(r=["AG:wq", "AG:wp"], w=["AG:yac"])
        for th in range(NTH):
            t0 = th * W
            pse = pq[4]
            for t in range(8):
                kb.dma("gpsimd", wts[:],
                       wt_in[t].rearrange("(c p) f -> p c f", p=128))
                kb.mm(pse[:, 0:W], selm[:, t * 128:(t + 1) * 128],
                      m8[:, t0:t0 + W])
                kb.v("tensor_copy", gbv[:, 0:W], pse[:, 0:W])
                for dc in range(NC_D):
                    kb.v("tensor_mul", xe[:, dc, :], xn[:, dc, t0:t0 + W],
                         gbv[:, 0:W], xr=["G2:a"], xw=["G2:a"])
                for oc in range(NC_D):
                    ps = pq[oc % 2]
                    for dc in range(NC_D):
                        kb.mm(ps[:, 0:W], wts[:, dc, oc * 128:(oc + 1) * 128],
                              xe[:, dc, :], start=(dc == 0),
                              stop=(dc == NC_D - 1), xr=["G2:a"], xw=["G2:a"])
                    if t == 0:
                        kb.v("tensor_copy", yacv[:, oc, :], ps[:, 0:W],
                             xw=["AG:yac"])
                    else:
                        kb.v("tensor_add", yacv[:, oc, :], yacv[:, oc, :],
                             ps[:, 0:W], xr=["AG:yac"], xw=["AG:yac"])
            for oc in range(NC_D):
                ps = pq[2 + (oc % 2)]
                kb.mm(ps[:, 0:W], btb[:, oc * 128:(oc + 1) * 128],
                      m8[:, t0:t0 + W])
                kb.v("tensor_add", yacv[:, oc, :], yacv[:, oc, :], ps[:, 0:W],
                     xr=["AG:yac"], xw=["AG:yac"])
                kb.act(silb[:, 0:W], yacv[:, oc, :], AF.Silu, xr=["AG:yac"])
                kb.v("tensor_mul", s2[:, oc, t0:t0 + W],
                     s2[:, oc, t0:t0 + W], silb[:, 0:W])
        x3v = AG[:, 0:NC_D * TOK].rearrange("p (c t) -> p c t", c=NC_D)
        kb.fence(r=["AG:yac"], w=["AG:x3"])
        for dc in range(NC_D):
            kb.dma("sync", x3v[:, dc], x_in[dc * 128:(dc + 1) * 128, :],
                   xw=["AG:x3"])
            kb.v("tensor_add", x3v[:, dc], x3v[:, dc], s2[:, dc],
                 xr=["AG:x3"], xw=["AG:x3"])
            kb.dma("sync", x3_out[dc * 128:(dc + 1) * 128, :], x3v[:, dc],
                   xr=["AG:x3"])
        kb.fence(r=["G2:a"], w=["G2:rows"])
        emit_ln(kb, C, x3v, xn[:], TOK, "ln3", xbc[:], sqc[:], bcb[:],
                tmpc[:], pq[0][:], pq[1][:], xrole=["AG:x3"],
                rows_ap=AG2[0:1, 0:6144], rrole=["G2:rows"])
        kb.dma("sync", xn3_out.rearrange("(c p) t -> p c t", p=128), xn[:])
        if standalone:
            kb.finalize()
    return nc


def _pe_table(S):
    import math
    pos = np.arange(S, dtype=np.float32)[:, None]
    div = np.exp(np.arange(0, 32, 2, dtype=np.float32) * (-math.log(10000.0) / 32))
    ang = pos * div
    pe = np.zeros((S, 32), np.float32)
    pe[:, 0::2] = np.sin(ang)
    pe[:, 1::2] = np.cos(ang)
    return pe


def run_L2(inputs, TOK, runner, x2_sh, xn2_sh):
    CONST = _const_inputs()
    S = 4 * TOK
    pe = _pe_table(S)
    WqkvT = np.ascontiguousarray(np.asarray(inputs["L2_attn_Wqkv"]).T)
    WprojT = np.ascontiguousarray(np.asarray(inputs["L2_attn_Wproj"]).T)
    bprojc = np.asarray(inputs["L2_attn_bproj"]).reshape(NC_D, 128).T.copy()
    sigs = np.ascontiguousarray(np.asarray(inputs["L2_ffn_tile_sigs"]))
    WtT = np.ascontiguousarray(np.asarray(inputs["L2_ffn_W"]).transpose(0, 2, 1))
    bt = np.ascontiguousarray(np.asarray(inputs["L2_ffn_b"]))
    selmat = np.zeros((8, 1024), np.float32)
    for t in range(8):
        selmat[t, t * 128:(t + 1) * 128] = 1.0
    maps = []
    for c in range(NCORES):
        b, p = c // 4, c % 4
        maps.append({
            "xn2": xn2_sh[c], "x2": x2_sh[c],
            "WqkvT": WqkvT, "WprojT": WprojT, "bprojc": bprojc,
            "tile_sigs": sigs,
            "peT": np.ascontiguousarray(pe[p * TOK:(p + 1) * TOK].T),
            "WtT": WtT, "bt": bt, "selmat": selmat,
            "ident": np.eye(128, dtype=np.float32), **CONST,
        })
    resG = runner(build_G(TOK), maps)
    return ([np.asarray(r["x3"]) for r in resG],
            [np.asarray(r["xn3"]) for r in resG])



def build_H(TOK, ctx=None):
    CH = min(512, TOK)
    NCH = TOK // CH
    TPC = CH // 128
    NTT = TOK // 128
    GC = GAMMA ** CH
    nc, es0, kb0, reg, standalone = _ctx_or_new(ctx)
    xn_in = _dio(nc, reg, "xn3", [D, TOK], BF16, "ExternalInput")
    wk_in = _dio(nc, reg, "Wk", [D, D], F32, "ExternalInput")
    wv_in = _dio(nc, reg, "Wv", [D, D], F32, "ExternalInput")
    gc_in = _dio(nc, reg, "gcols", [128, NTT], F32, "ExternalInput")
    t_out = _dio(nc, reg, "Tcore", [D, D], F32, "ExternalOutput")
    es = contextlib.ExitStack()
    with es:
        if standalone:
            kb = KB(nc, es)
        else:
            kb = kb0
            kb.es = es
        xn = es.enter_context(nc.sbuf_tensor("H_xn_sb", [128, NC_D, TOK], BF16))
        wk = es.enter_context(nc.sbuf_tensor("H_wk", [128, NC_D, 1024], BF16))
        wv = es.enter_context(nc.sbuf_tensor("H_wv", [128, NC_D, 1024], BF16))
        gcols = es.enter_context(nc.sbuf_tensor("H_gcs", [128, NTT], F32))
        kg = es.enter_context(nc.sbuf_tensor("H_kg", [128, 4, 1024], BF16))
        vt = es.enter_context(nc.sbuf_tensor("H_vt", [128, 4, 1024], BF16))
        S = es.enter_context(nc.sbuf_tensor("H_S", [128, NC_D, 1024], F32))
        ps = [es.enter_context(nc.psum_tensor(f"H_ps{i}", [128, 512], F32))
              for i in range(4)]
        kb.dma("sync", xn[:], xn_in.rearrange("(c p) t -> p c t", p=128))
        kb.dma("gpsimd", wk[:], wk_in.rearrange("(c p) f -> p c f", p=128))
        kb.dma("gpsimd", wv[:], wv_in.rearrange("(c p) f -> p c f", p=128))
        kb.dma("sync", gcols[:], gc_in)
        for m in range(NCH):
            for ti in range(TPC):
                t = m * TPC + ti
                for half in range(2):
                    p1, p2 = ps[2 * half], ps[2 * half + 1]
                    for dc in range(NC_D):
                        kb.mm(p1[:], xn[:, dc, t * 128:(t + 1) * 128],
                              wk[:, dc, half * 512:(half + 1) * 512],
                              start=(dc == 0), stop=(dc == NC_D - 1))
                        kb.mm(p2[:], xn[:, dc, t * 128:(t + 1) * 128],
                              wv[:, dc, half * 512:(half + 1) * 512],
                              start=(dc == 0), stop=(dc == NC_D - 1))
                    kb.v("tensor_scalar",
                         kg[:, ti, half * 512:(half + 1) * 512], p1[:],
                         gcols[:, t:t + 1], None, op0=ALU.mult)
                    kb.v("tensor_copy", vt[:, ti, half * 512:(half + 1) * 512],
                         p2[:])
            for kdc in range(NC_D):
                for half in range(2):
                    p1 = ps[2 * half]
                    for ti in range(TPC):
                        kb.mm(p1[:], kg[:, ti, kdc * 128:(kdc + 1) * 128],
                              vt[:, ti, half * 512:(half + 1) * 512],
                              start=(ti == 0), stop=(ti == TPC - 1))
                    if m == 0:
                        kb.v("tensor_copy",
                             S[:, kdc, half * 512:(half + 1) * 512], p1[:])
                    else:
                        kb.v("scalar_tensor_tensor",
                             S[:, kdc, half * 512:(half + 1) * 512],
                             S[:, kdc, half * 512:(half + 1) * 512], GC,
                             p1[:], op0=ALU.mult, op1=ALU.add)
        kb.dma("sync", t_out.rearrange("(c p) f -> p c f", p=128), S[:])
        if standalone:
            kb.finalize()
    return nc



def build_I(TOK, ctx=None):
    CH = min(512, TOK)
    NCH = TOK // CH
    TPC = CH // 128
    NTT = TOK // 128
    W = min(512, TOK)
    NTH = TOK // W
    GC = GAMMA ** CH
    nc, es0, kb0, reg, standalone = _ctx_or_new(ctx)
    xn_in = _dio(nc, reg, "xn3", [D, TOK], BF16, "ExternalInput")
    x_in = _dio(nc, reg, "x3", [D, TOK], F32, "ExternalInput")
    wq_in = _dio(nc, reg, "Wq", [D, D], F32, "ExternalInput")
    wk_in = _dio(nc, reg, "Wk", [D, D], F32, "ExternalInput")
    wv_in = _dio(nc, reg, "Wv", [D, D], F32, "ExternalInput")
    ts_in = None if (reg is not None and "tall" in reg) else \
        _dio(nc, reg, "Tsel", [3, D, D], F32, "ExternalInput")
    cv_in = _dio(nc, reg, "cvecb", [128, 4], F32, "ExternalInput") \
        if ts_in is None else None
    gc_in = _dio(nc, reg, "gcols", [128, NTT], F32, "ExternalInput")
    gq_in = _dio(nc, reg, "gqrow", [1, CH], F32, "ExternalInput")
    mk_in = _dio(nc, reg, "maskT", [CH, CH], F32, "ExternalInput")
    wg_in = _dio(nc, reg, "Wg3", [D, 8], F32, "ExternalInput")
    bg_in = _dio(nc, reg, "bg", [1, 8], F32, "ExternalInput")
    w1_in = _dio(nc, reg, "W1", [8, D, 4096], F32, "ExternalInput")
    b1_in = _dio(nc, reg, "b1T", [4096, 8], F32, "ExternalInput")
    w2_in = _dio(nc, reg, "W2", [8, 4096, D], F32, "ExternalInput")
    b2_in = _dio(nc, reg, "b2", [8, D], F32, "ExternalInput")
    sel_in = _dio(nc, reg, "selmat", [8, 1024], F32, "ExternalInput")
    id_in = _dio(nc, reg, "ident", [128, 128], F32, "ExternalInput")
    ones_bf_col = _dio(nc, reg, "ones_bf_col", [128, 1], F32, "ExternalInput")
    ones_f32_row = _dio(nc, reg, "ones_f32_row", [1, 128], F32, "ExternalInput")
    s3_dram = nc.dram_tensor("s3_scr", [D, TOK], BF16, kind="Internal").ap()
    x4_out = _dio(nc, reg, "x4", [D, TOK], F32, "ExternalOutput")
    es = contextlib.ExitStack()
    with es:
        if standalone:
            kb = KB(nc, es)
        else:
            kb = kb0
            kb.es = es
        xn = es.enter_context(nc.sbuf_tensor("I_xn_sb", [128, NC_D, TOK], BF16))
        s3c = es.enter_context(nc.sbuf_tensor("I_s3c", [128, NC_D, CH], BF16))
        AI2 = es.enter_context(nc.sbuf_tensor("I_AI2", [128, 6144], F32))
        kb.suppress.add("I_AI2")
        AI2b = AI2[:].bitcast(BF16)
        qf = AI2b[:, 0:NC_D * CH].rearrange("p (c f) -> p c f", c=NC_D)
        qg = AI2b[:, 4096:4096 + NC_D * CH].rearrange("p (c f) -> p c f", c=NC_D)
        kf = AI2b[:, 8192:8192 + NC_D * CH].rearrange("p (c f) -> p c f", c=NC_D)
        kg = es.enter_context(nc.sbuf_tensor("I_kg", [128, TPC, 1024], BF16))
        vt = es.enter_context(nc.sbuf_tensor("I_vt", [128, TPC, 1024], BF16))
        S = es.enter_context(nc.sbuf_tensor("I_S", [128, NC_D, 1024], F32))
        Sbv = es.enter_context(nc.sbuf_tensor("I_Sbv", [128, NC_D, 128], BF16))
        stb = es.enter_context(nc.sbuf_tensor("I_stb", [128, TPC, CH], BF16))
        mskb = es.enter_context(nc.sbuf_tensor("I_mskb", [128, TPC, CH], BF16))
        gqb = es.enter_context(nc.sbuf_tensor("I_gqb", [128, CH], BF16))
        gqr = es.enter_context(nc.sbuf_tensor("I_gqr", [1, CH], F32))
        AI = es.enter_context(nc.sbuf_tensor("I_AI", [128, 24576], BF16))
        kb.suppress.add("I_AI")
        wqs = AI[:, 0:8192].rearrange("p (c f) -> p c f", c=NC_D)
        wks = AI[:, 8192:16384].rearrange("p (c f) -> p c f", c=NC_D)
        wvs = AI[:, 16384:24576].rearrange("p (c f) -> p c f", c=NC_D)
        gcols = es.enter_context(nc.sbuf_tensor("I_gcs", [128, NTT], F32))
        wg_sb = es.enter_context(nc.sbuf_tensor("I_wg_sb", [128, NC_D, 8], BF16))
        bgb = es.enter_context(nc.sbuf_tensor("I_bgb", [128, 8], F32))
        bgr = es.enter_context(nc.sbuf_tensor("I_bgr", [1, 8], F32))
        lg = es.enter_context(nc.sbuf_tensor("I_lg", [128, NTT, 8], F32))
        w1t = es.enter_context(nc.sbuf_tensor("I_w1t", [128, NTT, 1], F32))
        msk = es.enter_context(nc.sbuf_tensor("I_msk", [128, NTT, 8], F32))
        m8 = es.enter_context(nc.sbuf_tensor("I_m8", [8, TOK], BF16))
        m8c = es.enter_context(nc.sbuf_tensor("I_m8c", [8, TOK], BF16))
        mbf = es.enter_context(nc.sbuf_tensor("I_mbf", [128, 8], BF16))
        wrow = es.enter_context(nc.sbuf_tensor("I_wrow", [1, TOK], BF16))
        wtb = es.enter_context(nc.sbuf_tensor("I_wtb", [128, W], BF16))
        gsc = es.enter_context(nc.sbuf_tensor("I_gsc", [128, 64], F32))
        selm = es.enter_context(nc.sbuf_tensor("I_selm", [8, 1024], BF16))
        b1t = es.enter_context(nc.sbuf_tensor("I_b1t", [128, 32, 8], F32))
        sb1 = es.enter_context(nc.sbuf_tensor("I_sb1", [128, 32, 8], BF16))
        b2b = es.enter_context(nc.sbuf_tensor("I_b2b", [8, 1024], BF16))
        cst = es.enter_context(nc.sbuf_tensor("I_cst", [8, 1024], BF16))
        crow1 = es.enter_context(nc.sbuf_tensor("I_crow1", [1, 1024], BF16))
        wsl = AI[:, 0:16384]
        hsb = AI[:, 16384:16384 + 16 * W].rearrange("p (c f) -> p c f", c=16)
        yac = AI2[:, 0:NC_D * W // 2 * 2][:, 0:NC_D * W].rearrange(
            "p (c f) -> p c f", c=NC_D)
        xe = AI2b[:, 8192:8192 + NC_D * W].rearrange("p (c f) -> p c f", c=NC_D)
        gbv = es.enter_context(nc.sbuf_tensor("I_gbv", [128, W], BF16))
        silb = es.enter_context(nc.sbuf_tensor("I_silb", [128, W], BF16))
        tmpw = es.enter_context(nc.sbuf_tensor("I_tmpw", [128, W], F32))
        xch = es.enter_context(nc.sbuf_tensor("I_xch", [128, W], F32))
        ident = es.enter_context(nc.sbuf_tensor("I_ident_sb", [128, 128], BF16))
        cbf = es.enter_context(nc.sbuf_tensor("I_cbf", [128, 1], BF16))
        crow = es.enter_context(nc.sbuf_tensor("I_crow", [1, 128], F32))
        crbf = es.enter_context(nc.sbuf_tensor("I_crbf", [1, 128], BF16))
        pq = [es.enter_context(nc.psum_tensor(f"I_pq{i}", [128, 512], F32))
              for i in range(6)]
        ptb = es.enter_context(nc.psum_tensor("I_ptb", [128, 128], BF16))
        wsl1 = wsl.rearrange("p (c f) -> p c f", c=NC_D)
        wsl2 = wsl.rearrange("p (c f) -> p c f", c=16)
        Tch = wsl.bitcast(F32)[:, 0:4096].rearrange(
            "p (c f) -> p c f", c=NC_D)
        kb.dma("sync", crow[:], ones_f32_row)
        kb.dma("gpsimd", crbf[:], ones_f32_row)
        kb.dma("gpsimd", cbf[:], ones_bf_col)
        kb.dma("gpsimd", ident[:], id_in)
        kb.dma("sync", xn[:], xn_in.rearrange("(c p) t -> p c t", p=128))
        kb.dma("sync", gcols[:], gc_in)
        kb.dma("sync", gqr[:], gq_in)
        kb.mm(pq[0][:, 0:CH], crow[:], gqr[:])
        kb.v("tensor_copy", gqb[:], pq[0][:, 0:CH])
        kb.dma("gpsimd", mskb[:], mk_in.rearrange("(c p) f -> p c f", p=128))
        if ts_in is None:
            cvsb = es.enter_context(nc.sbuf_tensor("I_cv", [128, 4], F32))
            kb.dma("sync", cvsb[:], cv_in)
            for i2 in range(4):
                for half in range(2):
                    kb.dma("sync", Tch[:],
                           reg["tall"][i2][:, half * 512:(half + 1) * 512
                                           ].rearrange("(c p) f -> p c f",
                                                       p=128),
                           xr=["AI:f"], xw=["AI:f"])
                    for kdc in range(NC_D):
                        dst = S[:, kdc, half * 512:(half + 1) * 512]
                        if i2 == 0:
                            kb.v("tensor_scalar", dst, Tch[:, kdc, :],
                                 cvsb[:, i2:i2 + 1], None, op0=ALU.mult,
                                 xr=["AI:f"], xw=["AI:f"])
                        else:
                            kb.v("scalar_tensor_tensor", dst, Tch[:, kdc, :],
                                 cvsb[:, i2:i2 + 1], dst, op0=ALU.mult,
                                 op1=ALU.add, xr=["AI:f"], xw=["AI:f"])
        else:
            first = True
            for k in range(3):
                coef = float(np.float64(GAMMA) ** (TOK * k))
                if coef < 1e-38:
                    continue
                for half in range(2):
                    kb.dma("sync", Tch[:],
                           ts_in[k][:, half * 512:(half + 1) * 512].rearrange(
                               "(c p) f -> p c f", p=128), xr=["AI:f"], xw=["AI:f"])
                    for kdc in range(NC_D):
                        dst = S[:, kdc, half * 512:(half + 1) * 512]
                        if first:
                            kb.v("tensor_scalar", dst, Tch[:, kdc, :], coef, None,
                                 op0=ALU.mult, xr=["AI:f"], xw=["AI:f"])
                        else:
                            kb.v("scalar_tensor_tensor", dst, Tch[:, kdc, :],
                                 coef, dst, op0=ALU.mult, op1=ALU.add, xr=["AI:f"], xw=["AI:f"])
                first = False
            if first:
                kb.v("memset", S[:].rearrange("p c f -> p (c f)"), 0.0)
        kb.fence(r=["AI:f"], w=["AI:w"])
        kb.dma("gpsimd", wqs[:], wq_in.rearrange("(c p) f -> p c f", p=128), xr=["AI:w"], xw=["AI:w"])
        kb.dma("gpsimd", wks[:], wk_in.rearrange("(c p) f -> p c f", p=128), xr=["AI:w"], xw=["AI:w"])
        kb.dma("gpsimd", wvs[:], wv_in.rearrange("(c p) f -> p c f", p=128), xr=["AI:w"], xw=["AI:w"])
        for m in range(NCH):
            c0 = m * CH
            for dc in range(NC_D):
                ps = pq[dc % 2]
                for d2 in range(NC_D):
                    kb.mm(ps[:, 0:CH], wqs[:, d2, dc * 128:(dc + 1) * 128],
                          xn[:, d2, c0:c0 + CH],
                          start=(d2 == 0), stop=(d2 == NC_D - 1), xr=["AI:w"], xw=["AI:w"])
                kb.v("tensor_copy", qf[:, dc, :], ps[:, 0:CH], xr=["AI2:r"], xw=["AI2:r"])
                kb.v("tensor_mul", qg[:, dc, :], qf[:, dc, :], gqb[:, 0:CH], xr=["AI2:r"], xw=["AI2:r"])
                ps2 = pq[2 + dc % 2]
                for d2 in range(NC_D):
                    kb.mm(ps2[:, 0:CH], wks[:, d2, dc * 128:(dc + 1) * 128],
                          xn[:, d2, c0:c0 + CH],
                          start=(d2 == 0), stop=(d2 == NC_D - 1), xr=["AI:w"], xw=["AI:w"])
                kb.v("tensor_copy", kf[:, dc, :], ps2[:, 0:CH], xr=["AI2:r"], xw=["AI2:r"])
            for ti in range(TPC):
                t = m * TPC + ti
                for half in range(2):
                    ps = pq[4 + half]
                    for dc in range(NC_D):
                        kb.mm(ps[:], xn[:, dc, t * 128:(t + 1) * 128],
                              wvs[:, dc, half * 512:(half + 1) * 512],
                              start=(dc == 0), stop=(dc == NC_D - 1), xr=["AI:w"], xw=["AI:w"])
                    kb.v("tensor_copy", vt[:, ti, half * 512:(half + 1) * 512],
                         ps[:])
                    ps3 = pq[half]
                    for dc in range(NC_D):
                        kb.mm(ps3[:], xn[:, dc, t * 128:(t + 1) * 128],
                              wks[:, dc, half * 512:(half + 1) * 512],
                              start=(dc == 0), stop=(dc == NC_D - 1), xr=["AI:w"], xw=["AI:w"])
                    kb.v("tensor_scalar", kg[:, ti, half * 512:(half + 1) * 512],
                         ps3[:], gcols[:, t:t + 1], None, op0=ALU.mult)
            for jt in range(TPC):
                ps = pq[jt % 2]
                for dc in range(NC_D):
                    kb.mm(ps[:, 0:CH], kf[:, dc, jt * 128:(jt + 1) * 128],
                          qf[:, dc, :], start=(dc == 0), stop=(dc == NC_D - 1), xr=["AI2:r"], xw=["AI2:r"])
                kb.v("tensor_mul", stb[:, jt, :], ps[:, 0:CH], mskb[:, jt, :])
            for vc in range(NC_D):
                for kdc in range(NC_D):
                    kb.v("tensor_copy", Sbv[:, kdc, :],
                         S[:, kdc, vc * 128:(vc + 1) * 128])
                ps = pq[4 + (vc % 2)]
                for jt in range(TPC):
                    kb.mm(ps[:, 0:CH], vt[:, jt, vc * 128:(vc + 1) * 128],
                          stb[:, jt, :], start=(jt == 0), stop=False)
                for kdc in range(NC_D):
                    kb.mm(ps[:, 0:CH], Sbv[:, kdc, :], qg[:, kdc, :],
                          start=False, stop=(kdc == NC_D - 1), xr=["AI2:r"], xw=["AI2:r"])
                kb.v("tensor_copy", s3c[:, vc, :], ps[:, 0:CH])
            kb.dma("sync",
                   s3_dram.rearrange("(c p) t -> p c t", p=128)[:, :,
                       c0:c0 + CH], s3c[:])
            for kdc in range(NC_D):
                for half in range(2):
                    ps = pq[half]
                    for ti in range(TPC):
                        kb.mm(ps[:], kg[:, ti, kdc * 128:(kdc + 1) * 128],
                              vt[:, ti, half * 512:(half + 1) * 512],
                              start=(ti == 0), stop=(ti == TPC - 1))
                    kb.v("scalar_tensor_tensor",
                         S[:, kdc, half * 512:(half + 1) * 512],
                         S[:, kdc, half * 512:(half + 1) * 512], GC,
                         ps[:], op0=ALU.mult, op1=ALU.add)
        kb.dma("gpsimd", wg_sb[:], wg_in.rearrange("(c p) e -> p c e", p=128))
        kb.dma("sync", bgr[:], bg_in)
        kb.mm(pq[0][:, 0:8], crow[:], bgr[:])
        kb.v("tensor_copy", bgb[:], pq[0][:, 0:8])
        for tt in range(NTT):
            for dc in range(NC_D):
                kb.mm(pq[1][:, 0:8], xn[:, dc, tt * 128:(tt + 1) * 128],
                      wg_sb[:, dc, :], start=(dc == 0), stop=(dc == NC_D - 1))
            kb.v("tensor_copy", lg[:, tt, :], pq[1][:, 0:8])
        G2 = gsc
        for tt in range(NTT):
            l = lg[:, tt, :]
            m1 = G2[:, 0:1]; sc2 = G2[:, 8:16]; ssum = G2[:, 1:2]
            kb.v("tensor_add", sc2, l, bgb[:])
            kb.v("tensor_reduce", m1.rearrange("p (a b) -> p a b", a=1),
                 sc2.rearrange("p (a b) -> p a b", a=1), AX.X, ALU.max)
            kb.v("tensor_sub", sc2, sc2, m1.broadcast_to([128, 8]))
            kb.act(sc2, sc2, AF.Exp)
            kb.v("tensor_reduce", ssum.rearrange("p (a b) -> p a b", a=1),
                 sc2.rearrange("p (a b) -> p a b", a=1), AX.X, ALU.add)
            kb.v("reciprocal", ssum, ssum)
            kb.v("tensor_scalar", sc2, sc2, ssum, None, op0=ALU.mult)
            kb.v("tensor_reduce", w1t[:, tt, :].rearrange("p (a b) -> p a b", a=1),
                 sc2.rearrange("p (a b) -> p a b", a=1), AX.X, ALU.max)
            kb.v("tensor_tensor", msk[:, tt, :], sc2,
                 w1t[:, tt, :].broadcast_to([128, 8]), op=ALU.is_equal)
            kb.v("tensor_copy", mbf[:], msk[:, tt, :])
            kb.trans(ptb[0:8, 0:128], mbf[:], ident[:])
            kb.v("tensor_copy", m8[0:8, tt * 128:(tt + 1) * 128],
                 ptb[0:8, 0:128])
        kb.v("tensor_scalar", m8c[0:8, :], m8[0:8, :], -1.0, 1.0,
             op0=ALU.mult, op1=ALU.add)
        kb.dma("gpsimd", selm[:], sel_in)
        kb.dma("gpsimd", b2b[:], b2_in)
        kb.dma("sync", b1t[:], b1_in.rearrange("(c p) e -> p c e", p=128))
        kb.act(sb1[:].rearrange("p c e -> p (c e)"),
               b1t[:].rearrange("p c e -> p (c e)"), AF.Silu)
        kb.fence(r=["AI:w", "AI2:r"],
                 w=["AI:f", "AI:y", "AI2:f", "AI2:g", "AI2:z0", "AI2:z1",
                    "AI2:c0", "AI2:c1"])
        for tt in range(NTT):
            kb.v("tensor_copy", mbf[:, 0:1], w1t[:, tt, :])
            kb.trans(ptb[0:1, 0:128], mbf[:, 0:1], ident[:])
            kb.v("tensor_copy", wrow[:, tt * 128:(tt + 1) * 128],
                 ptb[0:1, 0:128])
        PW3 = min(1024, TOK)
        NP3 = TOK // PW3
        NW3 = PW3 // W
        h2v = S[:].rearrange("p c f -> p (c f)").bitcast(BF16).rearrange(
            "p (c f) -> p c f", c=16 * NW3)
        _ypools = [(kg[:].rearrange("p c f -> p (c f)"), TPC * 1024, None),
                   (vt[:].rearrange("p c f -> p (c f)"), TPC * 1024, None),
                   (AI[:, 16384:24576], 8192, "AI:y")]
        _yslabs = []
        _pi, _off = 0, 0
        for _oc in range(NC_D):
            while _off + TOK > _ypools[_pi][1]:
                _pi += 1
                _off = 0
            _yslabs.append((_ypools[_pi][0][:, _off:_off + TOK],
                            _ypools[_pi][2]))
            _off += TOK

        def yslab(oc):
            return _yslabs[oc][0]

        def yrole(oc):
            r = _yslabs[oc][1]
            return [r] if r else []
        gbv2 = AI2b[:, 8192:8192 + NW3 * W].rearrange(
            "p (c f) -> p c f", c=NW3)
        zsc2 = AI2b[:, 9216:9216 + 2 * W].rearrange(
            "p (c f) -> p c f", c=2)
        w1s3 = AI2b[:, 10240:12288]
        kb.fence(w=["I_S", "AI3:h"])
        kb.suppress.add("I_S")
        for e in range(8):
            for hh in range(2):
                c0 = hh * 2048
                kb.dma("gpsimd", wsl2[:, 0:16, :],
                       w2_in[e][c0:c0 + 2048, :].rearrange(
                           "(c p) f -> p c f", p=128), xr=["AI:f"],
                       xw=["AI:f"])
                for hc in range(16):
                    hcg = hh * 16 + hc
                    for half in range(2):
                        kb.mm(pq[4 + half][0:1, 0:512],
                              sb1[:, hcg:hcg + 1, e:e + 1].rearrange(
                                  "p a b -> p (a b)"),
                              wsl2[:, hc, half * 512:(half + 1) * 512],
                              start=(hh == 0 and hc == 0),
                              stop=(hh == 1 and hc == 15), xr=["AI:f"])
                for p3 in range(NP3):
                    for th2 in range(NW3):
                        t0 = p3 * PW3 + th2 * W
                        kb.mm(pq[3][:, 0:W], selm[:, e * 128:(e + 1) * 128],
                              m8[:, t0:t0 + W])
                        kb.v("tensor_copy", gbv2[:, th2, :], pq[3][:, 0:W],
                             xw=["AI2:g"])
                    for hc in range(16):
                        hcg = hh * 16 + hc
                        bf = hc % 2
                        kb.dma("gpsimd",
                               w1s3[:, bf * 1024:bf * 1024 + 1024]
                               .rearrange("p (c f) -> p c f", c=NC_D),
                               w1_in[e][:, c0 + hc * 128:c0 + hc * 128 + 128]
                               .rearrange("(c p) f -> p c f", p=128),
                               xw=[f"AI2:c{bf}"])
                        for th2 in range(NW3):
                            t0 = p3 * PW3 + th2 * W
                            idx = hc * NW3 + th2
                            ps = pq[idx % 2]
                            for dc in range(NC_D):
                                kb.mm(ps[:, 0:W],
                                      w1s3[:, bf * 1024 + dc * 128:
                                           bf * 1024 + dc * 128 + 128],
                                      xn[:, dc, t0:t0 + W], start=(dc == 0),
                                      stop=(dc == NC_D - 1),
                                      xr=[f"AI2:c{bf}"])
                            kb.v("scalar_tensor_tensor", zsc2[:, idx % 2, :],
                                 ps[:, 0:W], 1.0, gbv2[:, th2, :],
                                 op0=ALU.mult, op1=ALU.mult,
                                 xr=["AI2:g"], xw=[f"AI2:z{idx % 2}"])
                            kb.act(h2v[:, idx, 0:W], zsc2[:, idx % 2, :],
                                   AF.Silu,
                                   bias=b1t[:, hcg:hcg + 1, e:e + 1]
                                   .rearrange("p a b -> p (a b)"),
                                   xr=[f"AI2:z{idx % 2}"], xw=["AI3:h"])
                    for th2 in range(NW3):
                        t0 = p3 * PW3 + th2 * W
                        for oc in range(NC_D):
                            ps = pq[2 + (oc % 2)]
                            for hc in range(16):
                                kb.mm(ps[:, 0:W],
                                      wsl2[:, hc, oc * 128:(oc + 1) * 128],
                                      h2v[:, hc * NW3 + th2, 0:W],
                                      start=(hc == 0), stop=(hc == 15),
                                      xr=["AI:f", "AI3:h"])
                            if e == 0 and hh == 0:
                                kb.v("tensor_copy", yslab(oc)[:, t0:t0 + W],
                                     ps[:, 0:W], xw=yrole(oc))
                            else:
                                kb.v("tensor_add", yslab(oc)[:, t0:t0 + W],
                                     yslab(oc)[:, t0:t0 + W], ps[:, 0:W],
                                     xr=yrole(oc), xw=yrole(oc))
            kb.v("tensor_scalar", crow1[:, 0:512],
                 pq[4][0:1, 0:512], -1.0, None, op0=ALU.mult)
            kb.v("tensor_scalar", crow1[:, 512:1024],
                 pq[5][0:1, 0:512], -1.0, None, op0=ALU.mult)
            kb.dma("gpsimd", cst[e:e + 1, :], crow1[:])
        for th in range(NTH):
            t0 = th * W
            kb.mm(pq[5][:, 0:W], crbf[:], wrow[:, t0:t0 + W])
            kb.v("tensor_copy", wtb[:, 0:W], pq[5][:, 0:W])
            for oc in range(NC_D):
                ps = pq[4]
                kb.mm(ps[:, 0:W], cst[:, oc * 128:(oc + 1) * 128],
                      m8c[:, t0:t0 + W], start=True, stop=False)
                kb.mm(ps[:, 0:W], b2b[:, oc * 128:(oc + 1) * 128],
                      m8[:, t0:t0 + W], start=False, stop=True)
                kb.v("tensor_add", yslab(oc)[:, t0:t0 + W],
                     yslab(oc)[:, t0:t0 + W], ps[:, 0:W],
                     xr=yrole(oc), xw=yrole(oc))
                kb.v("tensor_mul", tmpw[:, 0:W], yslab(oc)[:, t0:t0 + W],
                     wtb[:, 0:W], xr=yrole(oc))
                kb.act(silb[:, 0:W], tmpw[:, 0:W], AF.Silu)
                kb.dma("sync", s3c[:, oc, 0:W],
                       s3_dram[oc * 128:(oc + 1) * 128, t0:t0 + W])
                kb.v("tensor_mul", silb[:, 0:W], silb[:, 0:W], s3c[:, oc, 0:W])
                kb.dma("sync", xch[:, 0:W],
                       x_in[oc * 128:(oc + 1) * 128, t0:t0 + W])
                kb.v("tensor_add", xch[:, 0:W], xch[:, 0:W], silb[:, 0:W])
                kb.dma("sync", x4_out[oc * 128:(oc + 1) * 128, t0:t0 + W],
                       xch[:, 0:W])
        if standalone:
            kb.finalize()
    return nc


def run_L3(inputs, TOK, runner, x3_sh, xn3_sh):
    CONST = _const_inputs()
    CH = min(512, TOK)
    NTT = TOK // 128
    j = np.arange(TOK)
    gcols = (GAMMA ** ((CH - 1) - (j % CH))).astype(np.float32)
    gcols = gcols.reshape(NTT, 128).T.copy()
    i = np.arange(CH)
    gqrow = (GAMMA ** (i + 1)).astype(np.float32)[None, :].copy()
    maskT = np.where(i[None, :] >= i[:, None],
                     np.float32(GAMMA) ** (i[None, :] - i[:, None]).astype(np.float32),
                     np.float32(0.0)).astype(np.float32)
    Wq = np.ascontiguousarray(np.asarray(inputs["L3_attn_weight_q"]))
    Wk = np.ascontiguousarray(np.asarray(inputs["L3_attn_weight_k"]))
    Wv = np.ascontiguousarray(np.asarray(inputs["L3_attn_weight_v"]))
    maps = []
    for c in range(NCORES):
        maps.append({"xn3": xn3_sh[c], "Wk": Wk, "Wv": Wv, "gcols": gcols})
    resH = runner(build_H(TOK), maps)
    Ts = [np.asarray(r["Tcore"]) for r in resH]
    Wg3 = np.ascontiguousarray(np.asarray(inputs["L3_ffn_Wg"]))
    bg = np.asarray(inputs["L3_ffn_bg"])[None, :].copy()
    W1 = np.ascontiguousarray(np.asarray(inputs["L3_ffn_W1"]))
    b1T = np.ascontiguousarray(np.asarray(inputs["L3_ffn_b1"]).T)
    W2 = np.ascontiguousarray(np.asarray(inputs["L3_ffn_W2"]))
    b2 = np.ascontiguousarray(np.asarray(inputs["L3_ffn_b2"]))
    selmat = np.zeros((8, 1024), np.float32)
    for t in range(8):
        selmat[t, t * 128:(t + 1) * 128] = 1.0
    maps = []
    for c in range(NCORES):
        b, p = c // 4, c % 4
        tsel = np.zeros((3, D, D), np.float32)
        for k in range(3):
            i2 = p - 1 - k
            if i2 >= 0:
                tsel[k] = Ts[b * 4 + i2]
        maps.append({
            "xn3": xn3_sh[c], "x3": x3_sh[c], "Wq": Wq, "Wk": Wk, "Wv": Wv,
            "Tsel": tsel, "gcols": gcols, "gqrow": gqrow, "maskT": maskT,
            "Wg3": Wg3, "bg": bg, "W1": W1, "b1T": b1T, "W2": W2, "b2": b2,
            "selmat": selmat, "ident": np.eye(128, dtype=np.float32),
            "ones_bf_col": CONST["ones_bf_col"],
            "ones_f32_row": CONST["ones_f32_row"],
        })
    resI = runner(build_I(TOK), maps)
    return [np.asarray(r["x4"]) for r in resI]


def _hw_runner(nc, maps):
    res = run_bass_kernel_spmd(nc, maps, core_ids=list(range(NCORES)))
    return res.results


def kernel(**inputs):
    inputs = {k: np.asarray(v) for k, v in inputs.items()}
    x = inputs["L0_attn_representations"]
    S = x.shape[1]
    TOK = S // 4
    runner = _hw_runner
    x1, xn1 = run_L0(inputs, TOK, runner)
    x2, xn2 = run_L1(inputs, TOK, runner, x1, xn1)
    x3, xn3 = run_L2(inputs, TOK, runner, x2, xn2)
    x4 = run_L3(inputs, TOK, runner, x3, xn3)
    out = np.empty((2, S, 1024), np.float32)
    for c in range(NCORES):
        b, p = c // 4, c % 4
        out[b, p * TOK:(p + 1) * TOK, :] = x4[c].T
    return out



_FUSED_CACHE = {}


def _cc(kb, kind, groups, in_ap, out_ap):
    return kb.e("gpsimd",
                lambda E: E.collective_compute(
                    kind, ALU.bypass, replica_groups=groups,
                    ins=[in_ap], outs=[out_ap]),
                r=[in_ap], w=[out_ap], dma=True, cc=True)


def build_fused(TOK):
    if TOK in _FUSED_CACHE:
        return _FUSED_CACHE[TOK]
    TB = 4 * TOK
    NTT = TOK // 128
    nc = bass.Bass("TRN2", target_bir_lowering=False)
    es0 = contextlib.ExitStack()
    with es0:
        kb = KB(nc, es0)
        reg = {}

        def internal(name, shape, dt):
            reg[name] = nc.dram_tensor(name, shape, dt, kind="Internal").ap()
        internal("xn", [D, TOK], BF16)
        internal("xnall", [4, D, TOK], BF16)
        internal("qk", [512, TB], BF16)
        internal("v", [TB, 2048], BF16)
        internal("att", [512 * 4, TB], BF16)
        internal("rs_in", [TB, D], BF16)
        internal("rs_out", [TOK, D], BF16)
        internal("x1", [D, TOK], F32)
        internal("xn1", [D, TOK], BF16)
        internal("zcum", [TOK, 2048], BF16)
        internal("cloc", [1, NTT * 2048], F32)
        internal("hrow", [1, 2048], F32)
        internal("hall", [4, 2048], F32)
        internal("x2", [D, TOK], F32)
        internal("xn2", [D, TOK], BF16)
        internal("x3", [D, TOK], F32)
        internal("xn3", [D, TOK], BF16)
        internal("Tcore", [D, D], F32)
        internal("tall", [4, D, D], F32)
        reg["x4"] = nc.dram_tensor("x4", [D, TOK], F32,
                                   kind="ExternalOutput").ap()
        ctx = FusedCtx(nc, es0, kb, reg)
        G8 = [[0, 1, 2, 3], [4, 5, 6, 7]]
        build_A(TOK, ctx)
        kb.barrier()
        _cc(kb, "AllGather", G8,
            reg["xn"].rearrange("(a d) t -> a (d t)", a=1),
            reg["xnall"].rearrange("r d t -> r (d t)"))
        kb.barrier()
        build_B(TOK, ctx)
        kb.barrier()
        build_C(TOK, ctx)
        kb.barrier()
        kb.e("gpsimd",
             lambda E: E.collective_compute(
                 "ReduceScatter", ALU.add, replica_groups=G8,
                 ins=[reg["rs_in"].rearrange("(a t) d -> a (t d)", a=1)],
                 outs=[reg["rs_out"].rearrange("(a t) d -> a (t d)", a=1)]),
             r=[reg["rs_in"]], w=[reg["rs_out"]], dma=True, cc=True)
        kb.barrier()
        build_D(TOK, ctx)
        kb.barrier()
        build_E(TOK, ctx)
        kb.barrier()
        _cc(kb, "AllGather", G8, reg["hrow"], reg["hall"])
        kb.barrier()
        build_F(TOK, ctx)
        kb.barrier()
        build_G(TOK, ctx)
        kb.barrier()
        build_H(TOK, ctx)
        kb.barrier()
        _cc(kb, "AllGather", G8,
            reg["Tcore"].rearrange("(a d) f -> a (d f)", a=1),
            reg["tall"].rearrange("r d f -> r (d f)"))
        kb.barrier()
        build_I(TOK, ctx)
        kb.finalize()
    _FUSED_CACHE[TOK] = (nc, sorted(_REG_INPUTS.get(id(reg), set())))
    return _FUSED_CACHE[TOK]


def prep_fused(inputs, TOK):
    CONST = _const_inputs()
    S = 4 * TOK
    NTT = TOK // 128
    x = np.asarray(inputs["L0_attn_representations"])
    WcT = np.ascontiguousarray(np.asarray(inputs["L0_attn_Wc"]).T)
    bcv = np.asarray(inputs["L0_attn_bc"])
    WoT = np.ascontiguousarray(np.asarray(inputs["L0_attn_Wo"]).T)
    bo = np.asarray(inputs["L0_attn_bo"]).reshape(NC_D, 128).T.copy()
    WgT = np.ascontiguousarray(np.asarray(inputs["L0_ffn_Wg"]).T)
    W1T = np.ascontiguousarray(np.asarray(inputs["L0_ffn_W1"]).transpose(0, 2, 1))
    W2T = np.ascontiguousarray(np.asarray(inputs["L0_ffn_W2"]).transpose(0, 2, 1))
    W3T = np.ascontiguousarray(np.asarray(inputs["L0_ffn_W3"]).transpose(0, 2, 1))
    COS, SIN = _dft_consts()
    a_params = np.asarray(inputs["L1_attn_a_params"])
    apos = np.zeros(1024, np.float32); apos[1:512] = a_params[:511]
    arev = np.zeros(1024, np.float32); arev[513:] = a_params[:511][::-1]
    apos_h = apos.reshape(NC_D, 128).T.copy()
    arev_h = arev.reshape(NC_D, 128).T.copy()
    B_w = np.ascontiguousarray(np.asarray(inputs["L1_attn_B_w"]))
    C_wT = np.ascontiguousarray(np.asarray(inputs["L1_attn_C_w"]).T)
    Dv = np.asarray(inputs["L1_attn_D"]).reshape(NC_D, 128).T.copy()
    gwT = np.ascontiguousarray(np.asarray(inputs["L1_ffn_gate_w"]).T)
    eb = np.asarray(inputs["L1_ffn_expert_bias"])[None, :].copy()
    w1 = np.ascontiguousarray(np.asarray(inputs["L1_ffn_w1"]))
    w2 = np.ascontiguousarray(np.asarray(inputs["L1_ffn_w2"]))
    w3 = np.ascontiguousarray(np.asarray(inputs["L1_ffn_w3"]))
    WqkvT = np.ascontiguousarray(np.asarray(inputs["L2_attn_Wqkv"]).T)
    WprojT = np.ascontiguousarray(np.asarray(inputs["L2_attn_Wproj"]).T)
    bprojc = np.asarray(inputs["L2_attn_bproj"]).reshape(NC_D, 128).T.copy()
    sigs = np.ascontiguousarray(np.asarray(inputs["L2_ffn_tile_sigs"]))
    WtT = np.ascontiguousarray(np.asarray(inputs["L2_ffn_W"]).transpose(0, 2, 1))
    bt = np.ascontiguousarray(np.asarray(inputs["L2_ffn_b"]))
    pe = _pe_table(S)
    Wq = np.ascontiguousarray(np.asarray(inputs["L3_attn_weight_q"]))
    Wk = np.ascontiguousarray(np.asarray(inputs["L3_attn_weight_k"]))
    Wv = np.ascontiguousarray(np.asarray(inputs["L3_attn_weight_v"]))
    CH = min(512, TOK)
    jj = np.arange(TOK)
    gcols = (GAMMA ** ((CH - 1) - (jj % CH))).astype(np.float32)
    gcols = gcols.reshape(NTT, 128).T.copy()
    ii = np.arange(CH)
    gqrow = (GAMMA ** (ii + 1)).astype(np.float32)[None, :].copy()
    maskT = np.where(ii[None, :] >= ii[:, None],
                     np.float32(GAMMA) ** (ii[None, :] - ii[:, None]).astype(np.float32),
                     np.float32(0.0)).astype(np.float32)
    Wg3 = np.ascontiguousarray(np.asarray(inputs["L3_ffn_Wg"]))
    bg = np.asarray(inputs["L3_ffn_bg"])[None, :].copy()
    W1_3 = np.ascontiguousarray(np.asarray(inputs["L3_ffn_W1"]))
    b1T = np.ascontiguousarray(np.asarray(inputs["L3_ffn_b1"]).T)
    W2_3 = np.ascontiguousarray(np.asarray(inputs["L3_ffn_W2"]))
    b2 = np.ascontiguousarray(np.asarray(inputs["L3_ffn_b2"]))
    selmat = np.zeros((8, 1024), np.float32)
    for t in range(8):
        selmat[t, t * 128:(t + 1) * 128] = 1.0
    ident = np.eye(128, dtype=np.float32)
    ones512 = np.ones((1, 512), np.float32)
    maps = []
    for c in range(NCORES):
        b, p = c // 4, c % 4
        qc, kc, vc = _heads_cols(4 * p)
        selw = np.zeros((4, 4), np.float32)
        for k in range(4):
            i2 = p - 1 - k
            if i2 >= 0:
                selw[i2, k] = 1.0
        cvecb = np.zeros((128, 4), np.float32)
        for i2 in range(p):
            cv = float(np.float64(GAMMA) ** (TOK * (p - 1 - i2)))
            cvecb[:, i2] = np.float32(cv)
        maps.append({
            "x": np.ascontiguousarray(x[b, p * TOK:(p + 1) * TOK, :].T),
            "wqk": np.ascontiguousarray(WcT[:, np.concatenate([qc, kc])]),
            "wv": np.ascontiguousarray(WcT[:, vc]),
            "bqk": bcv[np.concatenate([qc, kc])][None, :].copy(),
            "bv": bcv[vc][None, :].copy(),
            "onesr": ones512, "ident": ident,
            "onesc": np.ones((128, 1), np.float32),
            "woT": WoT, "woTs": np.ascontiguousarray(WoT[vc - 2048, :])
            if False else np.ascontiguousarray(WoT[np.concatenate([vc]) - 2048, :]),
            "bo": bo, "wgT": WgT,
            "w1T": W1T, "w2T": W2T, "w3T": W3T, "selmat": selmat,
            "apos": apos_h, "arev": arev_h, "B_w": B_w,
            "cosF": COS, "sinF": SIN,
            "ut128": np.triu(np.ones((128, 128), np.float32)),
            "selw": selw, "C_wT": C_wT, "Dcol": Dv, "gwT": gwT, "ebias": eb,
            "w1": w1, "w2": w2, "w3": w3,
            "WqkvT": WqkvT, "WprojT": WprojT, "bprojc": bprojc,
            "tile_sigs": sigs,
            "peT": np.ascontiguousarray(pe[p * TOK:(p + 1) * TOK].T),
            "WtT": WtT, "bt": bt,
            "Wq": Wq, "Wk": Wk, "Wv": Wv, "cvecb": cvecb,
            "gcols": gcols, "gqrow": gqrow, "maskT": maskT,
            "Wg3": Wg3, "bg": bg, "W1": W1_3, "b1T": b1T, "W2": W2_3,
            "b2": b2, **CONST,
        })
    return maps


def kernel(**inputs):
    inputs = {k: np.asarray(v) for k, v in inputs.items()}
    x = inputs["L0_attn_representations"]
    S = x.shape[1]
    TOK = S // 4
    nc, in_names = build_fused(TOK)
    maps = prep_fused(inputs, TOK)
    for m in maps:
        missing = set(in_names) - set(m)
        extra = set(m) - set(in_names)
        assert not missing, f"missing inputs {missing}"
        for k in extra:
            del m[k]
    res = _fused_runner(nc, maps)
    out = np.empty((2, S, 1024), np.float32)
    for c in range(NCORES):
        b, p = c // 4, c % 4
        out[b, p * TOK:(p + 1) * TOK, :] = np.asarray(res[c]["x4"]).T
    return out


def _fused_runner(nc, maps):
    return run_bass_kernel_spmd(nc, maps, core_ids=list(range(NCORES))).results



_GROUP_CACHE = {}

_GROUPS = {
    "g2": (("B", "C"), {"qk": ([512, None], "BF16"), "v": ([None, 2048], "BF16")}),
}


def build_group(TOK, key, parts, internals):
    ck = (TOK, key)
    if ck in _GROUP_CACHE:
        return _GROUP_CACHE[ck]
    nc = bass.Bass("TRN2", target_bir_lowering=False)
    es0 = contextlib.ExitStack()
    with es0:
        kb = KB(nc, es0)
        reg = {}
        for name, (shape, dt) in internals.items():
            reg[name] = nc.dram_tensor(name, shape, dt, kind="Internal").ap()
        ctx = FusedCtx(nc, es0, kb, reg)
        first = True
        for fn in parts:
            if not first:
                kb.barrier()
            fn(TOK, ctx)
            first = False
        kb.finalize()
    _GROUP_CACHE[ck] = (nc, sorted(_REG_INPUTS.get(id(reg), set())))
    return _GROUP_CACHE[ck]


def kernel(**inputs):
    inputs = {k: np.asarray(v) for k, v in inputs.items()}
    x = inputs["L0_attn_representations"]
    S = x.shape[1]
    TOK = S // 4
    TB = 4 * TOK
    NTT = TOK // 128
    runner = _fused_runner

    def run_group(key, parts, internals, maps):
        nc, in_names = build_group(TOK, key, tuple(parts), internals)
        for m in maps:
            missing = set(in_names) - set(m)
            assert not missing, f"group {key} missing {missing}"
            for k in list(m):
                if k not in in_names:
                    del m[k]
        return runner(nc, maps)

    CONST = _const_inputs()
    ident = np.eye(128, dtype=np.float32)
    ones512 = np.ones((1, 512), np.float32)
    maps = []
    for c in range(NCORES):
        b, p = c // 4, c % 4
        maps.append({"x": np.ascontiguousarray(x[b, p * TOK:(p + 1) * TOK, :].T),
                     **CONST})
    r1 = run_group("g1", [build_A], {}, maps)
    xn_sh = [np.asarray(r["xn"]) for r in r1]
    WcT = np.ascontiguousarray(np.asarray(inputs["L0_attn_Wc"]).T)
    bcv = np.asarray(inputs["L0_attn_bc"])
    maps = []
    for c in range(NCORES):
        b, p = c // 4, c % 4
        qc, kc, vc = _heads_cols(4 * p)
        xnf = np.concatenate([xn_sh[b * 4 + j] for j in range(4)], axis=1)
        maps.append({
            "xnf": np.ascontiguousarray(xnf),
            "wqk": np.ascontiguousarray(WcT[:, np.concatenate([qc, kc])]),
            "wv": np.ascontiguousarray(WcT[:, vc]),
            "bqk": bcv[np.concatenate([qc, kc])][None, :].copy(),
            "bv": bcv[vc][None, :].copy(),
            "onesr": ones512, "ident": ident,
            "onesc": np.ones((128, 1), np.float32),
        })
    r2 = run_group("g2", [build_B, build_C],
                   {"qk": ([512, TB], BF16), "v": ([TB, 2048], BF16)}, maps)
    att = [np.asarray(r["att"]) for r in r2]
    WoT = np.ascontiguousarray(np.asarray(inputs["L0_attn_Wo"]).T)
    bo = np.asarray(inputs["L0_attn_bo"]).reshape(NC_D, 128).T.copy()
    WgT = np.ascontiguousarray(np.asarray(inputs["L0_ffn_Wg"]).T)
    W1T = np.ascontiguousarray(np.asarray(inputs["L0_ffn_W1"]).transpose(0, 2, 1))
    W2T = np.ascontiguousarray(np.asarray(inputs["L0_ffn_W2"]).transpose(0, 2, 1))
    W3T = np.ascontiguousarray(np.asarray(inputs["L0_ffn_W3"]).transpose(0, 2, 1))
    COS, SIN = _dft_consts()
    a_params = np.asarray(inputs["L1_attn_a_params"])
    apos = np.zeros(1024, np.float32); apos[1:512] = a_params[:511]
    arev = np.zeros(1024, np.float32); arev[513:] = a_params[:511][::-1]
    apos_h = apos.reshape(NC_D, 128).T.copy()
    arev_h = arev.reshape(NC_D, 128).T.copy()
    selmat = np.zeros((8, 1024), np.float32)
    for t in range(8):
        selmat[t, t * 128:(t + 1) * 128] = 1.0
    maps = []
    for c in range(NCORES):
        b, p = c // 4, c % 4
        rng = slice(p * TOK, (p + 1) * TOK)
        o_all = np.concatenate(
            [att[b * 4 + j][:, rng] for j in range(4)], axis=0)
        maps.append({
            "o_all": np.ascontiguousarray(o_all),
            "woT": WoT, "bo": bo,
            "x": np.ascontiguousarray(x[b, rng, :].T),
            "xn": xn_sh[c],
            "wgT": WgT, "w1T": W1T, "w2T": W2T, "w3T": W3T,
            "ident": ident, "selmat": selmat,
            "apos": apos_h, "arev": arev_h,
            "B_w": np.ascontiguousarray(np.asarray(inputs["L1_attn_B_w"])),
            "cosF": COS, "sinF": SIN,
            "ut128": np.triu(np.ones((128, 128), np.float32)),
            **CONST,
        })
    r3 = run_group("g3", [build_D, build_E], {}, maps)
    x1 = [np.asarray(r["x1"]) for r in r3]
    xn1 = [np.asarray(r["xn1"]) for r in r3]
    Hs = [np.asarray(r["hrow"]).reshape(2048) for r in r3]
    C_wT = np.ascontiguousarray(np.asarray(inputs["L1_attn_C_w"]).T)
    Dv = np.asarray(inputs["L1_attn_D"]).reshape(NC_D, 128).T.copy()
    gwT = np.ascontiguousarray(np.asarray(inputs["L1_ffn_gate_w"]).T)
    eb = np.asarray(inputs["L1_ffn_expert_bias"])[None, :].copy()
    pe = _pe_table(S)
    CH = min(512, TOK)
    jj = np.arange(TOK)
    gcols = (GAMMA ** ((CH - 1) - (jj % CH))).astype(np.float32)
    gcols = gcols.reshape(NTT, 128).T.copy()
    maps = []
    for c in range(NCORES):
        b, p = c // 4, c % 4
        hsel = np.zeros((1, 8192), np.float32)
        for k in range(4):
            i2 = p - 1 - k
            if i2 >= 0:
                hsel[0, k * 2048:(k + 1) * 2048] = Hs[b * 4 + i2]
        maps.append({
            "zcum": np.asarray(r3[c]["zcum"]),
            "cloc": np.asarray(r3[c]["cloc"]),
            "Hsel": hsel, "apos": apos_h, "arev": arev_h,
            "cosF": COS, "sinF": SIN, "C_wT": C_wT, "Dcol": Dv,
            "xn1": xn1[c], "x1": x1[c], "gwT": gwT, "ebias": eb,
            "w1": np.ascontiguousarray(np.asarray(inputs["L1_ffn_w1"])),
            "w2": np.ascontiguousarray(np.asarray(inputs["L1_ffn_w2"])),
            "w3": np.ascontiguousarray(np.asarray(inputs["L1_ffn_w3"])),
            "ident": ident, "selmat": selmat,
            "WqkvT": np.ascontiguousarray(np.asarray(inputs["L2_attn_Wqkv"]).T),
            "WprojT": np.ascontiguousarray(np.asarray(inputs["L2_attn_Wproj"]).T),
            "bprojc": np.asarray(inputs["L2_attn_bproj"]).reshape(NC_D, 128).T.copy(),
            "tile_sigs": np.ascontiguousarray(np.asarray(inputs["L2_ffn_tile_sigs"])),
            "peT": np.ascontiguousarray(pe[p * TOK:(p + 1) * TOK].T),
            "WtT": np.ascontiguousarray(np.asarray(inputs["L2_ffn_W"]).transpose(0, 2, 1)),
            "bt": np.ascontiguousarray(np.asarray(inputs["L2_ffn_b"])),
            "Wk": np.ascontiguousarray(np.asarray(inputs["L3_attn_weight_k"])),
            "Wv": np.ascontiguousarray(np.asarray(inputs["L3_attn_weight_v"])),
            "gcols": gcols,
            **CONST,
        })
    r4 = run_group("g4", [build_F, build_G, build_H],
                   {"x2": ([D, TOK], F32), "xn2": ([D, TOK], BF16)}, maps)
    x3 = [np.asarray(r["x3"]) for r in r4]
    xn3 = [np.asarray(r["xn3"]) for r in r4]
    Ts = [np.asarray(r["Tcore"]) for r in r4]
    ii = np.arange(CH)
    gqrow = (GAMMA ** (ii + 1)).astype(np.float32)[None, :].copy()
    maskT = np.where(ii[None, :] >= ii[:, None],
                     np.float32(GAMMA) ** (ii[None, :] - ii[:, None]).astype(np.float32),
                     np.float32(0.0)).astype(np.float32)
    maps = []
    for c in range(NCORES):
        b, p = c // 4, c % 4
        tsel = np.zeros((3, D, D), np.float32)
        for k in range(3):
            i2 = p - 1 - k
            if i2 >= 0:
                tsel[k] = Ts[b * 4 + i2]
        maps.append({
            "xn3": xn3[c], "x3": x3[c],
            "Wq": np.ascontiguousarray(np.asarray(inputs["L3_attn_weight_q"])),
            "Wk": np.ascontiguousarray(np.asarray(inputs["L3_attn_weight_k"])),
            "Wv": np.ascontiguousarray(np.asarray(inputs["L3_attn_weight_v"])),
            "Tsel": tsel, "gcols": gcols, "gqrow": gqrow, "maskT": maskT,
            "Wg3": np.ascontiguousarray(np.asarray(inputs["L3_ffn_Wg"])),
            "bg": np.asarray(inputs["L3_ffn_bg"])[None, :].copy(),
            "W1": np.ascontiguousarray(np.asarray(inputs["L3_ffn_W1"])),
            "b1T": np.ascontiguousarray(np.asarray(inputs["L3_ffn_b1"]).T),
            "W2": np.ascontiguousarray(np.asarray(inputs["L3_ffn_W2"])),
            "b2": np.ascontiguousarray(np.asarray(inputs["L3_ffn_b2"])),
            "selmat": selmat, "ident": ident,
            **CONST,
        })
    r5 = run_group("g5", [build_I], {}, maps)
    out = np.empty((2, S, 1024), np.float32)
    for c in range(NCORES):
        b, p = c // 4, c % 4
        out[b, p * TOK:(p + 1) * TOK, :] = np.asarray(r5[c]["x4"]).T
    return out


# version 11 (final)
# speedup vs baseline: 1.3373x; 1.0179x over previous
import contextlib
import functools
import numpy as np
import ml_dtypes

import concourse.bass as bass
import concourse.mybir as mybir
from concourse.bass_utils import run_bass_kernel_spmd

F32 = mybir.dt.float32
BF16 = mybir.dt.bfloat16
AF = mybir.ActivationFunctionType
ALU = mybir.AluOpType
AX = mybir.AxisListType
BF = ml_dtypes.bfloat16

NCORES = 8
D = 1024
NC_D = 8
GAMMA = 0.96875


ENGS = ("sync", "gpsimd", "vector", "scalar", "tensor")


def _is_psum(ap):
    t = getattr(ap, "tensor", None)
    return t is not None and type(t).__name__ == "PSumTensorHandle"


class _Op:
    __slots__ = ("eng", "dma", "fn", "r", "w", "idx", "deps", "store", "cc")

    def __init__(self, eng, dma, fn, r, w, idx, deps, store, cc=False):
        self.eng, self.dma, self.fn = eng, dma, fn
        self.r, self.w, self.idx, self.deps, self.store = r, w, idx, deps, store
        self.cc = cc


def _nm(ap):
    if isinstance(ap, str):
        return ap
    n = getattr(ap, "name", None)
    if callable(n):
        n = n()
    return n


class KB:

    def __init__(self, nc, es):
        self.nc = nc
        self.es = es
        self.ops = []
        self._fdum = es.enter_context(nc.sbuf_tensor("fdum", [1, 8], F32))
        self._floor = None
        self._barn = 0
        self.last_eng_op = {}
        self.suppress = set()
        self.last_w = {}
        self.readers = {}
        self.sems = {}
        for key in (("sync", True), ("gpsimd", True), ("gpsimd", False),
                    ("vector", False), ("scalar", False), ("tensor", False),
                    ("scalar", True), ("gpsimd", "cc")):
            self.sems[key] = es.enter_context(
                nc.semaphore(f"ch_{key[0]}_{key[1]}"))

    def e(self, eng, fn, r=(), w=(), dma=False, store=False, cc=False):
        idx = len(self.ops)
        deps = set()
        sup = self.suppress
        rn = [x for x in (_nm(a) for a in r) if x not in sup]
        wn = [x for x in (_nm(a) for a in w) if x not in sup]
        for a in r:
            if not isinstance(a, str) and _is_psum(a):
                n = _nm(a)
                if n not in wn:
                    wn = wn + [n]
        for n in rn:
            lw = self.last_w.get(n)
            if lw is not None:
                deps.add(lw)
        for n in wn:
            lw = self.last_w.get(n)
            if lw is not None:
                deps.add(lw)
            for t in self.readers.get(n, {}).values():
                deps.add(t)
        if self._floor is not None and idx != self._floor:
            deps.add(self._floor)
        op = _Op(eng, dma, fn, rn, wn, idx, deps, store, cc)
        self.ops.append(op)
        self.last_eng_op[eng] = idx
        for n in rn:
            self.readers.setdefault(n, {})[eng] = idx
        for n in wn:
            self.last_w[n] = idx
            self.readers[n] = {}
        return idx

    def fence(self, r=(), w=()):
        return self.e("vector", lambda E: E.memset(self._fdum[:], 0.0),
                      r=list(r) + ["fdum"], w=list(w) + ["fdum"])

    def barrier(self):
        deps = set(self.last_eng_op.values())
        idx = self.e("vector", lambda E: E.memset(self._fdum[:], 0.0),
                     r=["fdum"], w=["fdum"])
        self.ops[idx].deps |= deps
        self._floor = idx

    def dma(self, queue, out, in_, xr=(), xw=(), **kw):
        return self.e(queue, lambda E: E.dma_start(out=out, in_=in_, **kw),
                      r=[in_] + list(xr), w=[out] + list(xw), dma=True,
                      store=not hasattr(out.tensor, "base_partition"))

    def mm(self, out, lhsT, rhs, start=True, stop=True, xr=(), xw=()):
        return self.e("tensor",
                      lambda E: E.matmul(out, lhsT, rhs, start=start, stop=stop),
                      r=[lhsT, rhs] + list(xr), w=[out] + list(xw))

    def trans(self, out, in_, ident, xr=(), xw=()):
        return self.e("tensor", lambda E: E.transpose(out, in_, ident),
                      r=[in_, ident] + list(xr), w=[out] + list(xw))

    def act(self, out, in_, func, bias=0.0, scale=1.0, accum_out=None,
            xr=(), xw=()):
        ins = [in_]
        outs = [out]
        if isinstance(bias, bass.AP) or hasattr(bias, "tensor"):
            ins = ins + [bias]
        if isinstance(scale, bass.AP) or hasattr(scale, "tensor"):
            ins = ins + [scale]
        if accum_out is not None:
            outs = outs + [accum_out]
        return self.e("scalar",
                      lambda E: E.activation(out=out, in_=in_, func=func,
                                             bias=bias, scale=scale,
                                             accum_out=accum_out),
                      r=ins + list(xr), w=outs + list(xw))

    def v(self, meth, out, *ins, xr=(), xw=(), **kw):
        def fn(E):
            return getattr(E, meth)(out, *ins, **kw)
        rd = [a for a in ins if hasattr(a, "tensor")] + list(xr)
        rd += [a for a in kw.values() if hasattr(a, "tensor")]
        wr = [out] + list(xw)
        if hasattr(kw.get("accum_out"), "tensor"):
            wr.append(kw["accum_out"])
        return self.e("vector", fn, r=rd, w=wr)

    def g(self, meth, out, *ins, xr=(), xw=(), **kw):
        def fn(E):
            return getattr(E, meth)(out, *ins, **kw)
        rd = [a for a in ins if hasattr(a, "tensor")] + list(xr)
        return self.e("gpsimd", fn, r=rd, w=[out] + list(xw))

    def finalize(self):
        ops = self.ops
        for e in ("sync", "gpsimd", "scalar"):
            key = (e, "f")
            self.sems[key] = self.es.enter_context(nc_sem(self.nc, f"fn_{e}"))
        dep_marked = set()
        for c in ops:
            for p in c.deps:
                if ops[p].eng == "tensor" and c.eng == "tensor":
                    continue
                dep_marked.add(p)
        val = {}
        fence = {}
        cnt = {}
        dma_cum = {}
        for o in ops:
            if o.dma:
                kd = (o.eng, "cc") if o.cc else (o.eng, True)
                dma_cum[kd] = dma_cum.get(kd, 0) + 16
                cnt[kd] = dma_cum[kd]
                if o.idx in dep_marked:
                    kf = (o.eng, "f")
                    cnt[kf] = cnt.get(kf, 0) + 1
                    val[o.idx] = (kf, cnt[kf])
                    fence[o.idx] = (kd, dma_cum[kd], cnt[kf])
            elif o.idx in dep_marked:
                k = (o.eng, False)
                cnt[k] = cnt.get(k, 0) + 1
                val[o.idx] = (k, cnt[k])
        clocks = {e: {} for e in ENGS}
        snaps = {}
        waits = {}
        for c in ops:
            cl = clocks[c.eng]
            wl = []
            for p in sorted(c.deps):
                if ops[p].eng == "tensor" and c.eng == "tensor":
                    continue
                k, v = val[p]
                if cl.get(k, 0) >= v:
                    continue
                wl.append((k, v))
                cl[k] = v
                for k2, v2 in snaps[p].items():
                    if cl.get(k2, 0) < v2:
                        cl[k2] = v2
            waits[c.idx] = wl
            if c.idx in val:
                k, v = val[c.idx]
                sn = dict(cl)
                sn[k] = max(sn.get(k, 0), v)
                if c.dma:
                    kdc_ = fence[c.idx][0]
                    sn[kdc_] = max(sn.get(kdc_, 0), fence[c.idx][1])
                snaps[c.idx] = sn
        streams = {e: [] for e in ENGS}
        for o in ops:
            streams[o.eng].append(o)
        sems = self.sems
        final_cnt = dict(dma_cum)

        with self.nc.Block() as blk:
            def make_body(E):
                def body(engobj):
                    for o in streams[E]:
                        for (k, v) in waits[o.idx]:
                            engobj.wait_ge(sems[k], v)
                        inst = o.fn(engobj)
                        if o.dma:
                            kd = (E, "cc") if o.cc else (E, True)
                            inst.then_inc(sems[kd], 16)
                            if o.idx in fence:
                                kd2, dcum, fc = fence[o.idx]
                                engobj.wait_ge(sems[kd2], dcum)
                                engobj.sem_inc(sems[(E, "f")], 1)
                        elif o.idx in val:
                            inst.then_inc(sems[(E, False)], 1)
                    for kd3, v3 in final_cnt.items():
                        if kd3[0] == E:
                            engobj.wait_ge(sems[kd3], v3)
                return body
            for E in ENGS:
                if streams[E]:
                    getattr(blk, E)(make_body(E))


def nc_sem(nc, name):
    return nc.semaphore(name)




class FusedCtx:
    def __init__(self, nc, es, kb, reg):
        self.nc, self.es, self.kb, self.reg = nc, es, kb, reg


def _ctx_or_new(ctx):
    if ctx is not None:
        return ctx.nc, ctx.es, ctx.kb, ctx.reg, False
    nc = bass.Bass("TRN2", target_bir_lowering=False)
    return nc, None, None, None, True


_REG_INPUTS = {}


def _dio(nc, reg, name, shape, dtype, kind):
    if reg is not None:
        if name in reg:
            return reg[name]
        ap = nc.dram_tensor(name, shape, dtype, kind=kind).ap()
        reg[name] = ap
        if kind == "ExternalInput":
            _REG_INPUTS.setdefault(id(reg), set()).add(name)
        return ap
    return nc.dram_tensor(name, shape, dtype, kind=kind).ap()



def fmT(sb, dc=None, t0=None, t1=None):
    a = sb[:]
    if dc is not None:
        a = a[:, dc]
        if t0 is not None:
            a = a[:, t0:t1]
    return a


def emit_ln(kb, C, x3, xn3, TOK, pref, xbc, sqc, bc, tmpc, psA, psB,
            xrole=(), bcrole=(), rows_ap=None, rrole=()):
    nc, es = kb.nc, kb.es
    W = min(512, TOK)
    NT = TOK // W
    T = TOK
    rows = (rows_ap if rows_ap is not None else
            es.enter_context(nc.sbuf_tensor(pref + "rows", [1, 3 * TOK], F32))[:])
    A = rows[:, 0:T]
    B = rows[:, T:2 * T]
    Cm = rows[:, 2 * T:3 * T]
    for tc in range(NT):
        t0, t1 = tc * W, (tc + 1) * W
        for dc in range(NC_D):
            xb = xbc[:, (dc % 2) * W:(dc % 2) * W + W]
            sq = sqc[:, (dc % 2) * W:(dc % 2) * W + W]
            kb.v("tensor_copy", xb, x3[:, dc, t0:t1], xr=list(xrole))
            kb.act(sq, x3[:, dc, t0:t1], AF.Square, xr=list(xrole))
            kb.mm(psA[0:1, 0:W], C["ones_bf_col"], xb,
                  start=(dc == 0), stop=(dc == NC_D - 1))
            kb.mm(psB[0:1, 0:W], C["ones_bf_col"], sq,
                  start=(dc == 0), stop=(dc == NC_D - 1))
        kb.v("tensor_copy", A[:, t0:t1], psA[0:1, 0:W], xw=list(rrole))
        kb.v("tensor_copy", B[:, t0:t1], psB[0:1, 0:W], xw=list(rrole))
    RR = list(rrole)
    kb.v("tensor_scalar", A, A, 1.0 / D, None, op0=ALU.mult, xr=RR, xw=RR)
    kb.v("tensor_mul", Cm, A, A, xr=RR, xw=RR)
    kb.v("scalar_tensor_tensor", B, B, 1.0 / D, Cm,
         op0=ALU.mult, op1=ALU.subtract, xr=RR, xw=RR)
    kb.act(B, B, AF.Ln, bias=C["eps"], xr=RR, xw=RR)
    kb.act(B, B, AF.Exp, scale=-0.5, xr=RR, xw=RR)
    kb.v("tensor_mul", Cm, A, B, xr=RR, xw=RR)
    for tc in range(NT):
        t0, t1 = tc * W, (tc + 1) * W
        kb.mm(psA[:, 0:W], C["ones_f32_row"], B[:, t0:t1], xr=RR)
        kb.v("tensor_copy", bc[:, t0:t1], psA[:, 0:W], xw=list(bcrole))
        kb.mm(psB[:, 0:W], C["ones_f32_row"], Cm[:, t0:t1], xr=RR)
        kb.v("tensor_copy", bc[:, T + t0:T + t1], psB[:, 0:W],
             xw=list(bcrole))
    for dc in range(NC_D):
        for tc in range(NT):
            t0, t1 = tc * W, (tc + 1) * W
            kb.v("tensor_mul", tmpc[:, 0:W], x3[:, dc, t0:t1], bc[:, t0:t1],
                 xr=list(xrole) + list(bcrole))
            kb.v("tensor_sub", xn3[:, dc, t0:t1], tmpc[:, 0:W],
                 bc[:, T + t0:T + t1], xr=list(bcrole))



def build_A(TOK, ctx=None):
    nc, es0, kb0, reg, standalone = _ctx_or_new(ctx)
    es = contextlib.ExitStack()
    x_in = _dio(nc, reg, "x", [D, TOK], F32, "ExternalInput")
    ones_bf_col = _dio(nc, reg, "ones_bf_col", [128, 1], F32, "ExternalInput")
    ones_f32_row = _dio(nc, reg, "ones_f32_row", [1, 128], F32, "ExternalInput")
    eps_in = _dio(nc, reg, "eps", [1, 1], F32, "ExternalInput")
    xn_out = _dio(nc, reg, "xn", [D, TOK], BF16, "ExternalOutput")
    with es:
        if standalone:
            kb = KB(nc, es)
        else:
            kb = kb0
            kb.es = es
        x_sb = es.enter_context(nc.sbuf_tensor("A_x_sb", [128, NC_D, TOK], F32))
        xn_sb = es.enter_context(nc.sbuf_tensor("A_xn_sb", [128, NC_D, TOK], BF16))
        cbf = es.enter_context(nc.sbuf_tensor("A_cbf", [128, 1], BF16))
        crow = es.enter_context(nc.sbuf_tensor("A_crow", [1, 128], F32))
        ceps = es.enter_context(nc.sbuf_tensor("A_ceps", [1, 1], F32))
        C = {"ones_bf_col": cbf[:], "ones_f32_row": crow[:], "eps": ceps[:]}
        xbc = es.enter_context(nc.sbuf_tensor("A_xbc", [128, 1024], BF16))
        sqc = es.enter_context(nc.sbuf_tensor("A_sqc", [128, 1024], BF16))
        bcb = es.enter_context(nc.sbuf_tensor("A_bcb", [128, 2 * TOK], F32))
        tmpc = es.enter_context(nc.sbuf_tensor("A_tmpc", [128, TOK], F32))
        psA = es.enter_context(nc.psum_tensor("A_psA", [128, 512], F32))
        psB = es.enter_context(nc.psum_tensor("A_psB", [128, 512], F32))
        kb.dma("sync", x_sb[:], x_in.rearrange("(c p) t -> p c t", p=128))
        kb.dma("gpsimd", cbf[:], ones_bf_col)
        kb.dma("sync", crow[:], ones_f32_row)
        kb.dma("sync", ceps[:], eps_in)
        emit_ln(kb, C, x_sb[:], xn_sb[:], TOK, "ln0", xbc[:], sqc[:], bcb[:], tmpc[:], psA[:], psB[:])
        kb.dma("sync", xn_out.rearrange("(c p) t -> p c t", p=128), xn_sb[:])
        if standalone:
            kb.finalize()
    return nc



def _const_inputs():
    return {
        "ones_bf_col": np.ones((128, 1), np.float32),
        "ones_f32_row": np.ones((1, 128), np.float32),
        "eps": np.full((1, 1), 1e-5, np.float32),
    }






def build_B(TOK, ctx=None):
    TB = 4 * TOK
    W = 512
    nc, es0, kb0, reg, standalone = _ctx_or_new(ctx)
    xn_in = None if (reg is not None and "xnall" in reg) else \
        _dio(nc, reg, "xnf", [D, TB], BF16, "ExternalInput")
    wqk_in = _dio(nc, reg, "wqk", [D, 512], F32, "ExternalInput")
    wv_in = _dio(nc, reg, "wv", [D, 2048], F32, "ExternalInput")
    bqk_in = _dio(nc, reg, "bqk", [1, 512], F32, "ExternalInput")
    bv_in = _dio(nc, reg, "bv", [1, 2048], F32, "ExternalInput")
    ones_in = _dio(nc, reg, "onesr", [1, 512], F32, "ExternalInput")
    qk_out = _dio(nc, reg, "qk", [512, TB], BF16, "ExternalOutput")
    v_out = _dio(nc, reg, "v", [TB, 2048], BF16, "ExternalOutput")
    es = contextlib.ExitStack()
    with es:
        if standalone:
            kb = KB(nc, es)
        else:
            kb = kb0
            kb.es = es
        xnf = es.enter_context(nc.sbuf_tensor("B_xnf_sb", [128, NC_D, TB], BF16))
        wqk = es.enter_context(nc.sbuf_tensor("B_wqk_sb", [128, NC_D, 512], BF16))
        wv = es.enter_context(nc.sbuf_tensor("B_wv_sb", [128, NC_D, 2048], BF16))
        bqk = es.enter_context(nc.sbuf_tensor("B_bqk_sb", [1, 512], BF16))
        bv = es.enter_context(nc.sbuf_tensor("B_bv_sb", [1, 2048], BF16))
        onesr = es.enter_context(nc.sbuf_tensor("B_onesr_sb", [1, 512], BF16))
        stq = es.enter_context(nc.sbuf_tensor("B_stq", [128, 512], BF16))
        stgf = es.enter_context(nc.sbuf_tensor("B_stgf", [128, 512], F32))
        stv = es.enter_context(nc.sbuf_tensor("B_stv", [128, 2048], BF16))
        psq = [es.enter_context(nc.psum_tensor(f"B_psq{i}", [128, 512], F32))
               for i in range(2)]
        psv = [es.enter_context(nc.psum_tensor(f"B_psv{i}", [128, 512], F32))
               for i in range(4)]
        if xn_in is None:
            for r in range(4):
                kb.dma("sync", xnf[:, :, r * TOK:(r + 1) * TOK],
                       reg["xnall"][r].rearrange("(c p) t -> p c t", p=128))
        else:
            kb.dma("sync", xnf[:], xn_in.rearrange("(c p) t -> p c t", p=128))
        kb.dma("gpsimd", wqk[:], wqk_in.rearrange("(c p) f -> p c f", p=128))
        kb.dma("gpsimd", wv[:], wv_in.rearrange("(c p) f -> p c f", p=128))
        kb.dma("gpsimd", bqk[:], bqk_in)
        kb.dma("gpsimd", bv[:], bv_in)
        kb.dma("gpsimd", onesr[:], ones_in)
        for cc in range(4):
            for tc in range(TB // W):
                t0, t1 = tc * W, (tc + 1) * W
                ps = psq[tc % 2]
                for dc in range(NC_D):
                    kb.mm(ps[:], wqk[:, dc, cc * 128:(cc + 1) * 128],
                          xnf[:, dc, t0:t1], start=(dc == 0), stop=False)
                kb.mm(ps[:], bqk[:, cc * 128:(cc + 1) * 128], onesr[:],
                      start=False, stop=True)
                kb.act(stgf[:], ps[:], AF.Copy)
                kb.v("scalar_tensor_tensor", stq[:], stgf[:], 0.01, stgf[:],
                     op0=ALU.mult, op1=ALU.max)
                kb.dma("sync", qk_out[cc * 128:(cc + 1) * 128, t0:t1], stq[:])
        for tt in range(TB // 128):
            for vc in range(4):
                f0, f1 = vc * W, (vc + 1) * W
                ps = psv[vc]
                for dc in range(NC_D):
                    kb.mm(ps[:], xnf[:, dc, tt * 128:(tt + 1) * 128],
                          wv[:, dc, f0:f1], start=(dc == 0), stop=False)
                kb.mm(ps[:], onesr[:, 0:128], bv[:, f0:f1],
                      start=False, stop=True)
                kb.act(stgf[:], ps[:], AF.Copy)
                kb.v("scalar_tensor_tensor", stv[:, f0:f1], stgf[:], 0.01,
                     stgf[:], op0=ALU.mult, op1=ALU.max)
            kb.dma("sync", v_out[tt * 128:(tt + 1) * 128, :], stv[:])
        if standalone:
            kb.finalize()
    return nc



def build_C(TOK, ctx=None):
    TB = 4 * TOK
    KT = TB // 128
    QW = min(512, TOK)
    NQ = TB // QW
    nc, es0, kb0, reg, standalone = _ctx_or_new(ctx)
    qk_in = _dio(nc, reg, "qk", [512, TB], BF16, "ExternalInput")
    v_in = _dio(nc, reg, "v", [TB, 2048], BF16, "ExternalInput")
    id_in = _dio(nc, reg, "ident", [128, 128], F32, "ExternalInput")
    ones_in = _dio(nc, reg, "onesr", [1, 512], F32, "ExternalInput")
    onesc_in = _dio(nc, reg, "onesc", [128, 1], F32, "ExternalInput")
    out = _dio(nc, reg, "att", [2048, TB], BF16, "ExternalOutput")
    fused_rs = reg is not None and "rs_in" in reg
    if fused_rs:
        wos_in = _dio(nc, reg, "woTs", [2048, D], F32, "ExternalInput")
    es = contextlib.ExitStack()
    with es:
        if standalone:
            kb = KB(nc, es)
        else:
            kb = kb0
            kb.es = es
        q_sb = es.enter_context(nc.sbuf_tensor("C_q_sb", [64, TB], BF16))
        k_sb = es.enter_context(nc.sbuf_tensor("C_k_sb", [64, TB], BF16))
        v_sb = es.enter_context(nc.sbuf_tensor("C_v_sb", [128, KT, 512], BF16))
        S_sb = es.enter_context(nc.sbuf_tensor("C_S_sb", [128, KT, QW], BF16))
        ident = es.enter_context(nc.sbuf_tensor("C_ident_sb", [128, 128], BF16))
        onesr = es.enter_context(nc.sbuf_tensor("C_onesr_sb", [1, 512], BF16))
        onesc = es.enter_context(nc.sbuf_tensor("C_onesc_sb", [128, 1], BF16))
        onesrf = es.enter_context(nc.sbuf_tensor("C_onesrf", [1, 512], F32))
        macc = es.enter_context(nc.sbuf_tensor("C_macc", [128, QW], F32))
        macb = es.enter_context(nc.sbuf_tensor("C_macb", [128, QW], BF16))
        mq = es.enter_context(nc.sbuf_tensor("C_mq", [128, QW // 128], BF16))
        mrow = es.enter_context(nc.sbuf_tensor("C_mrow", [1, QW], BF16))
        mb_sb = es.enter_context(nc.sbuf_tensor("C_mb_sb", [128, QW], BF16))
        drow = es.enter_context(nc.sbuf_tensor("C_drow", [1, QW], F32))
        rrow = es.enter_context(nc.sbuf_tensor("C_rrow", [1, QW], F32))
        rb_sb = es.enter_context(nc.sbuf_tensor("C_rb_sb", [128, QW], F32))
        ostg = es.enter_context(nc.sbuf_tensor("C_ostg", [128, 4, QW], BF16))
        ps_s = [es.enter_context(nc.psum_tensor(f"C_ps_s{i}", [128, QW], F32))
                for i in range(2)]
        ps_av = [es.enter_context(nc.psum_tensor(f"C_ps_av{i}", [128, QW], F32))
                 for i in range(4)]
        ps_d = es.enter_context(nc.psum_tensor("C_ps_d", [1, QW], F32))
        ps_t = es.enter_context(nc.psum_tensor("C_ps_t", [128, 128], BF16))
        kb.dma("gpsimd", ident[:], id_in)
        kb.dma("gpsimd", onesr[:], ones_in)
        kb.dma("gpsimd", onesc[:], onesc_in)
        kb.dma("sync", onesrf[:], ones_in)
        kb.suppress.add("C_S_sb")
        GSZ = max(1, KT // 8)
        for h in range(4):
            kb.dma("sync", q_sb[:], qk_in[h * 64:(h + 1) * 64, :])
            kb.dma("sync", k_sb[:], qk_in[256 + h * 64:256 + (h + 1) * 64, :])
            kb.dma("sync", v_sb[:],
                   v_in[:, h * 512:(h + 1) * 512].rearrange(
                       "(n p) c -> p n c", p=128))
            for i in range(NQ + 1):
                lo = i < NQ
                hi = i > 0
                q0 = i * QW
                for kt in range(KT):
                    if hi:
                        for dv in range(4):
                            kb.mm(ps_av[dv][:],
                                  v_sb[:, kt, dv * 128:(dv + 1) * 128],
                                  S_sb[:, kt, :],
                                  start=(kt == 0), stop=(kt == KT - 1),
                                  xr=[f"CS{kt}"])
                        kb.mm(ps_d[:], onesc[:], S_sb[:, kt, :],
                              start=(kt == 0), stop=(kt == KT - 1),
                              xr=[f"CS{kt}"])
                    if lo:
                        ps = ps_s[kt % 2]
                        kb.mm(ps[:], k_sb[:, kt * 128:(kt + 1) * 128],
                              q_sb[:, q0:q0 + QW])
                        kb.act(S_sb[:, kt, :], ps[:], AF.Copy, scale=0.125,
                               xw=[f"CS{kt}"])
                        if kt == 0:
                            kb.v("tensor_copy", macc[:], S_sb[:, kt, :],
                                 xr=[f"CS{kt}"])
                        else:
                            kb.v("tensor_max", macc[:], macc[:],
                                 S_sb[:, kt, :], xr=[f"CS{kt}"])
                if hi:
                    kb.v("tensor_copy", drow[:], ps_d[:])
                    kb.v("reciprocal", rrow[:], drow[:])
                if lo:
                    kb.v("tensor_copy", macb[:], macc[:])
                    for qb in range(QW // 128):
                        kb.trans(ps_t[:], macb[:, qb * 128:(qb + 1) * 128],
                                 ident[:])
                        kb.v("tensor_reduce",
                             mq[:, qb:qb + 1].rearrange("p (a b) -> p a b",
                                                        a=1),
                             ps_t[:].rearrange("p (a b) -> p a b", a=1),
                             AX.X, ALU.max)
                    for qb in range(QW // 128):
                        kb.trans(ps_t[0:1, 0:128], mq[:, qb:qb + 1], ident[:])
                        kb.v("tensor_copy", mrow[:, qb * 128:(qb + 1) * 128],
                             ps_t[0:1, 0:128])
                    kb.mm(ps_s[0][:], onesr[:, 0:128], mrow[:])
                    kb.v("tensor_copy", mb_sb[:], ps_s[0][:])
                if hi:
                    kb.mm(ps_s[1][:], onesrf[:, 0:128], rrow[:])
                    kb.v("tensor_copy", rb_sb[:], ps_s[1][:])
                    for dv in range(4):
                        kb.v("tensor_mul", ostg[:, dv, :], ps_av[dv][:],
                             rb_sb[:])
                        kb.dma("sync",
                               out[h * 512 + dv * 128:h * 512 + (dv + 1) * 128,
                                   q0 - QW:q0], ostg[:, dv, :])
                if lo:
                    for g0 in range(0, KT, GSZ):
                        roles = [f"CS{k}" for k in range(g0, g0 + GSZ)]
                        kb.v("tensor_sub", S_sb[:, g0:g0 + GSZ, :],
                             S_sb[:, g0:g0 + GSZ, :],
                             mb_sb[:].rearrange(
                                 "p (a q) -> p a q", a=1).broadcast_to(
                                 [128, GSZ, QW]),
                             xr=roles, xw=roles)
                        kb.act(S_sb[:, g0:g0 + GSZ, :].rearrange(
                                   "p k q -> p (k q)"),
                               S_sb[:, g0:g0 + GSZ, :].rearrange(
                                   "p k q -> p (k q)"), AF.Exp,
                               xr=roles, xw=roles)
        if fused_rs:
            _sf = S_sb[:].rearrange("p c f -> p (c f)")
            wos = _sf[:, 0:16384].rearrange("p (c f) -> p c f", c=16)
            acb = _sf[:, 16384:18432].rearrange("p (c f) -> p c f", c=16)
            pst = _sf[:, 18432:19456]
            pp2 = [ps_av[0], ps_av[1]]
            ALLS = [f"CS{k}" for k in range(KT)]
            kb.dma("gpsimd", wos[:],
                   wos_in.rearrange("(c p) f -> p c f", p=128),
                   xr=ALLS, xw=ALLS)
            for tt in range(TB // 128):
                kb.dma("sync", acb[:],
                       out[:, tt * 128:(tt + 1) * 128].rearrange(
                           "(c p) t -> p c t", p=128),
                       xr=ALLS, xw=ALLS)
                for half in range(1024 // QW):
                    ps = pp2[half % 2]
                    for vchunk in range(16):
                        kb.mm(ps[:], acb[:, vchunk, :],
                              wos[:, vchunk, half * QW:(half + 1) * QW],
                              start=(vchunk == 0), stop=(vchunk == 15),
                              xr=ALLS)
                    kb.v("tensor_copy", pst[:, half * QW:(half + 1) * QW],
                         ps[:], xr=ALLS, xw=ALLS)
                kb.dma("sync", reg["rs_in"][tt * 128:(tt + 1) * 128, :],
                       pst[:], xr=ALLS)
        if standalone:
            kb.finalize()
    return nc



def build_D(TOK, ctx=None):
    W = min(512, TOK)
    NTH = TOK // W
    NTT = TOK // 128
    VC = 64
    FC = 16
    nc, es0, kb0, reg, standalone = _ctx_or_new(ctx)
    fused_rs = reg is not None and "rs_out" in reg
    o_in = None if fused_rs else \
        _dio(nc, reg, "o_all", [8192, TOK], BF16, "ExternalInput")
    wo_in = None if (reg is not None and "rs_out" in reg) else \
        _dio(nc, reg, "woT", [8192, D], F32, "ExternalInput")
    bo_in = _dio(nc, reg, "bo", [128, NC_D], F32, "ExternalInput")
    x_in = _dio(nc, reg, "x", [D, TOK], F32, "ExternalInput")
    xn_in = _dio(nc, reg, "xn", [D, TOK], BF16, "ExternalInput")
    wg_in = _dio(nc, reg, "wgT", [D, 8], F32, "ExternalInput")
    w1_in = _dio(nc, reg, "w1T", [8, D, 2048], F32, "ExternalInput")
    w2_in = _dio(nc, reg, "w2T", [8, D, 2048], F32, "ExternalInput")
    w3_in = _dio(nc, reg, "w3T", [8, 2048, D], F32, "ExternalInput")
    id_in = _dio(nc, reg, "ident", [128, 128], F32, "ExternalInput")
    sel_in = _dio(nc, reg, "selmat", [8, 1024], F32, "ExternalInput")
    ones_bf_col = _dio(nc, reg, "ones_bf_col", [128, 1], F32, "ExternalInput")
    ones_f32_row = _dio(nc, reg, "ones_f32_row", [1, 128], F32, "ExternalInput")
    eps_in = _dio(nc, reg, "eps", [1, 1], F32, "ExternalInput")
    x1_out = _dio(nc, reg, "x1", [D, TOK], F32, "ExternalOutput")
    xn1_out = _dio(nc, reg, "xn1", [D, TOK], BF16, "ExternalOutput")
    es = contextlib.ExitStack()
    with es:
        if standalone:
            kb = KB(nc, es)
        else:
            kb = kb0
            kb.es = es
        A1 = es.enter_context(nc.sbuf_tensor("D_A1", [128, 32768], BF16))
        xn = es.enter_context(nc.sbuf_tensor("D_xn_sb", [128, NC_D, TOK], BF16))
        s_fm = es.enter_context(nc.sbuf_tensor("D_s_fm", [128, NC_D, TOK], BF16))
        A6 = es.enter_context(nc.sbuf_tensor("D_A6", [128, 8192], BF16))
        oacc = es.enter_context(
            nc.sbuf_tensor("D_oacc", [128, NC_D, min(1024, TOK)], BF16))
        xe = es.enter_context(nc.sbuf_tensor("D_xe_sb", [128, NC_D, 512], BF16))
        lg = es.enter_context(nc.sbuf_tensor("D_lg", [128, TOK // 128, 8], F32))
        gsc = es.enter_context(nc.sbuf_tensor("D_gsc", [128, 64], F32))
        gbf = es.enter_context(nc.sbuf_tensor("D_gbf", [128, 8], BF16))
        gfm = es.enter_context(nc.sbuf_tensor("D_gfm", [8, TOK], BF16))
        selmD = es.enter_context(nc.sbuf_tensor("D_selmD", [8, 1024], BF16))
        gb = es.enter_context(nc.sbuf_tensor("D_gb", [128, W], BF16))
        xbc = es.enter_context(nc.sbuf_tensor("D_xbc", [128, 1024], BF16))
        sqc = es.enter_context(nc.sbuf_tensor("D_sqc", [128, 1024], BF16))

        ptm = es.enter_context(nc.sbuf_tensor("D_ptm", [128, 512], BF16))
        ttm = es.enter_context(nc.sbuf_tensor("D_ttm", [128, 128], F32))
        sil = es.enter_context(nc.sbuf_tensor("D_sil", [128, W], BF16))
        bo_sb = es.enter_context(nc.sbuf_tensor("D_bo_sb", [128, NC_D, 1], F32))
        ident = es.enter_context(nc.sbuf_tensor("D_ident_sb", [128, 128], BF16))
        cbf = es.enter_context(nc.sbuf_tensor("D_cbf", [128, 1], BF16))
        crow = es.enter_context(nc.sbuf_tensor("D_crow", [1, 128], F32))
        crbf = es.enter_context(nc.sbuf_tensor("D_crbf", [1, 128], BF16))
        ceps = es.enter_context(nc.sbuf_tensor("D_ceps", [1, 1], F32))
        wg_sb = es.enter_context(nc.sbuf_tensor("D_wg_sb", [128, NC_D, 8], BF16))
        C = {"ones_bf_col": cbf[:], "ones_f32_row": crow[:], "eps": ceps[:]}
        pp = [es.enter_context(nc.psum_tensor(f"D_pp{i}", [128, 512], F32))
              for i in range(2)]
        pt = es.enter_context(nc.psum_tensor("D_pt", [128, 128], BF16))
        ph = [es.enter_context(nc.psum_tensor(f"D_ph{i}", [128, W], F32))
              for i in range(2)]
        po = [es.enter_context(nc.psum_tensor(f"D_po{i}", [128, W], F32))
              for i in range(2)]
        pg = es.enter_context(nc.psum_tensor("D_pg", [128, 512], F32))
        kb.dma("gpsimd", cbf[:], ones_bf_col)
        kb.dma("sync", crow[:], ones_f32_row)
        kb.dma("gpsimd", crbf[:], ones_f32_row)
        kb.dma("sync", ceps[:], eps_in)
        kb.dma("gpsimd", ident[:], id_in)
        kb.dma("sync", bo_sb[:].rearrange("p c o -> p (c o)"), bo_in)
        kb.dma("sync", xn[:], xn_in.rearrange("(c p) t -> p c t", p=128))
        kb.dma("gpsimd", wg_sb[:], wg_in.rearrange("(c p) e -> p c e", p=128))
        kb.dma("gpsimd", selmD[:], sel_in)

        for tt in range(NTT):
            for dc in range(NC_D):
                kb.mm(pg[:, 0:8], xn[:, dc, tt * 128:(tt + 1) * 128],
                      wg_sb[:, dc, :], start=(dc == 0), stop=(dc == NC_D - 1))
            kb.v("tensor_copy", lg[:, tt, :], pg[:, 0:8])
        G = gsc
        for tt in range(NTT):
            l = lg[:, tt, :]
            m1 = G[:, 0:1]; eq1 = G[:, 8:16]; tmp = G[:, 16:24]
            m2 = G[:, 1:2]; eq2 = G[:, 24:32]; num = G[:, 32:40]
            den = G[:, 2:3]; sel = G[:, 40:48]; gam = G[:, 48:56]
            e21 = G[:, 3:4]
            kb.v("tensor_reduce", m1.rearrange("p (a b) -> p a b", a=1),
                 l.rearrange("p (a b) -> p a b", a=1), AX.X, ALU.max)
            kb.v("tensor_tensor", eq1, l, m1.broadcast_to([128, 8]), op=ALU.is_equal)
            kb.v("scalar_tensor_tensor", tmp, eq1, -1e30, l, op0=ALU.mult, op1=ALU.add)
            kb.v("tensor_reduce", m2.rearrange("p (a b) -> p a b", a=1),
                 tmp.rearrange("p (a b) -> p a b", a=1), AX.X, ALU.max)
            kb.v("tensor_tensor", eq2, tmp, m2.broadcast_to([128, 8]), op=ALU.is_equal)
            kb.v("tensor_sub", e21, m2, m1)
            kb.act(e21, e21, AF.Exp)
            kb.v("tensor_scalar", den, e21, 1.0, None, op0=ALU.add)
            kb.v("reciprocal", den, den)
            kb.v("tensor_sub", num, l, m1.broadcast_to([128, 8]))
            kb.act(num, num, AF.Exp)
            kb.v("tensor_add", sel, eq1, eq2)
            kb.v("tensor_mul", gam, num, sel)
            kb.v("tensor_scalar", gam, gam, den, None, op0=ALU.mult)
            kb.v("tensor_copy", gbf[:], gam)
            kb.trans(pt[0:8, 0:128], gbf[:], ident[:])
            kb.v("tensor_copy", gfm[0:8, tt * 128:(tt + 1) * 128],
                 pt[0:8, 0:128])

        if fused_rs:
            prj = A6[:].bitcast(F32)[:, 0:2048].bitcast(BF16)
            for tt in range(NTT):
                kb.dma("sync", prj[:, 0:1024],
                       reg["rs_out"][tt * 128:(tt + 1) * 128, :])
                for dc in range(NC_D):
                    kb.trans(pt[:], prj[:, dc * 128:(dc + 1) * 128], ident[:])
                    kb.v("tensor_scalar", ttm[:], pt[:], bo_sb[:, dc, :], None,
                         op0=ALU.add)
                    kb.v("scalar_tensor_tensor", ttm[:], ttm[:], 0.01, ttm[:],
                         op0=ALU.mult, op1=ALU.max)
                    kb.v("tensor_add", s_fm[:, dc, tt * 128:(tt + 1) * 128],
                         ttm[:], xn[:, dc, tt * 128:(tt + 1) * 128])
        else:
            woh = A1[:].rearrange("p (v f) -> p v f", v=64)
            ocol = A6[:].rearrange("p (v f) -> p v f", v=64)
            for dh in range(2):
                kb.dma("gpsimd", woh,
                       wo_in[:, dh * 512:(dh + 1) * 512].rearrange(
                           "(v p) f -> p v f", p=128))
                for tt in range(NTT):
                    kb.dma("sync", ocol,
                           o_in[:, tt * 128:(tt + 1) * 128].rearrange(
                               "(v p) t -> p v t", p=128))
                    ps = pp[tt % 2]
                    for vc in range(VC):
                        kb.mm(ps[:], ocol[:, vc, :], woh[:, vc, :],
                              start=(vc == 0), stop=(vc == VC - 1))
                    kb.v("tensor_copy", ptm[:], ps[:])
                    for q in range(4):
                        dc = dh * 4 + q
                        kb.trans(pt[:], ptm[:, q * 128:(q + 1) * 128], ident[:])
                        kb.v("tensor_scalar", ttm[:], pt[:], bo_sb[:, dc, :],
                             None, op0=ALU.add)
                        kb.v("scalar_tensor_tensor", ttm[:], ttm[:], 0.01,
                             ttm[:], op0=ALU.mult, op1=ALU.max)
                        kb.v("tensor_add",
                             s_fm[:, dc, tt * 128:(tt + 1) * 128],
                             ttm[:], xn[:, dc, tt * 128:(tt + 1) * 128])

        PW0 = min(1024, TOK)
        NP0 = TOK // PW0
        NW0 = PW0 // W
        w3res = A1[:, 0:16384].rearrange("p (c f) -> p c f", c=FC)
        h1p = A1[:, 16384:32768].rearrange("p (c f) -> p c f", c=FC * NW0)
        xef = xe[:].rearrange("p c f -> p (c f)")
        gb2 = A6[:, 0:2 * W].rearrange("p (c f) -> p c f", c=2)
        kb.suppress.add("D_xe_sb")
        for p in range(NP0):
            for e in range(8):
                kb.dma("gpsimd", w3res,
                       w3_in[e].rearrange("(c p) f -> p c f", p=128))
                for th2 in range(NW0):
                    t0 = p * PW0 + th2 * W
                    kb.mm(pg[:, 0:W], selmD[:, e * 128:(e + 1) * 128],
                          gfm[:, t0:t0 + W])
                    kb.v("tensor_copy", gb2[:, th2, :], pg[:, 0:W])
                for fc in range(FC):
                    bf = fc % 2
                    kb.dma("gpsimd",
                           xef[:, bf * 1024:bf * 1024 + 1024].rearrange(
                               "p (c f) -> p c f", c=NC_D),
                           w1_in[e][:, fc * 128:(fc + 1) * 128].rearrange(
                               "(c p) f -> p c f", p=128),
                           xw=[f"DX:a{bf}"])
                    kb.dma("gpsimd",
                           xef[:, 2048 + bf * 1024:
                               2048 + bf * 1024 + 1024].rearrange(
                               "p (c f) -> p c f", c=NC_D),
                           w2_in[e][:, fc * 128:(fc + 1) * 128].rearrange(
                               "(c p) f -> p c f", p=128),
                           xw=[f"DX:b{bf}"])
                    for th2 in range(NW0):
                        t0 = p * PW0 + th2 * W
                        idx = fc * NW0 + th2
                        psA = ph[idx % 2]
                        psB = po[idx % 2]
                        for dc in range(NC_D):
                            kb.mm(psA[:, 0:W],
                                  xef[:, bf * 1024 + dc * 128:
                                      bf * 1024 + dc * 128 + 128],
                                  xn[:, dc, t0:t0 + W], start=(dc == 0),
                                  stop=(dc == NC_D - 1), xr=[f"DX:a{bf}"])
                        for dc in range(NC_D):
                            kb.mm(psB[:, 0:W],
                                  xef[:, 2048 + bf * 1024 + dc * 128:
                                      2048 + bf * 1024 + dc * 128 + 128],
                                  xn[:, dc, t0:t0 + W], start=(dc == 0),
                                  stop=(dc == NC_D - 1), xr=[f"DX:b{bf}"])
                        kb.act(sil[:, 0:W], psA[:, 0:W], AF.Silu)
                        kb.v("scalar_tensor_tensor", gb[:, 0:W], psB[:, 0:W],
                             1.0, gb2[:, th2, :], op0=ALU.mult, op1=ALU.mult)
                        kb.v("tensor_mul", h1p[:, idx, 0:W], sil[:, 0:W],
                             gb[:, 0:W])
                for th2 in range(NW0):
                    t0w = th2 * W
                    for dc in range(NC_D):
                        ps = po[dc % 2]
                        for fc in range(FC):
                            kb.mm(ps[:, 0:W],
                                  w3res[:, fc, dc * 128:(dc + 1) * 128],
                                  h1p[:, fc * NW0 + th2, 0:W],
                                  start=(fc == 0), stop=(fc == FC - 1))
                        if e == 0:
                            kb.v("tensor_copy", oacc[:, dc, t0w:t0w + W],
                                 ps[:, 0:W])
                        else:
                            kb.v("tensor_add", oacc[:, dc, t0w:t0w + W],
                                 oacc[:, dc, t0w:t0w + W], ps[:, 0:W])
            for th2 in range(NW0):
                t0 = p * PW0 + th2 * W
                t0w = th2 * W
                for dc in range(NC_D):
                    kb.act(sil[:, 0:W], oacc[:, dc, t0w:t0w + W], AF.Silu)
                    kb.v("tensor_mul", s_fm[:, dc, t0:t0 + W],
                         s_fm[:, dc, t0:t0 + W], sil[:, 0:W])

        x1v = A1[:].bitcast(F32)[:, 0:NC_D * TOK].rearrange(
            "p (c t) -> p c t", c=NC_D)
        a6f = A6[:].bitcast(F32)
        xch = a6f[:, 0:TOK]
        tmpc = a6f[:, TOK:2 * TOK]
        for dc in range(NC_D):
            kb.dma("sync", xch, x_in[dc * 128:(dc + 1) * 128, :])
            kb.v("tensor_add", x1v[:, dc], xch, s_fm[:, dc])
            kb.dma("sync", x1_out[dc * 128:(dc + 1) * 128, :], x1v[:, dc])
        bcbv = oacc[:].rearrange("p c t -> p (c t)")
        emit_ln(kb, C, x1v, xn[:], TOK, "ln1", xbc[:], sqc[:], bcbv, tmpc,
                pp[0][:], pp[1][:])
        kb.dma("sync", xn1_out.rearrange("(c p) t -> p c t", p=128), xn[:])
        if standalone:
            kb.finalize()
    return nc



def _heads_cols(hg):
    q = np.concatenate([np.arange(h * 64, (h + 1) * 64) for h in range(hg, hg + 4)])
    k = 1024 + q
    v = np.concatenate([2048 + np.arange(h * 512, (h + 1) * 512)
                        for h in range(hg, hg + 4)])
    return q, k, v


def run_L0(inputs, TOK, runner):
    CONST = _const_inputs()
    ident = np.eye(128, dtype=np.float32)
    ones512 = np.ones((1, 512), np.float32)
    x = np.asarray(inputs["L0_attn_representations"])
    S = x.shape[1]
    assert S == 4 * TOK
    maps = []
    for c in range(NCORES):
        b, p = c // 4, c % 4
        xs = np.ascontiguousarray(x[b, p * TOK:(p + 1) * TOK, :].T)
        maps.append({"x": xs, **CONST})
    resA = runner(build_A(TOK), maps)
    xn_sh = [np.asarray(r["xn"]) for r in resA]
    WcT = np.ascontiguousarray(np.asarray(inputs["L0_attn_Wc"]).T)
    bc = np.asarray(inputs["L0_attn_bc"])
    maps = []
    for c in range(NCORES):
        b, p = c // 4, c % 4
        qc, kc, vc = _heads_cols(4 * p)
        xnf = np.concatenate([xn_sh[b * 4 + j] for j in range(4)], axis=1)
        maps.append({
            "xnf": np.ascontiguousarray(xnf),
            "wqk": np.ascontiguousarray(WcT[:, np.concatenate([qc, kc])]),
            "wv": np.ascontiguousarray(WcT[:, vc]),
            "bqk": bc[np.concatenate([qc, kc])][None, :].copy(),
            "bv": bc[vc][None, :].copy(),
            "onesr": ones512,
        })
    resB = runner(build_B(TOK), maps)
    maps = []
    for c in range(NCORES):
        maps.append({
            "qk": np.asarray(resB[c]["qk"]),
            "v": np.asarray(resB[c]["v"]),
            "ident": ident, "onesr": ones512,
            "onesc": np.ones((128, 1), np.float32),
        })
    resC = runner(build_C(TOK), maps)
    WoT = np.ascontiguousarray(np.asarray(inputs["L0_attn_Wo"]).T)
    bo = np.asarray(inputs["L0_attn_bo"]).reshape(NC_D, 128).T.copy()
    WgT = np.ascontiguousarray(np.asarray(inputs["L0_ffn_Wg"]).T)
    W1T = np.ascontiguousarray(np.asarray(inputs["L0_ffn_W1"]).transpose(0, 2, 1))
    W2T = np.ascontiguousarray(np.asarray(inputs["L0_ffn_W2"]).transpose(0, 2, 1))
    W3T = np.ascontiguousarray(np.asarray(inputs["L0_ffn_W3"]).transpose(0, 2, 1))
    maps = []
    for c in range(NCORES):
        b, p = c // 4, c % 4
        rng = slice(p * TOK, (p + 1) * TOK)
        o_all = np.concatenate(
            [np.asarray(resC[b * 4 + j]["att"])[:, rng] for j in range(4)], axis=0)
        xs = np.ascontiguousarray(x[b, rng, :].T)
        selmat = np.zeros((8, 1024), np.float32)
        for t8 in range(8):
            selmat[t8, t8 * 128:(t8 + 1) * 128] = 1.0
        maps.append({
            "o_all": np.ascontiguousarray(o_all),
            "woT": WoT, "bo": bo, "x": xs, "xn": xn_sh[c],
            "wgT": WgT, "w1T": W1T, "w2T": W2T, "w3T": W3T,
            "ident": ident, "selmat": selmat, **CONST,
        })
    resD = runner(build_D(TOK), maps)
    x1 = [np.asarray(r["x1"]) for r in resD]
    xn1 = [np.asarray(r["xn1"]) for r in resD]
    return x1, xn1


LR_LAM = 0
LR_LM = 2048
LR_CUR = 4096
LR_T1 = 6144
LR_T2 = 8192
LR_LC = 10240


def emit_row_cmul(kb, L, o_off, a_off, b_off, t_off):
    t1, t2 = L[:, t_off:t_off + 1024], L[:, t_off + 1024:t_off + 2048]
    are, aim = L[:, a_off:a_off + 1024], L[:, a_off + 1024:a_off + 2048]
    bre, bim = L[:, b_off:b_off + 1024], L[:, b_off + 1024:b_off + 2048]
    ore, oim = L[:, o_off:o_off + 1024], L[:, o_off + 1024:o_off + 2048]
    u1 = L[:, t_off + 2048:t_off + 3072]
    u2 = L[:, t_off + 3072:t_off + 4096]
    kb.v("tensor_mul", t1, are, bre)
    kb.v("tensor_mul", t2, aim, bim)
    kb.v("tensor_mul", u1, are, bim)
    kb.v("tensor_mul", u2, aim, bre)
    kb.v("tensor_sub", ore, t1, t2)
    kb.v("tensor_add", oim, u1, u2)


def emit_lambda_tables(kb, C, lrows, P, PP, stg, stq, tq, pbc, psA, psB, SCR):
    L = lrows
    om = L[:, LR_T1:LR_T1 + 1024]
    den = L[:, LR_T1 + 1024:LR_T1 + 2048]
    rec = L[:, LR_T2:LR_T2 + 1024]
    num = L[:, LR_T2 + 1024:LR_T2 + 2048]
    kb.v("tensor_mul", den, om, om)
    kb.v("tensor_scalar", den, den, 1.0, None, op0=ALU.add)
    kb.v("reciprocal", rec, den)
    kb.v("tensor_scalar", num, den, -1.0, 2.0, op0=ALU.mult, op1=ALU.add)
    kb.v("tensor_mul", L[:, LR_LAM:LR_LAM + 1024], num, rec)
    kb.v("tensor_mul", L[:, LR_LAM + 1024:LR_LAM + 2048], om, rec)
    kb.v("tensor_scalar", L[:, LR_LAM + 1024:LR_LAM + 2048],
         L[:, LR_LAM + 1024:LR_LAM + 2048], -2.0, None, op0=ALU.mult)
    kb.v("memset", P[0:1, 0:1024], 1.0)
    kb.v("memset", P[0:1, 1024:2048], 0.0)
    kb.v("tensor_copy", L[:, LR_LM:LR_LM + 2048], L[:, LR_LAM:LR_LAM + 2048])
    m = 1
    while m < 128:
        for q in range(4):
            kb.mm(psA[0:m, 0:512], C["ones_f32_row"][:, 0:m],
                  L[:, LR_LM + q * 512:LR_LM + (q + 1) * 512])
            kb.v("tensor_copy", stg[0:m, q * 512:(q + 1) * 512],
                 psA[0:m, 0:512], xr=[SCR], xw=[SCR])
        kb.v("tensor_mul", tq[0:m, 0:1024], P[0:m, 0:1024], stg[0:m, 0:1024],
             xr=[SCR], xw=[SCR])
        kb.v("tensor_mul", tq[0:m, 1024:2048], P[0:m, 1024:2048],
             stg[0:m, 1024:2048], xr=[SCR], xw=[SCR])
        kb.v("tensor_sub", stq[0:m, 0:1024], tq[0:m, 0:1024],
             tq[0:m, 1024:2048], xr=[SCR], xw=[SCR])
        kb.v("tensor_mul", tq[0:m, 0:1024], P[0:m, 0:1024],
             stg[0:m, 1024:2048], xr=[SCR], xw=[SCR])
        kb.v("tensor_mul", tq[0:m, 1024:2048], P[0:m, 1024:2048],
             stg[0:m, 0:1024], xr=[SCR], xw=[SCR])
        kb.v("tensor_add", stq[0:m, 1024:2048], tq[0:m, 0:1024],
             tq[0:m, 1024:2048], xr=[SCR], xw=[SCR])
        kb.dma("gpsimd", P[m:2 * m, :], stq[0:m, :], xr=[SCR])
        emit_row_cmul(kb, L, LR_LM, LR_LM, LR_LM, LR_T1)
        m *= 2
    for q in range(4):
        kb.mm(psB[:, 0:512], C["ones_f32_row"],
              L[:, LR_LAM + q * 512:LR_LAM + (q + 1) * 512])
        kb.v("tensor_copy", pbc[:, q * 512:(q + 1) * 512], psB[:, 0:512],
             xr=[SCR], xw=[SCR])
    kb.v("tensor_mul", tq[:, 0:1024], P[:, 0:1024], pbc[:, 0:1024],
         xr=[SCR], xw=[SCR])
    kb.v("tensor_mul", tq[:, 1024:2048], P[:, 1024:2048], pbc[:, 1024:2048],
         xr=[SCR], xw=[SCR])
    kb.v("tensor_sub", PP[:, 0:1024], tq[:, 0:1024], tq[:, 1024:2048],
         xr=[SCR])
    kb.v("tensor_mul", tq[:, 0:1024], P[:, 0:1024], pbc[:, 1024:2048],
         xr=[SCR], xw=[SCR])
    kb.v("tensor_mul", tq[:, 1024:2048], P[:, 1024:2048], pbc[:, 0:1024],
         xr=[SCR], xw=[SCR])
    kb.v("tensor_add", PP[:, 1024:2048], tq[:, 0:1024], tq[:, 1024:2048],
         xr=[SCR])
    kb.dma("gpsimd", L[:, LR_LC:LR_LC + 2048], PP[127:128, :])


def emit_omega(kb, lrows, adiff, AR, are_sin, sin_in, psA, SIN_ROLE):
    for half in range(2):
        kb.dma("sync", are_sin,
               sin_in[:, half * 512:(half + 1) * 512].rearrange(
                   "(c p) f -> p c f", p=128), xw=[SIN_ROLE])
        for nch in range(NC_D):
            kb.mm(psA[0:1, 0:512], adiff[:, nch:nch + 1],
                  are_sin[:, nch, :] if True else None,
                  start=(nch == 0), stop=(nch == NC_D - 1), xr=[SIN_ROLE])
        kb.v("tensor_copy",
             lrows[:, LR_T1 + half * 512:LR_T1 + (half + 1) * 512],
             psA[0:1, 0:512])



def build_E(TOK, ctx=None):
    NTT = TOK // 128
    nc, es0, kb0, reg, standalone = _ctx_or_new(ctx)
    xn_in = _dio(nc, reg, "xn1", [D, TOK], BF16, "ExternalInput")
    apos_in = _dio(nc, reg, "apos", [128, NC_D], F32, "ExternalInput")
    arev_in = _dio(nc, reg, "arev", [128, NC_D], F32, "ExternalInput")
    bw_in = _dio(nc, reg, "B_w", [1024, 1024], F32, "ExternalInput")
    cos_in = _dio(nc, reg, "cosF", [1024, 1024], F32, "ExternalInput")
    sin_in = _dio(nc, reg, "sinF", [1024, 1024], F32, "ExternalInput")
    ut_in = _dio(nc, reg, "ut128", [128, 128], F32, "ExternalInput")
    ones_f32_row = _dio(nc, reg, "ones_f32_row", [1, 128], F32, "ExternalInput")
    zc_out = _dio(nc, reg, "zcum", [TOK, 2048], BF16, "ExternalOutput")
    cl_out = _dio(nc, reg, "cloc", [1, NTT * 2048], F32, "ExternalOutput")
    h_out = _dio(nc, reg, "hrow", [1, 2048], F32, "ExternalOutput")
    es = contextlib.ExitStack()
    with es:
        if standalone:
            kb = KB(nc, es)
        else:
            kb = kb0
            kb.es = es
        ARE = es.enter_context(nc.sbuf_tensor("E_ARE", [128, 8192], F32))
        kb.suppress.add("E_ARE")
        lrows = es.enter_context(nc.sbuf_tensor("E_lrows", [1, 12288], F32))
        P = es.enter_context(nc.sbuf_tensor("E_P", [128, 2048], F32))
        PP = es.enter_context(nc.sbuf_tensor("E_PP", [128, 2048], F32))
        acolp = es.enter_context(nc.sbuf_tensor("E_acolp", [128, NC_D], F32))
        acolr = es.enter_context(nc.sbuf_tensor("E_acolr", [128, NC_D], F32))
        adiff = es.enter_context(nc.sbuf_tensor("E_adiff", [128, NC_D], F32))
        Bh = es.enter_context(nc.sbuf_tensor("E_Bh", [128, NC_D, 2048], BF16))
        utb = es.enter_context(nc.sbuf_tensor("E_utb", [128, 128], BF16))
        crow = es.enter_context(nc.sbuf_tensor("E_crow", [1, 128], F32))
        zbf = es.enter_context(nc.sbuf_tensor("E_zbf", [128, 2048], BF16))
        zcs = es.enter_context(nc.sbuf_tensor("E_zcs", [128, 2048], BF16))
        zlast = es.enter_context(nc.sbuf_tensor("E_zlast", [1, 2048], BF16))
        p127 = es.enter_context(nc.sbuf_tensor("E_p127", [1, 2048], F32))
        tch = es.enter_context(nc.sbuf_tensor("E_tch", [128, 1024], F32))
        C = {"ones_f32_row": crow[:]}
        pb = [es.enter_context(nc.psum_tensor(f"E_pb{i}", [128, 512], F32))
              for i in range(4)]
        pz = [es.enter_context(nc.psum_tensor(f"E_pz{i}", [128, 512], F32))
              for i in range(2)]
        are_sin = ARE[:, 0:4096].rearrange("p (c f) -> p c f", c=NC_D)
        stg = ARE[:, 4096:6144][0:64, :]
        stq = ARE[:, 6144:8192][0:64, :]
        tqv = ARE[:, 0:2048]
        pbc = ARE[:, 2048:4096]
        areb = ARE[:].bitcast(BF16)
        bwv = areb[:, 0:8192].rearrange("p (c f) -> p c f", c=NC_D)
        trig = areb[:, 8192:16384].rearrange("p (c f) -> p c f", c=NC_D)
        xnv = areb[:, 0:NC_D * TOK].rearrange("p (c t) -> p c t", c=NC_D)

        kb.dma("sync", crow[:], ones_f32_row)
        kb.dma("gpsimd", acolp[:], apos_in)
        kb.dma("gpsimd", acolr[:], arev_in)
        kb.dma("gpsimd", utb[:], ut_in)
        kb.v("tensor_sub", adiff[:], acolr[:], acolp[:])
        emit_omega(kb, lrows[:], adiff[:], ARE, are_sin, sin_in, pb[0][:],
                   "ARE:sin")
        kb.fence(r=["ARE:sin"], w=["ARE:scr"])
        emit_lambda_tables(kb, C, lrows[:], P[:], PP[:], stg, stq, tqv, pbc,
                           pb[0][:], pb[1][:], "ARE:scr")
        kb.dma("gpsimd", p127[:], P[127:128, :])
        kb.fence(r=["ARE:scr"], w=["ARE:bw", "ARE:trig"])
        kb.dma("gpsimd", bwv, bw_in.rearrange("(c p) f -> p c f", p=128),
               xw=["ARE:bw"])
        for part in range(2):
            tin = cos_in if part == 0 else sin_in
            sgn = 1.0 if part == 0 else -1.0
            kb.dma("gpsimd", trig, tin.rearrange("(c p) f -> p c f", p=128),
                   xw=["ARE:trig"])
            for dc in range(NC_D):
                for q in range(2):
                    ps = pb[2 + q]
                    for nch in range(NC_D):
                        kb.mm(ps[:], bwv[:, nch, dc * 128:(dc + 1) * 128],
                              trig[:, nch, q * 512:(q + 1) * 512],
                              start=(nch == 0), stop=(nch == NC_D - 1),
                              xr=["ARE:bw", "ARE:trig"])
                    kb.v("tensor_scalar",
                         Bh[:, dc, part * 1024 + q * 512:part * 1024 + (q + 1) * 512],
                         ps[:], sgn, None, op0=ALU.mult)
        kb.fence(r=["ARE:bw", "ARE:trig"], w=["ARE:xn"])
        kb.dma("sync", xnv, xn_in.rearrange("(c p) t -> p c t", p=128),
               xw=["ARE:xn"])
        kb.v("memset", lrows[:, LR_CUR:LR_CUR + 2048], 0.0)
        L = lrows[:]
        for t in range(NTT):
            for q4 in range(4):
                ps = pb[q4]
                for dc in range(NC_D):
                    kb.mm(ps[:], xnv[:, dc, t * 128:(t + 1) * 128],
                          Bh[:, dc, q4 * 512:(q4 + 1) * 512],
                          start=(dc == 0), stop=(dc == NC_D - 1),
                          xr=["ARE:xn"])
            for q in range(2):
                kb.v("tensor_mul", tch[:, 0:512],
                     P[:, q * 512:(q + 1) * 512], pb[q][:])
                kb.v("tensor_mul", tch[:, 512:1024],
                     P[:, 1024 + q * 512:1024 + (q + 1) * 512], pb[2 + q][:])
                kb.v("tensor_add", zbf[:, q * 512:(q + 1) * 512],
                     tch[:, 0:512], tch[:, 512:1024])
            for q in range(2):
                kb.v("tensor_mul", tch[:, 0:512],
                     P[:, q * 512:(q + 1) * 512], pb[2 + q][:])
                kb.v("tensor_mul", tch[:, 512:1024],
                     P[:, 1024 + q * 512:1024 + (q + 1) * 512], pb[q][:])
                kb.v("tensor_sub", zbf[:, 1024 + q * 512:1024 + (q + 1) * 512],
                     tch[:, 0:512], tch[:, 512:1024])
            for q4 in range(4):
                ps = pz[q4 % 2]
                kb.mm(ps[:], utb[:], zbf[:, q4 * 512:(q4 + 1) * 512])
                kb.v("tensor_copy", zcs[:, q4 * 512:(q4 + 1) * 512], ps[:])
            kb.dma("sync", zc_out[t * 128:(t + 1) * 128, :], zcs[:])
            kb.dma("sync", cl_out[:, t * 2048:(t + 1) * 2048],
                   L[:, LR_CUR:LR_CUR + 2048])
            kb.dma("gpsimd", zlast[:], zcs[127:128, :])
            zl = zlast[:]
            kb.v("tensor_mul", L[:, LR_T2:LR_T2 + 1024], p127[:, 0:1024],
                 zl[:, 0:1024])
            kb.v("tensor_mul", L[:, LR_T2 + 1024:LR_T2 + 2048],
                 p127[:, 1024:2048], zl[:, 1024:2048])
            kb.v("tensor_sub", L[:, LR_T1:LR_T1 + 1024],
                 L[:, LR_T2:LR_T2 + 1024], L[:, LR_T2 + 1024:LR_T2 + 2048])
            kb.v("tensor_mul", L[:, LR_T2:LR_T2 + 1024], p127[:, 0:1024],
                 zl[:, 1024:2048])
            kb.v("tensor_mul", L[:, LR_T2 + 1024:LR_T2 + 2048],
                 p127[:, 1024:2048], zl[:, 0:1024])
            kb.v("tensor_add", L[:, LR_T1 + 1024:LR_T1 + 2048],
                 L[:, LR_T2:LR_T2 + 1024], L[:, LR_T2 + 1024:LR_T2 + 2048])
            emit_row_cmul(kb, L, LR_CUR, LR_LC, LR_CUR, 0)
            kb.v("tensor_add", L[:, LR_CUR:LR_CUR + 2048],
                 L[:, LR_CUR:LR_CUR + 2048], L[:, LR_T1:LR_T1 + 2048])
        kb.dma("sync", h_out, lrows[:, LR_CUR:LR_CUR + 2048])
        if standalone:
            kb.finalize()
    return nc



def build_F(TOK, ctx=None):
    NTT = TOK // 128
    W = min(512, TOK)
    NTH = TOK // W
    NTC = NTT // 4 if NTT >= 4 else 1
    TPC = min(4, NTT)
    HH = 1408
    NHC = 11
    nc, es0, kb0, reg, standalone = _ctx_or_new(ctx)
    zc_in = _dio(nc, reg, "zcum", [TOK, 2048], BF16, "ExternalInput")
    cl_in = _dio(nc, reg, "cloc", [1, NTT * 2048], F32, "ExternalInput")
    hs_in = None if (reg is not None and "hall" in reg) else \
        _dio(nc, reg, "Hsel", [1, 8192], F32, "ExternalInput")
    selw_in = _dio(nc, reg, "selw", [4, 4], F32, "ExternalInput") \
        if hs_in is None else None
    apos_in = _dio(nc, reg, "apos", [128, NC_D], F32, "ExternalInput")
    arev_in = _dio(nc, reg, "arev", [128, NC_D], F32, "ExternalInput")
    cos_in = _dio(nc, reg, "cosF", [1024, 1024], F32, "ExternalInput")
    sin_in = _dio(nc, reg, "sinF", [1024, 1024], F32, "ExternalInput")
    cwt_in = _dio(nc, reg, "C_wT", [1024, 1024], F32, "ExternalInput")
    dcol_in = _dio(nc, reg, "Dcol", [128, NC_D], F32, "ExternalInput")
    xn_in = _dio(nc, reg, "xn1", [D, TOK], BF16, "ExternalInput")
    x_in = _dio(nc, reg, "x1", [D, TOK], F32, "ExternalInput")
    gw_in = _dio(nc, reg, "gwT", [D, 8], F32, "ExternalInput")
    eb_in = _dio(nc, reg, "ebias", [1, 8], F32, "ExternalInput")
    w1_in = _dio(nc, reg, "w1", [8, D, 2816], F32, "ExternalInput")
    w2_in = _dio(nc, reg, "w2", [8, 2816, D], F32, "ExternalInput")
    w3_in = _dio(nc, reg, "w3", [8, D, 2816], F32, "ExternalInput")
    id_in = _dio(nc, reg, "ident", [128, 128], F32, "ExternalInput")
    ones_bf_col = _dio(nc, reg, "ones_bf_col", [128, 1], F32, "ExternalInput")
    ones_f32_row = _dio(nc, reg, "ones_f32_row", [1, 128], F32, "ExternalInput")
    eps_in = _dio(nc, reg, "eps", [1, 1], F32, "ExternalInput")
    s1_dram = nc.dram_tensor("s1_scr", [D, TOK], BF16, kind="Internal").ap()
    g_dram = nc.dram_tensor("g_scr", [D, TOK], BF16, kind="Internal").ap()
    x2_out = _dio(nc, reg, "x2", [D, TOK], F32, "ExternalOutput")
    xn2_out = _dio(nc, reg, "xn2", [D, TOK], BF16, "ExternalOutput")
    es = contextlib.ExitStack()
    with es:
        if standalone:
            kb = KB(nc, es)
        else:
            kb = kb0
            kb.es = es
        ARE = es.enter_context(nc.sbuf_tensor("F_ARE", [128, 20480], F32))
        kb.suppress.add("F_ARE")
        lrows = es.enter_context(nc.sbuf_tensor("F_lrows", [1, 12288], F32))
        P = es.enter_context(nc.sbuf_tensor("F_P", [128, 2048], F32))
        PP = es.enter_context(nc.sbuf_tensor("F_PP", [128, 2048], F32))
        xn = es.enter_context(nc.sbuf_tensor("F_xn_sb", [128, NC_D, TOK], BF16))
        tch = es.enter_context(nc.sbuf_tensor("F_tch", [128, 1024], F32))
        acolp = es.enter_context(nc.sbuf_tensor("F_acolp", [128, NC_D], F32))
        acolr = es.enter_context(nc.sbuf_tensor("F_acolr", [128, NC_D], F32))
        adiff = es.enter_context(nc.sbuf_tensor("F_adiff", [128, NC_D], F32))
        dcol = es.enter_context(nc.sbuf_tensor("F_dcol", [128, NC_D, 1], F32))
        crow = es.enter_context(nc.sbuf_tensor("F_crow", [1, 128], F32))
        crbf = es.enter_context(nc.sbuf_tensor("F_crbf", [1, 128], BF16))
        cbf = es.enter_context(nc.sbuf_tensor("F_cbf", [128, 1], BF16))
        ceps = es.enter_context(nc.sbuf_tensor("F_ceps", [1, 1], F32))
        ident = es.enter_context(nc.sbuf_tensor("F_ident_sb", [128, 128], BF16))
        clrow = es.enter_context(nc.sbuf_tensor("F_clrow", [1, 2048], F32))
        ctot = es.enter_context(nc.sbuf_tensor("F_ctot", [1, 2048], F32))
        wg_sb = es.enter_context(nc.sbuf_tensor("F_wg_sb", [128, NC_D, 8], BF16))
        ebb = es.enter_context(nc.sbuf_tensor("F_ebb", [128, 8], F32))
        lg = es.enter_context(nc.sbuf_tensor("F_lg", [128, NTT, 8], F32))
        gtm = es.enter_context(nc.sbuf_tensor("F_gtm", [128, NTT, 8], F32))
        gsc = es.enter_context(nc.sbuf_tensor("F_gsc", [128, 64], F32))
        gbf = es.enter_context(nc.sbuf_tensor("F_gbf", [128, 1], BF16))
        grow = es.enter_context(nc.sbuf_tensor("F_grow", [1, W], BF16))
        gb = es.enter_context(nc.sbuf_tensor("F_gb", [128, W], BF16))
        silb = es.enter_context(nc.sbuf_tensor("F_silb", [128, W], BF16))
        xbc = es.enter_context(nc.sbuf_tensor("F_xbc", [128, 1024], BF16))
        sqc = es.enter_context(nc.sbuf_tensor("F_sqc", [128, 1024], BF16))
        tmpc = es.enter_context(nc.sbuf_tensor("F_tmpc", [128, 512], F32))
        C = {"ones_bf_col": cbf[:], "ones_f32_row": crow[:], "eps": ceps[:]}
        pb = [es.enter_context(nc.psum_tensor(f"F_pb{i}", [128, 512], F32))
              for i in range(4)]
        po = [es.enter_context(nc.psum_tensor(f"F_po{i}", [128, 512], F32))
              for i in range(2)]
        ptb = es.enter_context(nc.psum_tensor("F_ptb", [128, 128], BF16))
        A = ARE[:]
        Ab = A.bitcast(BF16)
        are_sin = A[:, 8192:12288].rearrange("p (c f) -> p c f", c=NC_D)
        stg = A[:, 12288:14336][0:64, :]
        stq = A[:, 14336:16384][0:64, :]
        tqv = A[:, 16384:18432]
        pbc = A[:, 18432:20480]
        chre = Ab[:, 0:8192].rearrange("p (c f) -> p c f", c=NC_D)
        chim = Ab[:, 8192:16384].rearrange("p (c f) -> p c f", c=NC_D)
        trig = Ab[:, 16384:24576].rearrange("p (c f) -> p c f", c=NC_D)
        cwtb = Ab[:, 24576:32768].rearrange("p (c f) -> p c f", c=NC_D)
        hsel = A[:, 8192:16384][0:1, :]
        l2048 = A[:, 16384:18432][0:1, :]
        growE = A[:, 18432:20480][0:1, :]
        hfm = Ab[:, 16384:24576].rearrange("p (c f) -> p c f", c=16)
        zt = Ab[:, 24576:26624]
        ht = Ab[:, 26624:28672]
        s1st = Ab[:, 28672:28672 + NC_D * W].rearrange(
            "p (c f) -> p c f", c=NC_D)
        PW = min(1024, TOK)
        NP = TOK // PW
        NW = PW // W
        wslot2 = Ab[:, 0:11264].rearrange("p (c f) -> p c f", c=NHC)
        h1v = Ab[:, 11264:11264 + NHC * PW].rearrange(
            "p (c f) -> p c f", c=NHC * NW)
        oaccv = Ab[:, 22528:22528 + NC_D * TOK].rearrange(
            "p (c f) -> p c f", c=NC_D)
        gbp = Ab[:, 38912:38912 + NW * W].rearrange(
            "p (c f) -> p c f", c=NW)
        w1s = P[:].bitcast(BF16)
        w3s = PP[:].bitcast(BF16)
        x2v = A[:, 0:NC_D * TOK].rearrange("p (c t) -> p c t", c=NC_D)
        bcb = A[:, 16384:16384 + 2 * TOK]
        x1ch = A[:, 16384:16384 + TOK]
        s1ch = A[:, 16384 + TOK:16384 + TOK + TOK // 2].bitcast(BF16)
        gch = A[:, 16384 + TOK + TOK // 2:16384 + 2 * TOK].bitcast(BF16)

        kb.dma("sync", crow[:], ones_f32_row)
        kb.dma("gpsimd", crbf[:], ones_f32_row)
        kb.dma("gpsimd", cbf[:], ones_bf_col)
        kb.dma("sync", ceps[:], eps_in)
        kb.dma("gpsimd", ident[:], id_in)
        kb.dma("sync", dcol[:].rearrange("p c o -> p (c o)"), dcol_in)
        kb.dma("sync", xn[:], xn_in.rearrange("(c p) t -> p c t", p=128))
        kb.dma("gpsimd", acolp[:], apos_in)
        kb.dma("gpsimd", acolr[:], arev_in)
        kb.dma("gpsimd", wg_sb[:], gw_in.rearrange("(c p) e -> p c e", p=128))
        kb.v("tensor_sub", adiff[:], acolr[:], acolp[:])
        emit_omega(kb, lrows[:], adiff[:], ARE, are_sin, sin_in, pb[0][:],
                   "AF:sin")
        kb.fence(r=["AF:sin"], w=["AF:scr"])
        emit_lambda_tables(kb, C, lrows[:], P[:], PP[:], stg, stq, tqv, pbc,
                           pb[0][:], pb[1][:], "AF:scr")
        kb.fence(r=["AF:scr"], w=["AF:trig", "AF:cwt", "AF:chat"])
        kb.dma("gpsimd", cwtb, cwt_in.rearrange("(c p) f -> p c f", p=128),
               xw=["AF:cwt"])
        for part in range(2):
            tin = cos_in if part == 0 else sin_in
            dst = chre if part == 0 else chim
            sgn = (1.0 / 1024.0) if part == 0 else (-1.0 / 1024.0)
            kb.dma("gpsimd", trig, tin.rearrange("(c p) f -> p c f", p=128),
                   xw=["AF:trig"])
            for fc in range(NC_D):
                for q in range(2):
                    ps = pb[2 + q]
                    for nch in range(NC_D):
                        kb.mm(ps[:], trig[:, nch, fc * 128:(fc + 1) * 128],
                              cwtb[:, nch, q * 512:(q + 1) * 512],
                              start=(nch == 0), stop=(nch == NC_D - 1),
                              xr=["AF:trig", "AF:cwt"])
                    kb.v("tensor_scalar", dst[:, fc, q * 512:(q + 1) * 512],
                         ps[:], sgn, None, op0=ALU.mult, xw=["AF:chat"])
        kb.fence(r=["AF:trig", "AF:cwt"], w=["AF:erow"])
        if hs_in is None:
            hallsb = A[:, 16384:18432][0:4, :]
            selwsb = A[:, 18432:18436][0:4, :]
            kb.dma("sync", hallsb, reg["hall"][0:4, :], xw=["AF:erow"])
            kb.dma("sync", selwsb, selw_in, xw=["AF:erow"])
            for k in range(4):
                for q in range(4):
                    kb.mm(pb[0][0:1, 0:512], selwsb[:, k:k + 1],
                          hallsb[:, q * 512:(q + 1) * 512], xr=["AF:erow"])
                    kb.v("tensor_copy",
                         hsel[:, k * 2048 + q * 512:k * 2048 + (q + 1) * 512],
                         pb[0][0:1, 0:512], xw=["AF:erow"])
        else:
            kb.dma("sync", hsel, hs_in, xw=["AF:erow"])
        L = lrows[:]
        kb.v("tensor_copy", L[:, LR_LM:LR_LM + 2048], L[:, LR_LC:LR_LC + 2048])
        for _ in range(int(np.log2(NTT))):
            emit_row_cmul(kb, L, LR_LM, LR_LM, LR_LM, LR_T1)
        kb.v("tensor_copy", l2048, L[:, LR_LM:LR_LM + 2048], xw=["AF:erow"])
        kb.v("memset", L[:, LR_CUR:LR_CUR + 2048], 0.0)
        kb.v("memset", L[:, LR_LM:LR_LM + 1024], 1.0)
        kb.v("memset", L[:, LR_LM + 1024:LR_LM + 2048], 0.0)
        for k in range(4):
            hk = hsel[:, k * 2048:(k + 1) * 2048]
            kb.v("tensor_mul", L[:, LR_T1:LR_T1 + 1024],
                 L[:, LR_LM:LR_LM + 1024], hk[:, 0:1024], xr=["AF:erow"])
            kb.v("tensor_mul", L[:, LR_T1 + 1024:LR_T1 + 2048],
                 L[:, LR_LM + 1024:LR_LM + 2048], hk[:, 1024:2048],
                 xr=["AF:erow"])
            kb.v("tensor_sub", L[:, LR_T2:LR_T2 + 1024],
                 L[:, LR_T1:LR_T1 + 1024], L[:, LR_T1 + 1024:LR_T1 + 2048])
            kb.v("tensor_mul", L[:, LR_T1:LR_T1 + 1024],
                 L[:, LR_LM:LR_LM + 1024], hk[:, 1024:2048], xr=["AF:erow"])
            kb.v("tensor_mul", L[:, LR_T1 + 1024:LR_T1 + 2048],
                 L[:, LR_LM + 1024:LR_LM + 2048], hk[:, 0:1024], xr=["AF:erow"])
            kb.v("tensor_add", L[:, LR_T2 + 1024:LR_T2 + 2048],
                 L[:, LR_T1:LR_T1 + 1024], L[:, LR_T1 + 1024:LR_T1 + 2048])
            kb.v("tensor_add", L[:, LR_CUR:LR_CUR + 2048],
                 L[:, LR_CUR:LR_CUR + 2048], L[:, LR_T2:LR_T2 + 2048])
            if k < 3:
                kb.v("tensor_copy", L[:, LR_LAM:LR_LAM + 2048], l2048,
                     xr=["AF:erow"])
                emit_row_cmul(kb, L, LR_LM, LR_LM, LR_LAM, LR_T1)
        kb.fence(r=["AF:erow", "AF:chat"], w=["AF:hfm", "AF:zt", "AF:s1"])
        for tc in range(NTC):
            for ti in range(TPC):
                t = tc * TPC + ti
                kb.dma("sync", zt, zc_in[t * 128:(t + 1) * 128, :],
                       xw=["AF:zt"])
                kb.dma("sync", clrow[:], cl_in[:, t * 2048:(t + 1) * 2048])
                kb.v("tensor_add", ctot[:], clrow[:], L[:, LR_CUR:LR_CUR + 2048])
                emit_row_cmul(kb, L, LR_CUR, LR_LC, LR_CUR, LR_T1)
                for q in range(4):
                    kb.mm(pb[q][:], crow[:], ctot[:, q * 512:(q + 1) * 512])
                for q in range(2):
                    zre = zt[:, q * 512:(q + 1) * 512]
                    zim = zt[:, 1024 + q * 512:1024 + (q + 1) * 512]
                    pre = P[:, q * 512:(q + 1) * 512]
                    pim = P[:, 1024 + q * 512:1024 + (q + 1) * 512]
                    ppre = PP[:, q * 512:(q + 1) * 512]
                    ppim = PP[:, 1024 + q * 512:1024 + (q + 1) * 512]
                    cbre = pb[q][:]
                    cbim = pb[2 + q][:]
                    t1 = tch[:, 0:512]
                    t2 = tch[:, 512:1024]
                    kb.v("tensor_mul", t1, pre, zre, xr=["AF:zt"])
                    kb.v("tensor_mul", t2, pim, zim, xr=["AF:zt"])
                    kb.v("tensor_sub", t1, t1, t2)
                    kb.v("tensor_mul", t2, ppre, cbre)
                    kb.v("tensor_add", t1, t1, t2)
                    kb.v("tensor_mul", t2, ppim, cbim)
                    kb.v("tensor_sub", ht[:, q * 512:(q + 1) * 512], t1, t2,
                         xw=["AF:zt"])
                    kb.v("tensor_mul", t1, pre, zim, xr=["AF:zt"])
                    kb.v("tensor_mul", t2, pim, zre, xr=["AF:zt"])
                    kb.v("tensor_add", t1, t1, t2)
                    kb.v("tensor_mul", t2, ppre, cbim)
                    kb.v("tensor_add", t1, t1, t2)
                    kb.v("tensor_mul", t2, ppim, cbre)
                    kb.v("tensor_add",
                         ht[:, 1024 + q * 512:1024 + (q + 1) * 512], t1, t2,
                         xw=["AF:zt"])
                for fch in range(16):
                    kb.trans(ptb[:], ht[:, fch * 128:(fch + 1) * 128],
                             ident[:], xr=["AF:zt"])
                    kb.v("tensor_copy", hfm[:, fch, ti * 128:(ti + 1) * 128],
                         ptb[:], xw=["AF:hfm"])
            for dc in range(NC_D):
                ps = po[dc % 2]
                for fch in range(16):
                    lhs = (chre if fch < 8 else chim)[:, fch % 8,
                                                      dc * 128:(dc + 1) * 128]
                    kb.mm(ps[:, 0:W], lhs, hfm[:, fch, 0:W],
                          start=(fch == 0), stop=(fch == 15),
                          xr=["AF:hfm", "AF:chat"])
                kb.v("scalar_tensor_tensor", s1st[:, dc, 0:W],
                     xn[:, dc, tc * W:tc * W + W], dcol[:, dc, :], ps[:, 0:W],
                     op0=ALU.mult, op1=ALU.add, xw=["AF:s1"])
            kb.dma("sync",
                   s1_dram.rearrange("(c p) t -> p c t", p=128)[:, :,
                       tc * W:tc * W + W],
                   s1st[:, :, 0:W], xr=["AF:s1"])
        kb.dma("sync", ebb[0:1, :], eb_in)
        kb.mm(pb[0][:, 0:8], crow[:], ebb[0:1, :])
        kb.v("tensor_copy", ebb[:], pb[0][:, 0:8])
        for tt in range(NTT):
            for dc in range(NC_D):
                kb.mm(pb[1][:, 0:8], xn[:, dc, tt * 128:(tt + 1) * 128],
                      wg_sb[:, dc, :], start=(dc == 0), stop=(dc == NC_D - 1))
            kb.v("tensor_copy", lg[:, tt, :], pb[1][:, 0:8])
        G = gsc
        for tt in range(NTT):
            l = lg[:, tt, :]
            m1 = G[:, 0:1]; sc = G[:, 8:16]; tmp = G[:, 16:24]
            m2 = G[:, 1:2]; eq1 = G[:, 24:32]; eq2 = G[:, 32:40]
            ssum = G[:, 2:3]; bia = G[:, 40:48]
            kb.v("tensor_reduce", m1.rearrange("p (a b) -> p a b", a=1),
                 l.rearrange("p (a b) -> p a b", a=1), AX.X, ALU.max)
            kb.v("tensor_sub", sc, l, m1.broadcast_to([128, 8]))
            kb.act(sc, sc, AF.Exp)
            kb.v("tensor_reduce", ssum.rearrange("p (a b) -> p a b", a=1),
                 sc.rearrange("p (a b) -> p a b", a=1), AX.X, ALU.add)
            kb.v("reciprocal", ssum, ssum)
            kb.v("tensor_scalar", sc, sc, ssum, None, op0=ALU.mult)
            kb.v("tensor_add", bia, sc, ebb[:])
            kb.v("tensor_reduce", m1.rearrange("p (a b) -> p a b", a=1),
                 bia.rearrange("p (a b) -> p a b", a=1), AX.X, ALU.max)
            kb.v("tensor_tensor", eq1, bia, m1.broadcast_to([128, 8]),
                 op=ALU.is_equal)
            kb.v("scalar_tensor_tensor", tmp, eq1, -1e30, bia,
                 op0=ALU.mult, op1=ALU.add)
            kb.v("tensor_reduce", m2.rearrange("p (a b) -> p a b", a=1),
                 tmp.rearrange("p (a b) -> p a b", a=1), AX.X, ALU.max)
            kb.v("tensor_tensor", eq2, tmp, m2.broadcast_to([128, 8]),
                 op=ALU.is_equal)
            kb.v("tensor_add", eq1, eq1, eq2)
            kb.v("tensor_mul", gtm[:, tt, :], sc, eq1)
        kb.fence(r=["AF:sin", "AF:scr", "AF:trig", "AF:cwt", "AF:chat",
                    "AF:erow", "AF:hfm", "AF:zt", "AF:s1"],
                 w=["AF:w", "AF:h1", "AF:oacc", "AF:gb"])
        kb.fence(w=["F_P", "F_PP", "AF:w1c0", "AF:w1c1", "AF:w3c0",
                    "AF:w3c1"])
        kb.suppress.update({"F_P", "F_PP"})
        tmpcb = tmpc[:].bitcast(BF16)
        for e in range(8):
            for hh in range(2):
                c0 = hh * HH
                kb.dma("gpsimd", wslot2[:, 0:NHC, :],
                       w2_in[e][c0:c0 + HH, :].rearrange(
                           "(c p) f -> p c f", p=128), xw=["AF:w"])
                for p in range(NP):
                    for th2 in range(NW):
                        t0 = p * PW + th2 * W
                        for ti in range(W // 128):
                            tl = (t0 + ti * 128) // 128
                            kb.v("tensor_copy", gbf[:], gtm[:, tl, e:e + 1])
                            kb.trans(ptb[0:1, 0:128], gbf[:], ident[:])
                            kb.v("tensor_copy",
                                 grow[:, ti * 128:(ti + 1) * 128],
                                 ptb[0:1, 0:128])
                        kb.mm(pb[3][:, 0:W], crbf[:], grow[:])
                        kb.v("tensor_copy", gbp[:, th2, :], pb[3][:, 0:W],
                             xw=["AF:gb"])
                    for hc in range(NHC):
                        bf = hc % 2
                        kb.dma("gpsimd",
                               w1s[:, bf * 1024:bf * 1024 + 1024].rearrange(
                                   "p (c f) -> p c f", c=NC_D),
                               w1_in[e][:, c0 + hc * 128:c0 + hc * 128 + 128]
                               .rearrange("(c p) f -> p c f", p=128),
                               xw=[f"AF:w1c{bf}"])
                        kb.dma("gpsimd",
                               w3s[:, bf * 1024:bf * 1024 + 1024].rearrange(
                                   "p (c f) -> p c f", c=NC_D),
                               w3_in[e][:, c0 + hc * 128:c0 + hc * 128 + 128]
                               .rearrange("(c p) f -> p c f", p=128),
                               xw=[f"AF:w3c{bf}"])
                        for th2 in range(NW):
                            t0 = p * PW + th2 * W
                            idx = hc * NW + th2
                            psA = pb[(2 * idx) % 4]
                            psB = pb[(2 * idx + 1) % 4]
                            for dc in range(NC_D):
                                kb.mm(psA[:, 0:W],
                                      w1s[:, bf * 1024 + dc * 128:
                                          bf * 1024 + dc * 128 + 128],
                                      xn[:, dc, t0:t0 + W], start=(dc == 0),
                                      stop=(dc == NC_D - 1),
                                      xr=[f"AF:w1c{bf}"])
                            for dc in range(NC_D):
                                kb.mm(psB[:, 0:W],
                                      w3s[:, bf * 1024 + dc * 128:
                                          bf * 1024 + dc * 128 + 128],
                                      xn[:, dc, t0:t0 + W], start=(dc == 0),
                                      stop=(dc == NC_D - 1),
                                      xr=[f"AF:w3c{bf}"])
                            kb.v("scalar_tensor_tensor", tmpcb[:, 0:W],
                                 psA[:, 0:W], 1.0, gbp[:, th2, :],
                                 op0=ALU.mult, op1=ALU.mult, xr=["AF:gb"])
                            kb.act(tmpcb[:, 0:W], tmpcb[:, 0:W], AF.Silu)
                            kb.v("scalar_tensor_tensor", tmpcb[:, 512:512 + W],
                                 psB[:, 0:W], 1.0, gbp[:, th2, :],
                                 op0=ALU.mult, op1=ALU.mult, xr=["AF:gb"])
                            kb.v("tensor_mul", h1v[:, idx, :],
                                 tmpcb[:, 0:W], tmpcb[:, 512:512 + W],
                                 xw=["AF:h1"])
                    for th2 in range(NW):
                        t0 = p * PW + th2 * W
                        for dc in range(NC_D):
                            ps = po[dc % 2]
                            for hc in range(NHC):
                                kb.mm(ps[:, 0:W],
                                      wslot2[:, hc, dc * 128:(dc + 1) * 128],
                                      h1v[:, hc * NW + th2, :],
                                      start=(hc == 0), stop=(hc == NHC - 1),
                                      xr=["AF:w", "AF:h1"])
                            if e == 0 and hh == 0:
                                kb.v("tensor_copy",
                                     oaccv[:, dc, t0:t0 + W], ps[:, 0:W],
                                     xw=["AF:oacc"])
                            else:
                                kb.v("tensor_add", oaccv[:, dc, t0:t0 + W],
                                     oaccv[:, dc, t0:t0 + W], ps[:, 0:W],
                                     xr=["AF:oacc"], xw=["AF:oacc"])
        for th in range(NTH):
            t0 = th * W
            for dc in range(NC_D):
                kb.act(silb[:, 0:W], oaccv[:, dc, t0:t0 + W], AF.Silu,
                       xr=["AF:oacc"])
                kb.dma("sync",
                       g_dram.rearrange("(c p) t -> p c t", p=128)[:, :,
                           t0:t0 + W][:, dc],
                       silb[:, 0:W])
        kb.fence(r=["AF:w", "AF:h1", "AF:oacc", "AF:gb"],
                 w=["AF:x2", "AF:chk"])
        for dc in range(NC_D):
            kb.dma("sync", x1ch[:, 0:TOK], x_in[dc * 128:(dc + 1) * 128, :],
                   xw=["AF:chk"])
            kb.dma("sync", s1ch[:, 0:TOK], s1_dram[dc * 128:(dc + 1) * 128, :],
                   xw=["AF:chk"])
            kb.dma("sync", gch[:, 0:TOK], g_dram[dc * 128:(dc + 1) * 128, :],
                   xw=["AF:chk"])
            kb.v("tensor_mul", x2v[:, dc], s1ch[:, 0:TOK], gch[:, 0:TOK],
                 xr=["AF:chk"], xw=["AF:x2"])
            kb.v("tensor_add", x2v[:, dc], x2v[:, dc], x1ch[:, 0:TOK],
                 xr=["AF:chk", "AF:x2"], xw=["AF:x2"])
            kb.dma("sync", x2_out[dc * 128:(dc + 1) * 128, :], x2v[:, dc],
                   xr=["AF:x2"])
        kb.fence(r=["AF:chk", "AF:oacc", "AF:gb"], w=["AF:bcb"])
        emit_ln(kb, C, x2v, xn[:], TOK, "ln2", xbc[:], sqc[:],
                bcb, tmpc[:], pb[0][:], pb[1][:], xrole=["AF:x2"],
                bcrole=["AF:bcb"], rows_ap=lrows[:, 0:3 * TOK])
        kb.dma("sync", xn2_out.rearrange("(c p) t -> p c t", p=128), xn[:])
        if standalone:
            kb.finalize()
    return nc


def _dft_consts():
    jj = np.arange(1024)
    ang = 2.0 * np.pi * np.outer(jj, jj) / 1024.0
    return (np.cos(ang).astype(np.float32), np.sin(ang).astype(np.float32))


def run_L1(inputs, TOK, runner, x1_sh, xn1_sh):
    CONST = _const_inputs()
    NTT = TOK // 128
    COS, SIN = _dft_consts()
    a_params = np.asarray(inputs["L1_attn_a_params"])
    apos = np.zeros(1024, np.float32); apos[1:512] = a_params[:511]
    arev = np.zeros(1024, np.float32); arev[513:] = a_params[:511][::-1]
    apos_h = apos.reshape(NC_D, 128).T.copy()
    arev_h = arev.reshape(NC_D, 128).T.copy()
    B_w = np.ascontiguousarray(np.asarray(inputs["L1_attn_B_w"]))
    maps = []
    for c in range(NCORES):
        maps.append({
            "xn1": xn1_sh[c], "apos": apos_h, "arev": arev_h, "B_w": B_w,
            "cosF": COS, "sinF": SIN,
            "ut128": np.triu(np.ones((128, 128), np.float32)),
            "ones_f32_row": CONST["ones_f32_row"],
        })
    resE = runner(build_E(TOK), maps)
    Hs = [np.asarray(r["hrow"]).reshape(2048) for r in resE]
    C_wT = np.ascontiguousarray(np.asarray(inputs["L1_attn_C_w"]).T)
    Dv = np.asarray(inputs["L1_attn_D"]).reshape(NC_D, 128).T.copy()
    gwT = np.ascontiguousarray(np.asarray(inputs["L1_ffn_gate_w"]).T)
    eb = np.asarray(inputs["L1_ffn_expert_bias"])[None, :].copy()
    w1 = np.ascontiguousarray(np.asarray(inputs["L1_ffn_w1"]))
    w2 = np.ascontiguousarray(np.asarray(inputs["L1_ffn_w2"]))
    w3 = np.ascontiguousarray(np.asarray(inputs["L1_ffn_w3"]))
    maps = []
    for c in range(NCORES):
        b, p = c // 4, c % 4
        hsel = np.zeros((1, 8192), np.float32)
        for k in range(4):
            i2 = p - 1 - k
            if i2 >= 0:
                hsel[0, k * 2048:(k + 1) * 2048] = Hs[b * 4 + i2]
        maps.append({
            "zcum": np.asarray(resE[c]["zcum"]),
            "cloc": np.asarray(resE[c]["cloc"]),
            "Hsel": hsel, "apos": apos_h, "arev": arev_h,
            "cosF": COS, "sinF": SIN, "C_wT": C_wT, "Dcol": Dv,
            "xn1": xn1_sh[c], "x1": x1_sh[c], "gwT": gwT, "ebias": eb,
            "w1": w1, "w2": w2, "w3": w3,
            "ident": np.eye(128, dtype=np.float32), **CONST,
        })
    resF = runner(build_F(TOK), maps)
    return ([np.asarray(r["x2"]) for r in resF],
            [np.asarray(r["xn2"]) for r in resF])



def build_G(TOK, ctx=None):
    W = min(512, TOK)
    NTH = TOK // W
    NTT = TOK // 128
    nc, es0, kb0, reg, standalone = _ctx_or_new(ctx)
    xn_in = _dio(nc, reg, "xn2", [D, TOK], BF16, "ExternalInput")
    x_in = _dio(nc, reg, "x2", [D, TOK], F32, "ExternalInput")
    wqkv_in = _dio(nc, reg, "WqkvT", [D, 3072], F32, "ExternalInput")
    wproj_in = _dio(nc, reg, "WprojT", [D, D], F32, "ExternalInput")
    bproj_in = _dio(nc, reg, "bprojc", [128, NC_D], F32, "ExternalInput")
    sig_in = _dio(nc, reg, "tile_sigs", [8, 32], F32, "ExternalInput")
    pe_in = _dio(nc, reg, "peT", [32, TOK], F32, "ExternalInput")
    wt_in = _dio(nc, reg, "WtT", [8, D, D], F32, "ExternalInput")
    bt_in = _dio(nc, reg, "bt", [8, D], F32, "ExternalInput")
    sel_in = _dio(nc, reg, "selmat", [8, 1024], F32, "ExternalInput")
    id_in = _dio(nc, reg, "ident", [128, 128], F32, "ExternalInput")
    ones_bf_col = _dio(nc, reg, "ones_bf_col", [128, 1], F32, "ExternalInput")
    ones_f32_row = _dio(nc, reg, "ones_f32_row", [1, 128], F32, "ExternalInput")
    eps_in = _dio(nc, reg, "eps", [1, 1], F32, "ExternalInput")
    x3_out = _dio(nc, reg, "x3", [D, TOK], F32, "ExternalOutput")
    xn3_out = _dio(nc, reg, "xn3", [D, TOK], BF16, "ExternalOutput")
    es = contextlib.ExitStack()
    with es:
        if standalone:
            kb = KB(nc, es)
        else:
            kb = kb0
            kb.es = es
        xn = es.enter_context(nc.sbuf_tensor("G_xn_sb", [128, NC_D, TOK], BF16))
        AG = es.enter_context(nc.sbuf_tensor("G_AG", [128, 16384], F32))
        kb.suppress.add("G_AG")
        wqkv = AG[:].bitcast(BF16)[:, 0:NC_D * 3072].rearrange(
            "p (c f) -> p c f", c=NC_D)
        wproj = AG[:].bitcast(BF16)[:, NC_D * 3072:NC_D * 3072 + NC_D * 1024
                                    ].rearrange("p (c f) -> p c f", c=NC_D)
        s2 = es.enter_context(nc.sbuf_tensor("G_s2", [128, NC_D, TOK], BF16))
        wts = es.enter_context(nc.sbuf_tensor("G_wts", [128, NC_D, 1024], BF16))
        AG2 = es.enter_context(nc.sbuf_tensor("G_AG2", [128, 6144], F32))
        kb.suppress.add("G_AG2")
        R2 = ["G2:a"]
        qkvb = AG2[:].bitcast(BF16)[:, 0:3072]
        aat = AG2[:, 1536:2560]
        aab = AG2[:].bitcast(BF16)[:, 5120:6144]
        otf = AG2[:].bitcast(BF16)[:, 6144:7168]
        ptm = AG2[:].bitcast(BF16)[:, 7168:8192]
        sc = es.enter_context(nc.sbuf_tensor("G_sc", [128, 16, 16], F32))
        srd = es.enter_context(nc.sbuf_tensor("G_srd", [128, 16], F32))
        sig = es.enter_context(nc.sbuf_tensor("G_sig", [8, 32], F32))
        sigs = es.enter_context(nc.sbuf_tensor("G_sigs", [8, 32], BF16))
        sigt = es.enter_context(nc.sbuf_tensor("G_sigt", [32, 8], BF16))
        pet = es.enter_context(nc.sbuf_tensor("G_pet", [32, TOK], BF16))
        msk = es.enter_context(nc.sbuf_tensor("G_msk", [128, NTT, 8], F32))
        mbf = es.enter_context(nc.sbuf_tensor("G_mbf", [128, 8], BF16))
        m8 = es.enter_context(nc.sbuf_tensor("G_m8", [8, TOK], BF16))
        selm = es.enter_context(nc.sbuf_tensor("G_selm", [8, 1024], BF16))
        btb = es.enter_context(nc.sbuf_tensor("G_btb", [8, 1024], BF16))
        xe = AG2[:].bitcast(BF16)[:, 8192:8192 + NC_D * W].rearrange(
            "p (c f) -> p c f", c=NC_D)
        gbv = es.enter_context(nc.sbuf_tensor("G_gbv", [128, W], BF16))
        silb = es.enter_context(nc.sbuf_tensor("G_silb", [128, W], BF16))
        bpc = es.enter_context(nc.sbuf_tensor("G_bpc", [128, NC_D, 1], F32))
        ident = es.enter_context(nc.sbuf_tensor("G_ident_sb", [128, 128], BF16))
        cbf = es.enter_context(nc.sbuf_tensor("G_cbf", [128, 1], BF16))
        crow = es.enter_context(nc.sbuf_tensor("G_crow", [1, 128], F32))
        ceps = es.enter_context(nc.sbuf_tensor("G_ceps", [1, 1], F32))
        xbc = es.enter_context(nc.sbuf_tensor("G_xbc", [128, 1024], BF16))
        sqc = es.enter_context(nc.sbuf_tensor("G_sqc", [128, 1024], BF16))
        tmpc = es.enter_context(nc.sbuf_tensor("G_tmpc", [128, 512], F32))
        bcb = es.enter_context(nc.sbuf_tensor("G_bcb", [128, 2 * TOK], F32))

        C = {"ones_bf_col": cbf[:], "ones_f32_row": crow[:], "eps": ceps[:]}
        pq = [es.enter_context(nc.psum_tensor(f"G_pq{i}", [128, 512], F32))
              for i in range(6)]
        ptb = es.enter_context(nc.psum_tensor("G_ptb", [128, 128], BF16))
        kb.dma("sync", crow[:], ones_f32_row)
        kb.dma("gpsimd", cbf[:], ones_bf_col)
        kb.dma("sync", ceps[:], eps_in)
        kb.dma("gpsimd", ident[:], id_in)
        kb.dma("sync", xn[:], xn_in.rearrange("(c p) t -> p c t", p=128))
        kb.dma("gpsimd", wqkv, wqkv_in.rearrange("(c p) f -> p c f", p=128),
               xw=["AG:wq"])
        kb.dma("gpsimd", wproj, wproj_in.rearrange("(c p) f -> p c f", p=128),
               xw=["AG:wp"])
        kb.dma("sync", bpc[:].rearrange("p c o -> p (c o)"), bproj_in)
        kb.dma("sync", sig[:], sig_in)
        kb.dma("gpsimd", pet[:], pe_in)
        kb.dma("gpsimd", selm[:], sel_in)
        kb.dma("gpsimd", btb[:], bt_in)
        for tt in range(NTT):
            t0 = tt * 128
            for q6 in range(6):
                ps = pq[q6]
                for dc in range(NC_D):
                    kb.mm(ps[:], xn[:, dc, t0:t0 + 128],
                          wqkv[:, dc, q6 * 512:(q6 + 1) * 512],
                          start=(dc == 0), stop=(dc == NC_D - 1),
                          xr=["AG:wq"])
                kb.v("tensor_copy", qkvb[:, q6 * 512:(q6 + 1) * 512], ps[:], xr=["G2:a"], xw=["G2:a"])
            kb.v("tensor_scalar",
                 qkvb.rearrange("p (h x) -> p h x", h=16)[:, :, 0:64],
                 qkvb.rearrange("p (h x) -> p h x", h=16)[:, :, 0:64],
                 0.125, None, op0=ALU.mult, xr=["G2:a"], xw=["G2:a"])
            for h in range(16):
                qh = qkvb[:, h * 192:h * 192 + 64]
                for g in range(16):
                    kg = qkvb[:, g * 192 + 64:g * 192 + 128]
                    kb.v("tensor_mul", aat[:, 0:64], qh, kg,
                         xr=["G2:a"], xw=["G2:a"])
                    kb.v("tensor_reduce", sc[:, h:h + 1, g:g + 1],
                         aat[:, 0:64].rearrange("p (a b) -> p a b", a=1),
                         AX.X, ALU.add, xr=["G2:a"])
            kb.v("tensor_reduce", srd[:].rearrange("p (a b) -> p a b", a=16),
                 sc[:], AX.X, ALU.max)
            kb.v("tensor_sub", sc[:], sc[:],
                 srd[:].rearrange("p (a b) -> p a b", a=16).broadcast_to(
                     [128, 16, 16]))
            kb.act(sc[:].rearrange("p a b -> p (a b)"),
                   sc[:].rearrange("p a b -> p (a b)"), AF.Exp)
            kb.v("tensor_reduce", srd[:].rearrange("p (a b) -> p a b", a=16),
                 sc[:], AX.X, ALU.add)
            kb.v("reciprocal", srd[:], srd[:])
            for h in range(16):
                kb.v("tensor_scalar", sc[:, h, :], sc[:, h, :],
                     srd[:, h:h + 1], None, op0=ALU.mult)
            for h in range(16):
                acc = aat[:, h * 64:(h + 1) * 64]
                vg = qkvb[:, 128:192]
                kb.v("tensor_scalar", acc, vg,
                     sc[:, h, 0:1].rearrange("p a -> p a"), None, op0=ALU.mult,
                     xr=["G2:a"], xw=["G2:a"])
                for g in range(1, 16):
                    vg = qkvb[:, g * 192 + 128:g * 192 + 192]
                    kb.v("scalar_tensor_tensor", acc, vg,
                         sc[:, h, g:g + 1].rearrange("p a -> p a"), acc,
                         op0=ALU.mult, op1=ALU.add,
                         xr=["G2:a"], xw=["G2:a"])
            kb.v("tensor_copy", aab[:], aat[:], xr=["G2:a"], xw=["G2:a"])
            for ic in range(NC_D):
                kb.trans(ptb[:], aab[:, ic * 128:(ic + 1) * 128], ident[:], xr=["G2:a"], xw=["G2:a"])
                kb.v("tensor_copy", otf[:, ic * 128:(ic + 1) * 128], ptb[:], xr=["G2:a"], xw=["G2:a"])
            for half in range(2):
                ps = pq[half]
                for ic in range(NC_D):
                    kb.mm(ps[:], otf[:, ic * 128:(ic + 1) * 128],
                          wproj[:, ic, half * 512:(half + 1) * 512],
                          start=(ic == 0), stop=(ic == NC_D - 1),
                          xr=["G2:a", "AG:wp"])
                kb.v("tensor_copy", ptm[:, half * 512:(half + 1) * 512], ps[:], xr=["G2:a"], xw=["G2:a"])
            for dc in range(NC_D):
                kb.trans(ptb[:], ptm[:, dc * 128:(dc + 1) * 128], ident[:], xr=["G2:a"], xw=["G2:a"])
                kb.v("tensor_scalar", s2[:, dc, t0:t0 + 128], ptb[:],
                     bpc[:, dc, :], None, op0=ALU.add)
        kb.act(sigs[:], sig[:], AF.Sign)
        kb.trans(ptb[0:32, 0:8], sigs[:], ident[0:8, 0:8])
        kb.v("tensor_copy", sigt[:, 0:8], ptb[0:32, 0:8])
        for tt in range(NTT):
            kb.mm(pq[2][:, 0:8], pet[:, tt * 128:(tt + 1) * 128], sigt[:, 0:8])
            kb.v("tensor_copy", msk[:, tt, 0:8], pq[2][:, 0:8])
            kb.v("tensor_reduce", srd[:, 0:1].rearrange("p (a b) -> p a b", a=1),
                 msk[:, tt, 0:8].rearrange("p (a b) -> p a b", a=1),
                 AX.X, ALU.max)
            kb.v("tensor_tensor", msk[:, tt, 0:8], msk[:, tt, 0:8],
                 srd[:, 0:1].broadcast_to([128, 8]), op=ALU.is_equal)
            kb.v("tensor_copy", mbf[:], msk[:, tt, 0:8])
            kb.trans(ptb[0:8, 0:128], mbf[:], ident[:])
            kb.v("tensor_copy", m8[0:8, tt * 128:(tt + 1) * 128],
                 ptb[0:8, 0:128])
        yacv = AG[:, 0:NC_D * W].rearrange("p (c f) -> p c f", c=NC_D)
        kb.fence(r=["AG:wq", "AG:wp"], w=["AG:yac"])
        for th in range(NTH):
            t0 = th * W
            pse = pq[4]
            for t in range(8):
                kb.dma("gpsimd", wts[:],
                       wt_in[t].rearrange("(c p) f -> p c f", p=128))
                kb.mm(pse[:, 0:W], selm[:, t * 128:(t + 1) * 128],
                      m8[:, t0:t0 + W])
                kb.v("tensor_copy", gbv[:, 0:W], pse[:, 0:W])
                for dc in range(NC_D):
                    kb.v("tensor_mul", xe[:, dc, :], xn[:, dc, t0:t0 + W],
                         gbv[:, 0:W], xr=["G2:a"], xw=["G2:a"])
                for oc in range(NC_D):
                    ps = pq[oc % 2]
                    for dc in range(NC_D):
                        kb.mm(ps[:, 0:W], wts[:, dc, oc * 128:(oc + 1) * 128],
                              xe[:, dc, :], start=(dc == 0),
                              stop=(dc == NC_D - 1), xr=["G2:a"], xw=["G2:a"])
                    if t == 0:
                        kb.v("tensor_copy", yacv[:, oc, :], ps[:, 0:W],
                             xw=["AG:yac"])
                    else:
                        kb.v("tensor_add", yacv[:, oc, :], yacv[:, oc, :],
                             ps[:, 0:W], xr=["AG:yac"], xw=["AG:yac"])
            for oc in range(NC_D):
                ps = pq[2 + (oc % 2)]
                kb.mm(ps[:, 0:W], btb[:, oc * 128:(oc + 1) * 128],
                      m8[:, t0:t0 + W])
                kb.v("tensor_add", yacv[:, oc, :], yacv[:, oc, :], ps[:, 0:W],
                     xr=["AG:yac"], xw=["AG:yac"])
                kb.act(silb[:, 0:W], yacv[:, oc, :], AF.Silu, xr=["AG:yac"])
                kb.v("tensor_mul", s2[:, oc, t0:t0 + W],
                     s2[:, oc, t0:t0 + W], silb[:, 0:W])
        x3v = AG[:, 0:NC_D * TOK].rearrange("p (c t) -> p c t", c=NC_D)
        kb.fence(r=["AG:yac"], w=["AG:x3"])
        for dc in range(NC_D):
            kb.dma("sync", x3v[:, dc], x_in[dc * 128:(dc + 1) * 128, :],
                   xw=["AG:x3"])
            kb.v("tensor_add", x3v[:, dc], x3v[:, dc], s2[:, dc],
                 xr=["AG:x3"], xw=["AG:x3"])
            kb.dma("sync", x3_out[dc * 128:(dc + 1) * 128, :], x3v[:, dc],
                   xr=["AG:x3"])
        kb.fence(r=["G2:a"], w=["G2:rows"])
        emit_ln(kb, C, x3v, xn[:], TOK, "ln3", xbc[:], sqc[:], bcb[:],
                tmpc[:], pq[0][:], pq[1][:], xrole=["AG:x3"],
                rows_ap=AG2[0:1, 0:6144], rrole=["G2:rows"])
        kb.dma("sync", xn3_out.rearrange("(c p) t -> p c t", p=128), xn[:])
        if standalone:
            kb.finalize()
    return nc


def _pe_table(S):
    import math
    pos = np.arange(S, dtype=np.float32)[:, None]
    div = np.exp(np.arange(0, 32, 2, dtype=np.float32) * (-math.log(10000.0) / 32))
    ang = pos * div
    pe = np.zeros((S, 32), np.float32)
    pe[:, 0::2] = np.sin(ang)
    pe[:, 1::2] = np.cos(ang)
    return pe


def run_L2(inputs, TOK, runner, x2_sh, xn2_sh):
    CONST = _const_inputs()
    S = 4 * TOK
    pe = _pe_table(S)
    WqkvT = np.ascontiguousarray(np.asarray(inputs["L2_attn_Wqkv"]).T)
    WprojT = np.ascontiguousarray(np.asarray(inputs["L2_attn_Wproj"]).T)
    bprojc = np.asarray(inputs["L2_attn_bproj"]).reshape(NC_D, 128).T.copy()
    sigs = np.ascontiguousarray(np.asarray(inputs["L2_ffn_tile_sigs"]))
    WtT = np.ascontiguousarray(np.asarray(inputs["L2_ffn_W"]).transpose(0, 2, 1))
    bt = np.ascontiguousarray(np.asarray(inputs["L2_ffn_b"]))
    selmat = np.zeros((8, 1024), np.float32)
    for t in range(8):
        selmat[t, t * 128:(t + 1) * 128] = 1.0
    maps = []
    for c in range(NCORES):
        b, p = c // 4, c % 4
        maps.append({
            "xn2": xn2_sh[c], "x2": x2_sh[c],
            "WqkvT": WqkvT, "WprojT": WprojT, "bprojc": bprojc,
            "tile_sigs": sigs,
            "peT": np.ascontiguousarray(pe[p * TOK:(p + 1) * TOK].T),
            "WtT": WtT, "bt": bt, "selmat": selmat,
            "ident": np.eye(128, dtype=np.float32), **CONST,
        })
    resG = runner(build_G(TOK), maps)
    return ([np.asarray(r["x3"]) for r in resG],
            [np.asarray(r["xn3"]) for r in resG])



def build_H(TOK, ctx=None):
    CH = min(512, TOK)
    NCH = TOK // CH
    TPC = CH // 128
    NTT = TOK // 128
    GC = GAMMA ** CH
    nc, es0, kb0, reg, standalone = _ctx_or_new(ctx)
    xn_in = _dio(nc, reg, "xn3", [D, TOK], BF16, "ExternalInput")
    wk_in = _dio(nc, reg, "Wk", [D, D], F32, "ExternalInput")
    wv_in = _dio(nc, reg, "Wv", [D, D], F32, "ExternalInput")
    gc_in = _dio(nc, reg, "gcols", [128, NTT], F32, "ExternalInput")
    t_out = _dio(nc, reg, "Tcore", [D, D], F32, "ExternalOutput")
    es = contextlib.ExitStack()
    with es:
        if standalone:
            kb = KB(nc, es)
        else:
            kb = kb0
            kb.es = es
        xn = es.enter_context(nc.sbuf_tensor("H_xn_sb", [128, NC_D, TOK], BF16))
        wk = es.enter_context(nc.sbuf_tensor("H_wk", [128, NC_D, 1024], BF16))
        wv = es.enter_context(nc.sbuf_tensor("H_wv", [128, NC_D, 1024], BF16))
        gcols = es.enter_context(nc.sbuf_tensor("H_gcs", [128, NTT], F32))
        kg = es.enter_context(nc.sbuf_tensor("H_kg", [128, 4, 1024], BF16))
        vt = es.enter_context(nc.sbuf_tensor("H_vt", [128, 4, 1024], BF16))
        S = es.enter_context(nc.sbuf_tensor("H_S", [128, NC_D, 1024], F32))
        ps = [es.enter_context(nc.psum_tensor(f"H_ps{i}", [128, 512], F32))
              for i in range(4)]
        kb.dma("sync", xn[:], xn_in.rearrange("(c p) t -> p c t", p=128))
        kb.dma("gpsimd", wk[:], wk_in.rearrange("(c p) f -> p c f", p=128))
        kb.dma("gpsimd", wv[:], wv_in.rearrange("(c p) f -> p c f", p=128))
        kb.dma("sync", gcols[:], gc_in)
        for m in range(NCH):
            for ti in range(TPC):
                t = m * TPC + ti
                for half in range(2):
                    p1, p2 = ps[2 * half], ps[2 * half + 1]
                    for dc in range(NC_D):
                        kb.mm(p1[:], xn[:, dc, t * 128:(t + 1) * 128],
                              wk[:, dc, half * 512:(half + 1) * 512],
                              start=(dc == 0), stop=(dc == NC_D - 1))
                        kb.mm(p2[:], xn[:, dc, t * 128:(t + 1) * 128],
                              wv[:, dc, half * 512:(half + 1) * 512],
                              start=(dc == 0), stop=(dc == NC_D - 1))
                    kb.v("tensor_scalar",
                         kg[:, ti, half * 512:(half + 1) * 512], p1[:],
                         gcols[:, t:t + 1], None, op0=ALU.mult)
                    kb.v("tensor_copy", vt[:, ti, half * 512:(half + 1) * 512],
                         p2[:])
            for kdc in range(NC_D):
                for half in range(2):
                    p1 = ps[2 * half]
                    for ti in range(TPC):
                        kb.mm(p1[:], kg[:, ti, kdc * 128:(kdc + 1) * 128],
                              vt[:, ti, half * 512:(half + 1) * 512],
                              start=(ti == 0), stop=(ti == TPC - 1))
                    if m == 0:
                        kb.v("tensor_copy",
                             S[:, kdc, half * 512:(half + 1) * 512], p1[:])
                    else:
                        kb.v("scalar_tensor_tensor",
                             S[:, kdc, half * 512:(half + 1) * 512],
                             S[:, kdc, half * 512:(half + 1) * 512], GC,
                             p1[:], op0=ALU.mult, op1=ALU.add)
        kb.dma("sync", t_out.rearrange("(c p) f -> p c f", p=128), S[:])
        if standalone:
            kb.finalize()
    return nc



def build_I(TOK, ctx=None):
    CH = min(512, TOK)
    NCH = TOK // CH
    TPC = CH // 128
    NTT = TOK // 128
    W = min(512, TOK)
    NTH = TOK // W
    GC = GAMMA ** CH
    nc, es0, kb0, reg, standalone = _ctx_or_new(ctx)
    xn_in = _dio(nc, reg, "xn3", [D, TOK], BF16, "ExternalInput")
    x_in = _dio(nc, reg, "x3", [D, TOK], F32, "ExternalInput")
    wq_in = _dio(nc, reg, "Wq", [D, D], F32, "ExternalInput")
    wk_in = _dio(nc, reg, "Wk", [D, D], F32, "ExternalInput")
    wv_in = _dio(nc, reg, "Wv", [D, D], F32, "ExternalInput")
    ts_in = None if (reg is not None and "tall" in reg) else \
        _dio(nc, reg, "Tsel", [3, D, D], F32, "ExternalInput")
    cv_in = _dio(nc, reg, "cvecb", [128, 4], F32, "ExternalInput") \
        if ts_in is None else None
    gc_in = _dio(nc, reg, "gcols", [128, NTT], F32, "ExternalInput")
    gq_in = _dio(nc, reg, "gqrow", [1, CH], F32, "ExternalInput")
    mk_in = _dio(nc, reg, "maskT", [CH, CH], F32, "ExternalInput")
    wg_in = _dio(nc, reg, "Wg3", [D, 8], F32, "ExternalInput")
    bg_in = _dio(nc, reg, "bg", [1, 8], F32, "ExternalInput")
    w1_in = _dio(nc, reg, "W1", [8, D, 4096], F32, "ExternalInput")
    b1_in = _dio(nc, reg, "b1T", [4096, 8], F32, "ExternalInput")
    w2_in = _dio(nc, reg, "W2", [8, 4096, D], F32, "ExternalInput")
    b2_in = _dio(nc, reg, "b2", [8, D], F32, "ExternalInput")
    sel_in = _dio(nc, reg, "selmat", [8, 1024], F32, "ExternalInput")
    id_in = _dio(nc, reg, "ident", [128, 128], F32, "ExternalInput")
    ones_bf_col = _dio(nc, reg, "ones_bf_col", [128, 1], F32, "ExternalInput")
    ones_f32_row = _dio(nc, reg, "ones_f32_row", [1, 128], F32, "ExternalInput")
    s3_dram = nc.dram_tensor("s3_scr", [D, TOK], BF16, kind="Internal").ap()
    x4_out = _dio(nc, reg, "x4", [D, TOK], F32, "ExternalOutput")
    es = contextlib.ExitStack()
    with es:
        if standalone:
            kb = KB(nc, es)
        else:
            kb = kb0
            kb.es = es
        xn = es.enter_context(nc.sbuf_tensor("I_xn_sb", [128, NC_D, TOK], BF16))
        s3c = es.enter_context(nc.sbuf_tensor("I_s3c", [128, NC_D, CH], BF16))
        AI2 = es.enter_context(nc.sbuf_tensor("I_AI2", [128, 6144], F32))
        kb.suppress.add("I_AI2")
        AI2b = AI2[:].bitcast(BF16)
        qf = AI2b[:, 0:NC_D * CH].rearrange("p (c f) -> p c f", c=NC_D)
        qg = AI2b[:, 4096:4096 + NC_D * CH].rearrange("p (c f) -> p c f", c=NC_D)
        kf = AI2b[:, 8192:8192 + NC_D * CH].rearrange("p (c f) -> p c f", c=NC_D)
        kg = es.enter_context(nc.sbuf_tensor("I_kg", [128, TPC, 1024], BF16))
        vt = es.enter_context(nc.sbuf_tensor("I_vt", [128, TPC, 1024], BF16))
        S = es.enter_context(nc.sbuf_tensor("I_S", [128, NC_D, 1024], F32))
        Sbv = es.enter_context(nc.sbuf_tensor("I_Sbv", [128, NC_D, 128], BF16))
        stb = es.enter_context(nc.sbuf_tensor("I_stb", [128, TPC, CH], BF16))
        mskb = es.enter_context(nc.sbuf_tensor("I_mskb", [128, TPC, CH], BF16))
        gqb = es.enter_context(nc.sbuf_tensor("I_gqb", [128, CH], BF16))
        gqr = es.enter_context(nc.sbuf_tensor("I_gqr", [1, CH], F32))
        AI = es.enter_context(nc.sbuf_tensor("I_AI", [128, 24576], BF16))
        kb.suppress.add("I_AI")
        wqs = AI[:, 0:8192].rearrange("p (c f) -> p c f", c=NC_D)
        wks = AI[:, 8192:16384].rearrange("p (c f) -> p c f", c=NC_D)
        wvs = AI[:, 16384:24576].rearrange("p (c f) -> p c f", c=NC_D)
        gcols = es.enter_context(nc.sbuf_tensor("I_gcs", [128, NTT], F32))
        wg_sb = es.enter_context(nc.sbuf_tensor("I_wg_sb", [128, NC_D, 8], BF16))
        bgb = es.enter_context(nc.sbuf_tensor("I_bgb", [128, 8], F32))
        bgr = es.enter_context(nc.sbuf_tensor("I_bgr", [1, 8], F32))
        lg = es.enter_context(nc.sbuf_tensor("I_lg", [128, NTT, 8], F32))
        w1t = es.enter_context(nc.sbuf_tensor("I_w1t", [128, NTT, 1], F32))
        msk = es.enter_context(nc.sbuf_tensor("I_msk", [128, NTT, 8], F32))
        m8 = es.enter_context(nc.sbuf_tensor("I_m8", [8, TOK], BF16))
        m8c = es.enter_context(nc.sbuf_tensor("I_m8c", [8, TOK], BF16))
        mbf = es.enter_context(nc.sbuf_tensor("I_mbf", [128, 8], BF16))
        wrow = es.enter_context(nc.sbuf_tensor("I_wrow", [1, TOK], BF16))
        wtb = es.enter_context(nc.sbuf_tensor("I_wtb", [128, W], BF16))
        gsc = es.enter_context(nc.sbuf_tensor("I_gsc", [128, 64], F32))
        selm = es.enter_context(nc.sbuf_tensor("I_selm", [8, 1024], BF16))
        b1t = es.enter_context(nc.sbuf_tensor("I_b1t", [128, 32, 8], F32))
        sb1 = es.enter_context(nc.sbuf_tensor("I_sb1", [128, 32, 8], BF16))
        b2b = es.enter_context(nc.sbuf_tensor("I_b2b", [8, 1024], BF16))
        cst = es.enter_context(nc.sbuf_tensor("I_cst", [8, 1024], BF16))
        crow1 = es.enter_context(nc.sbuf_tensor("I_crow1", [1, 1024], BF16))
        wsl = AI[:, 0:16384]
        hsb = AI[:, 16384:16384 + 16 * W].rearrange("p (c f) -> p c f", c=16)
        yac = AI2[:, 0:NC_D * W // 2 * 2][:, 0:NC_D * W].rearrange(
            "p (c f) -> p c f", c=NC_D)
        xe = AI2b[:, 8192:8192 + NC_D * W].rearrange("p (c f) -> p c f", c=NC_D)
        gbv = es.enter_context(nc.sbuf_tensor("I_gbv", [128, W], BF16))
        silb = es.enter_context(nc.sbuf_tensor("I_silb", [128, W], BF16))
        tmpw = es.enter_context(nc.sbuf_tensor("I_tmpw", [128, W], F32))
        xch = es.enter_context(nc.sbuf_tensor("I_xch", [128, W], F32))
        ident = es.enter_context(nc.sbuf_tensor("I_ident_sb", [128, 128], BF16))
        cbf = es.enter_context(nc.sbuf_tensor("I_cbf", [128, 1], BF16))
        crow = es.enter_context(nc.sbuf_tensor("I_crow", [1, 128], F32))
        crbf = es.enter_context(nc.sbuf_tensor("I_crbf", [1, 128], BF16))
        pq = [es.enter_context(nc.psum_tensor(f"I_pq{i}", [128, 512], F32))
              for i in range(6)]
        ptb = es.enter_context(nc.psum_tensor("I_ptb", [128, 128], BF16))
        wsl1 = wsl.rearrange("p (c f) -> p c f", c=NC_D)
        wsl2 = wsl.rearrange("p (c f) -> p c f", c=16)
        Tch = wsl.bitcast(F32)[:, 0:4096].rearrange(
            "p (c f) -> p c f", c=NC_D)
        kb.dma("sync", crow[:], ones_f32_row)
        kb.dma("gpsimd", crbf[:], ones_f32_row)
        kb.dma("gpsimd", cbf[:], ones_bf_col)
        kb.dma("gpsimd", ident[:], id_in)
        kb.dma("sync", xn[:], xn_in.rearrange("(c p) t -> p c t", p=128))
        kb.dma("sync", gcols[:], gc_in)
        kb.dma("sync", gqr[:], gq_in)
        kb.mm(pq[0][:, 0:CH], crow[:], gqr[:])
        kb.v("tensor_copy", gqb[:], pq[0][:, 0:CH])
        kb.dma("gpsimd", mskb[:], mk_in.rearrange("(c p) f -> p c f", p=128))
        if ts_in is None:
            cvsb = es.enter_context(nc.sbuf_tensor("I_cv", [128, 4], F32))
            kb.dma("sync", cvsb[:], cv_in)
            for i2 in range(4):
                for half in range(2):
                    kb.dma("sync", Tch[:],
                           reg["tall"][i2][:, half * 512:(half + 1) * 512
                                           ].rearrange("(c p) f -> p c f",
                                                       p=128),
                           xr=["AI:f"], xw=["AI:f"])
                    for kdc in range(NC_D):
                        dst = S[:, kdc, half * 512:(half + 1) * 512]
                        if i2 == 0:
                            kb.v("tensor_scalar", dst, Tch[:, kdc, :],
                                 cvsb[:, i2:i2 + 1], None, op0=ALU.mult,
                                 xr=["AI:f"], xw=["AI:f"])
                        else:
                            kb.v("scalar_tensor_tensor", dst, Tch[:, kdc, :],
                                 cvsb[:, i2:i2 + 1], dst, op0=ALU.mult,
                                 op1=ALU.add, xr=["AI:f"], xw=["AI:f"])
        else:
            first = True
            for k in range(3):
                coef = float(np.float64(GAMMA) ** (TOK * k))
                if coef < 1e-38:
                    continue
                for half in range(2):
                    kb.dma("sync", Tch[:],
                           ts_in[k][:, half * 512:(half + 1) * 512].rearrange(
                               "(c p) f -> p c f", p=128), xr=["AI:f"], xw=["AI:f"])
                    for kdc in range(NC_D):
                        dst = S[:, kdc, half * 512:(half + 1) * 512]
                        if first:
                            kb.v("tensor_scalar", dst, Tch[:, kdc, :], coef, None,
                                 op0=ALU.mult, xr=["AI:f"], xw=["AI:f"])
                        else:
                            kb.v("scalar_tensor_tensor", dst, Tch[:, kdc, :],
                                 coef, dst, op0=ALU.mult, op1=ALU.add, xr=["AI:f"], xw=["AI:f"])
                first = False
            if first:
                kb.v("memset", S[:].rearrange("p c f -> p (c f)"), 0.0)
        kb.fence(r=["AI:f"], w=["AI:w"])
        kb.dma("gpsimd", wqs[:], wq_in.rearrange("(c p) f -> p c f", p=128), xr=["AI:w"], xw=["AI:w"])
        kb.dma("gpsimd", wks[:], wk_in.rearrange("(c p) f -> p c f", p=128), xr=["AI:w"], xw=["AI:w"])
        kb.dma("gpsimd", wvs[:], wv_in.rearrange("(c p) f -> p c f", p=128), xr=["AI:w"], xw=["AI:w"])
        for m in range(NCH):
            c0 = m * CH
            for dc in range(NC_D):
                ps = pq[dc % 2]
                for d2 in range(NC_D):
                    kb.mm(ps[:, 0:CH], wqs[:, d2, dc * 128:(dc + 1) * 128],
                          xn[:, d2, c0:c0 + CH],
                          start=(d2 == 0), stop=(d2 == NC_D - 1), xr=["AI:w"], xw=["AI:w"])
                kb.v("tensor_copy", qf[:, dc, :], ps[:, 0:CH], xr=["AI2:r"], xw=["AI2:r"])
                kb.v("tensor_mul", qg[:, dc, :], qf[:, dc, :], gqb[:, 0:CH], xr=["AI2:r"], xw=["AI2:r"])
                ps2 = pq[2 + dc % 2]
                for d2 in range(NC_D):
                    kb.mm(ps2[:, 0:CH], wks[:, d2, dc * 128:(dc + 1) * 128],
                          xn[:, d2, c0:c0 + CH],
                          start=(d2 == 0), stop=(d2 == NC_D - 1), xr=["AI:w"], xw=["AI:w"])
                kb.v("tensor_copy", kf[:, dc, :], ps2[:, 0:CH], xr=["AI2:r"], xw=["AI2:r"])
            for ti in range(TPC):
                t = m * TPC + ti
                for half in range(2):
                    ps = pq[4 + half]
                    for dc in range(NC_D):
                        kb.mm(ps[:], xn[:, dc, t * 128:(t + 1) * 128],
                              wvs[:, dc, half * 512:(half + 1) * 512],
                              start=(dc == 0), stop=(dc == NC_D - 1), xr=["AI:w"], xw=["AI:w"])
                    kb.v("tensor_copy", vt[:, ti, half * 512:(half + 1) * 512],
                         ps[:])
                    ps3 = pq[half]
                    for dc in range(NC_D):
                        kb.mm(ps3[:], xn[:, dc, t * 128:(t + 1) * 128],
                              wks[:, dc, half * 512:(half + 1) * 512],
                              start=(dc == 0), stop=(dc == NC_D - 1), xr=["AI:w"], xw=["AI:w"])
                    kb.v("tensor_scalar", kg[:, ti, half * 512:(half + 1) * 512],
                         ps3[:], gcols[:, t:t + 1], None, op0=ALU.mult)
            for jt in range(TPC):
                ps = pq[jt % 2]
                for dc in range(NC_D):
                    kb.mm(ps[:, 0:CH], kf[:, dc, jt * 128:(jt + 1) * 128],
                          qf[:, dc, :], start=(dc == 0), stop=(dc == NC_D - 1), xr=["AI2:r"], xw=["AI2:r"])
                kb.v("tensor_mul", stb[:, jt, :], ps[:, 0:CH], mskb[:, jt, :])
            for vc in range(NC_D):
                for kdc in range(NC_D):
                    kb.v("tensor_copy", Sbv[:, kdc, :],
                         S[:, kdc, vc * 128:(vc + 1) * 128])
                ps = pq[4 + (vc % 2)]
                for jt in range(TPC):
                    kb.mm(ps[:, 0:CH], vt[:, jt, vc * 128:(vc + 1) * 128],
                          stb[:, jt, :], start=(jt == 0), stop=False)
                for kdc in range(NC_D):
                    kb.mm(ps[:, 0:CH], Sbv[:, kdc, :], qg[:, kdc, :],
                          start=False, stop=(kdc == NC_D - 1), xr=["AI2:r"], xw=["AI2:r"])
                kb.v("tensor_copy", s3c[:, vc, :], ps[:, 0:CH])
            kb.dma("sync",
                   s3_dram.rearrange("(c p) t -> p c t", p=128)[:, :,
                       c0:c0 + CH], s3c[:])
            for kdc in range(NC_D):
                for half in range(2):
                    ps = pq[half]
                    for ti in range(TPC):
                        kb.mm(ps[:], kg[:, ti, kdc * 128:(kdc + 1) * 128],
                              vt[:, ti, half * 512:(half + 1) * 512],
                              start=(ti == 0), stop=(ti == TPC - 1))
                    kb.v("scalar_tensor_tensor",
                         S[:, kdc, half * 512:(half + 1) * 512],
                         S[:, kdc, half * 512:(half + 1) * 512], GC,
                         ps[:], op0=ALU.mult, op1=ALU.add)
        kb.dma("gpsimd", wg_sb[:], wg_in.rearrange("(c p) e -> p c e", p=128))
        kb.dma("sync", bgr[:], bg_in)
        kb.mm(pq[0][:, 0:8], crow[:], bgr[:])
        kb.v("tensor_copy", bgb[:], pq[0][:, 0:8])
        for tt in range(NTT):
            for dc in range(NC_D):
                kb.mm(pq[1][:, 0:8], xn[:, dc, tt * 128:(tt + 1) * 128],
                      wg_sb[:, dc, :], start=(dc == 0), stop=(dc == NC_D - 1))
            kb.v("tensor_copy", lg[:, tt, :], pq[1][:, 0:8])
        G2 = gsc
        for tt in range(NTT):
            l = lg[:, tt, :]
            m1 = G2[:, 0:1]; sc2 = G2[:, 8:16]; ssum = G2[:, 1:2]
            kb.v("tensor_add", sc2, l, bgb[:])
            kb.v("tensor_reduce", m1.rearrange("p (a b) -> p a b", a=1),
                 sc2.rearrange("p (a b) -> p a b", a=1), AX.X, ALU.max)
            kb.v("tensor_sub", sc2, sc2, m1.broadcast_to([128, 8]))
            kb.act(sc2, sc2, AF.Exp)
            kb.v("tensor_reduce", ssum.rearrange("p (a b) -> p a b", a=1),
                 sc2.rearrange("p (a b) -> p a b", a=1), AX.X, ALU.add)
            kb.v("reciprocal", ssum, ssum)
            kb.v("tensor_scalar", sc2, sc2, ssum, None, op0=ALU.mult)
            kb.v("tensor_reduce", w1t[:, tt, :].rearrange("p (a b) -> p a b", a=1),
                 sc2.rearrange("p (a b) -> p a b", a=1), AX.X, ALU.max)
            kb.v("tensor_tensor", msk[:, tt, :], sc2,
                 w1t[:, tt, :].broadcast_to([128, 8]), op=ALU.is_equal)
            kb.v("tensor_copy", mbf[:], msk[:, tt, :])
            kb.trans(ptb[0:8, 0:128], mbf[:], ident[:])
            kb.v("tensor_copy", m8[0:8, tt * 128:(tt + 1) * 128],
                 ptb[0:8, 0:128])
        kb.v("tensor_scalar", m8c[0:8, :], m8[0:8, :], -1.0, 1.0,
             op0=ALU.mult, op1=ALU.add)
        kb.dma("gpsimd", selm[:], sel_in)
        kb.dma("gpsimd", b2b[:], b2_in)
        kb.dma("sync", b1t[:], b1_in.rearrange("(c p) e -> p c e", p=128))
        kb.act(sb1[:].rearrange("p c e -> p (c e)"),
               b1t[:].rearrange("p c e -> p (c e)"), AF.Silu)
        kb.fence(r=["AI:w", "AI2:r"],
                 w=["AI:f", "AI:y", "AI2:f", "AI2:g", "AI2:z0", "AI2:z1",
                    "AI2:c0", "AI2:c1"])
        for tt in range(NTT):
            kb.v("tensor_copy", mbf[:, 0:1], w1t[:, tt, :])
            kb.trans(ptb[0:1, 0:128], mbf[:, 0:1], ident[:])
            kb.v("tensor_copy", wrow[:, tt * 128:(tt + 1) * 128],
                 ptb[0:1, 0:128])
        PW3 = min(1024, TOK)
        NP3 = TOK // PW3
        NW3 = PW3 // W
        h2v = S[:].rearrange("p c f -> p (c f)").bitcast(BF16).rearrange(
            "p (c f) -> p c f", c=16 * NW3)
        _ypools = [(kg[:].rearrange("p c f -> p (c f)"), TPC * 1024, None),
                   (vt[:].rearrange("p c f -> p (c f)"), TPC * 1024, None),
                   (AI[:, 16384:24576], 8192, "AI:y")]
        _yslabs = []
        _pi, _off = 0, 0
        for _oc in range(NC_D):
            while _off + TOK > _ypools[_pi][1]:
                _pi += 1
                _off = 0
            _yslabs.append((_ypools[_pi][0][:, _off:_off + TOK],
                            _ypools[_pi][2]))
            _off += TOK

        def yslab(oc):
            return _yslabs[oc][0]

        def yrole(oc):
            r = _yslabs[oc][1]
            return [r] if r else []
        gbv2 = AI2b[:, 8192:8192 + NW3 * W].rearrange(
            "p (c f) -> p c f", c=NW3)
        zsc2 = AI2b[:, 9216:9216 + 2 * W].rearrange(
            "p (c f) -> p c f", c=2)
        w1s3 = AI2b[:, 10240:12288]
        kb.fence(w=["I_S", "AI3:h"])
        kb.suppress.add("I_S")
        for e in range(8):
            for hh in range(2):
                c0 = hh * 2048
                kb.dma("gpsimd", wsl2[:, 0:16, :],
                       w2_in[e][c0:c0 + 2048, :].rearrange(
                           "(c p) f -> p c f", p=128), xr=["AI:f"],
                       xw=["AI:f"])
                for hc in range(16):
                    hcg = hh * 16 + hc
                    for half in range(2):
                        kb.mm(pq[4 + half][0:1, 0:512],
                              sb1[:, hcg:hcg + 1, e:e + 1].rearrange(
                                  "p a b -> p (a b)"),
                              wsl2[:, hc, half * 512:(half + 1) * 512],
                              start=(hh == 0 and hc == 0),
                              stop=(hh == 1 and hc == 15), xr=["AI:f"])
                for p3 in range(NP3):
                    for th2 in range(NW3):
                        t0 = p3 * PW3 + th2 * W
                        kb.mm(pq[3][:, 0:W], selm[:, e * 128:(e + 1) * 128],
                              m8[:, t0:t0 + W])
                        kb.v("tensor_copy", gbv2[:, th2, :], pq[3][:, 0:W],
                             xw=["AI2:g"])
                    for hc in range(16):
                        hcg = hh * 16 + hc
                        bf = hc % 2
                        kb.dma("gpsimd",
                               w1s3[:, bf * 1024:bf * 1024 + 1024]
                               .rearrange("p (c f) -> p c f", c=NC_D),
                               w1_in[e][:, c0 + hc * 128:c0 + hc * 128 + 128]
                               .rearrange("(c p) f -> p c f", p=128),
                               xw=[f"AI2:c{bf}"])
                        for th2 in range(NW3):
                            t0 = p3 * PW3 + th2 * W
                            idx = hc * NW3 + th2
                            ps = pq[idx % 2]
                            for dc in range(NC_D):
                                kb.mm(ps[:, 0:W],
                                      w1s3[:, bf * 1024 + dc * 128:
                                           bf * 1024 + dc * 128 + 128],
                                      xn[:, dc, t0:t0 + W], start=(dc == 0),
                                      stop=(dc == NC_D - 1),
                                      xr=[f"AI2:c{bf}"])
                            kb.v("scalar_tensor_tensor", zsc2[:, idx % 2, :],
                                 ps[:, 0:W], 1.0, gbv2[:, th2, :],
                                 op0=ALU.mult, op1=ALU.mult,
                                 xr=["AI2:g"], xw=[f"AI2:z{idx % 2}"])
                            kb.act(h2v[:, idx, 0:W], zsc2[:, idx % 2, :],
                                   AF.Silu,
                                   bias=b1t[:, hcg:hcg + 1, e:e + 1]
                                   .rearrange("p a b -> p (a b)"),
                                   xr=[f"AI2:z{idx % 2}"], xw=["AI3:h"])
                    for th2 in range(NW3):
                        t0 = p3 * PW3 + th2 * W
                        for oc in range(NC_D):
                            ps = pq[2 + (oc % 2)]
                            for hc in range(16):
                                kb.mm(ps[:, 0:W],
                                      wsl2[:, hc, oc * 128:(oc + 1) * 128],
                                      h2v[:, hc * NW3 + th2, 0:W],
                                      start=(hc == 0), stop=(hc == 15),
                                      xr=["AI:f", "AI3:h"])
                            if e == 0 and hh == 0:
                                kb.v("tensor_copy", yslab(oc)[:, t0:t0 + W],
                                     ps[:, 0:W], xw=yrole(oc))
                            else:
                                kb.v("tensor_add", yslab(oc)[:, t0:t0 + W],
                                     yslab(oc)[:, t0:t0 + W], ps[:, 0:W],
                                     xr=yrole(oc), xw=yrole(oc))
            kb.v("tensor_scalar", crow1[:, 0:512],
                 pq[4][0:1, 0:512], -1.0, None, op0=ALU.mult)
            kb.v("tensor_scalar", crow1[:, 512:1024],
                 pq[5][0:1, 0:512], -1.0, None, op0=ALU.mult)
            kb.dma("gpsimd", cst[e:e + 1, :], crow1[:])
        for th in range(NTH):
            t0 = th * W
            kb.mm(pq[5][:, 0:W], crbf[:], wrow[:, t0:t0 + W])
            kb.v("tensor_copy", wtb[:, 0:W], pq[5][:, 0:W])
            for oc in range(NC_D):
                ps = pq[4]
                kb.mm(ps[:, 0:W], cst[:, oc * 128:(oc + 1) * 128],
                      m8c[:, t0:t0 + W], start=True, stop=False)
                kb.mm(ps[:, 0:W], b2b[:, oc * 128:(oc + 1) * 128],
                      m8[:, t0:t0 + W], start=False, stop=True)
                kb.v("tensor_add", yslab(oc)[:, t0:t0 + W],
                     yslab(oc)[:, t0:t0 + W], ps[:, 0:W],
                     xr=yrole(oc), xw=yrole(oc))
                kb.v("tensor_mul", tmpw[:, 0:W], yslab(oc)[:, t0:t0 + W],
                     wtb[:, 0:W], xr=yrole(oc))
                kb.act(silb[:, 0:W], tmpw[:, 0:W], AF.Silu)
                kb.dma("sync", s3c[:, oc, 0:W],
                       s3_dram[oc * 128:(oc + 1) * 128, t0:t0 + W])
                kb.v("tensor_mul", silb[:, 0:W], silb[:, 0:W], s3c[:, oc, 0:W])
                kb.dma("sync", xch[:, 0:W],
                       x_in[oc * 128:(oc + 1) * 128, t0:t0 + W])
                kb.v("tensor_add", xch[:, 0:W], xch[:, 0:W], silb[:, 0:W])
                kb.dma("sync", x4_out[oc * 128:(oc + 1) * 128, t0:t0 + W],
                       xch[:, 0:W])
        if standalone:
            kb.finalize()
    return nc


def run_L3(inputs, TOK, runner, x3_sh, xn3_sh):
    CONST = _const_inputs()
    CH = min(512, TOK)
    NTT = TOK // 128
    j = np.arange(TOK)
    gcols = (GAMMA ** ((CH - 1) - (j % CH))).astype(np.float32)
    gcols = gcols.reshape(NTT, 128).T.copy()
    i = np.arange(CH)
    gqrow = (GAMMA ** (i + 1)).astype(np.float32)[None, :].copy()
    maskT = np.where(i[None, :] >= i[:, None],
                     np.float32(GAMMA) ** (i[None, :] - i[:, None]).astype(np.float32),
                     np.float32(0.0)).astype(np.float32)
    Wq = np.ascontiguousarray(np.asarray(inputs["L3_attn_weight_q"]))
    Wk = np.ascontiguousarray(np.asarray(inputs["L3_attn_weight_k"]))
    Wv = np.ascontiguousarray(np.asarray(inputs["L3_attn_weight_v"]))
    maps = []
    for c in range(NCORES):
        maps.append({"xn3": xn3_sh[c], "Wk": Wk, "Wv": Wv, "gcols": gcols})
    resH = runner(build_H(TOK), maps)
    Ts = [np.asarray(r["Tcore"]) for r in resH]
    Wg3 = np.ascontiguousarray(np.asarray(inputs["L3_ffn_Wg"]))
    bg = np.asarray(inputs["L3_ffn_bg"])[None, :].copy()
    W1 = np.ascontiguousarray(np.asarray(inputs["L3_ffn_W1"]))
    b1T = np.ascontiguousarray(np.asarray(inputs["L3_ffn_b1"]).T)
    W2 = np.ascontiguousarray(np.asarray(inputs["L3_ffn_W2"]))
    b2 = np.ascontiguousarray(np.asarray(inputs["L3_ffn_b2"]))
    selmat = np.zeros((8, 1024), np.float32)
    for t in range(8):
        selmat[t, t * 128:(t + 1) * 128] = 1.0
    maps = []
    for c in range(NCORES):
        b, p = c // 4, c % 4
        tsel = np.zeros((3, D, D), np.float32)
        for k in range(3):
            i2 = p - 1 - k
            if i2 >= 0:
                tsel[k] = Ts[b * 4 + i2]
        maps.append({
            "xn3": xn3_sh[c], "x3": x3_sh[c], "Wq": Wq, "Wk": Wk, "Wv": Wv,
            "Tsel": tsel, "gcols": gcols, "gqrow": gqrow, "maskT": maskT,
            "Wg3": Wg3, "bg": bg, "W1": W1, "b1T": b1T, "W2": W2, "b2": b2,
            "selmat": selmat, "ident": np.eye(128, dtype=np.float32),
            "ones_bf_col": CONST["ones_bf_col"],
            "ones_f32_row": CONST["ones_f32_row"],
        })
    resI = runner(build_I(TOK), maps)
    return [np.asarray(r["x4"]) for r in resI]


def _hw_runner(nc, maps):
    res = run_bass_kernel_spmd(nc, maps, core_ids=list(range(NCORES)))
    return res.results


def kernel(**inputs):
    inputs = {k: np.asarray(v) for k, v in inputs.items()}
    x = inputs["L0_attn_representations"]
    S = x.shape[1]
    TOK = S // 4
    runner = _hw_runner
    x1, xn1 = run_L0(inputs, TOK, runner)
    x2, xn2 = run_L1(inputs, TOK, runner, x1, xn1)
    x3, xn3 = run_L2(inputs, TOK, runner, x2, xn2)
    x4 = run_L3(inputs, TOK, runner, x3, xn3)
    out = np.empty((2, S, 1024), np.float32)
    for c in range(NCORES):
        b, p = c // 4, c % 4
        out[b, p * TOK:(p + 1) * TOK, :] = x4[c].T
    return out



_FUSED_CACHE = {}


def _cc(kb, kind, groups, in_ap, out_ap):
    return kb.e("gpsimd",
                lambda E: E.collective_compute(
                    kind, ALU.bypass, replica_groups=groups,
                    ins=[in_ap], outs=[out_ap]),
                r=[in_ap], w=[out_ap], dma=True, cc=True)


def build_fused(TOK):
    if TOK in _FUSED_CACHE:
        return _FUSED_CACHE[TOK]
    TB = 4 * TOK
    NTT = TOK // 128
    nc = bass.Bass("TRN2", target_bir_lowering=False)
    es0 = contextlib.ExitStack()
    with es0:
        kb = KB(nc, es0)
        reg = {}

        def internal(name, shape, dt):
            reg[name] = nc.dram_tensor(name, shape, dt, kind="Internal").ap()
        internal("xn", [D, TOK], BF16)
        internal("xnall", [4, D, TOK], BF16)
        internal("qk", [512, TB], BF16)
        internal("v", [TB, 2048], BF16)
        internal("att", [512 * 4, TB], BF16)
        internal("rs_in", [TB, D], BF16)
        internal("rs_out", [TOK, D], BF16)
        internal("x1", [D, TOK], F32)
        internal("xn1", [D, TOK], BF16)
        internal("zcum", [TOK, 2048], BF16)
        internal("cloc", [1, NTT * 2048], F32)
        internal("hrow", [1, 2048], F32)
        internal("hall", [4, 2048], F32)
        internal("x2", [D, TOK], F32)
        internal("xn2", [D, TOK], BF16)
        internal("x3", [D, TOK], F32)
        internal("xn3", [D, TOK], BF16)
        internal("Tcore", [D, D], F32)
        internal("tall", [4, D, D], F32)
        reg["x4"] = nc.dram_tensor("x4", [D, TOK], F32,
                                   kind="ExternalOutput").ap()
        ctx = FusedCtx(nc, es0, kb, reg)
        G8 = [[0, 1, 2, 3], [4, 5, 6, 7]]
        build_A(TOK, ctx)
        kb.barrier()
        _cc(kb, "AllGather", G8,
            reg["xn"].rearrange("(a d) t -> a (d t)", a=1),
            reg["xnall"].rearrange("r d t -> r (d t)"))
        kb.barrier()
        build_B(TOK, ctx)
        kb.barrier()
        build_C(TOK, ctx)
        kb.barrier()
        kb.e("gpsimd",
             lambda E: E.collective_compute(
                 "ReduceScatter", ALU.add, replica_groups=G8,
                 ins=[reg["rs_in"].rearrange("(a t) d -> a (t d)", a=1)],
                 outs=[reg["rs_out"].rearrange("(a t) d -> a (t d)", a=1)]),
             r=[reg["rs_in"]], w=[reg["rs_out"]], dma=True, cc=True)
        kb.barrier()
        build_D(TOK, ctx)
        kb.barrier()
        build_E(TOK, ctx)
        kb.barrier()
        _cc(kb, "AllGather", G8, reg["hrow"], reg["hall"])
        kb.barrier()
        build_F(TOK, ctx)
        kb.barrier()
        build_G(TOK, ctx)
        kb.barrier()
        build_H(TOK, ctx)
        kb.barrier()
        _cc(kb, "AllGather", G8,
            reg["Tcore"].rearrange("(a d) f -> a (d f)", a=1),
            reg["tall"].rearrange("r d f -> r (d f)"))
        kb.barrier()
        build_I(TOK, ctx)
        kb.finalize()
    _FUSED_CACHE[TOK] = (nc, sorted(_REG_INPUTS.get(id(reg), set())))
    return _FUSED_CACHE[TOK]


def prep_fused(inputs, TOK):
    CONST = _const_inputs()
    S = 4 * TOK
    NTT = TOK // 128
    x = np.asarray(inputs["L0_attn_representations"])
    WcT = np.ascontiguousarray(np.asarray(inputs["L0_attn_Wc"]).T)
    bcv = np.asarray(inputs["L0_attn_bc"])
    WoT = np.ascontiguousarray(np.asarray(inputs["L0_attn_Wo"]).T)
    bo = np.asarray(inputs["L0_attn_bo"]).reshape(NC_D, 128).T.copy()
    WgT = np.ascontiguousarray(np.asarray(inputs["L0_ffn_Wg"]).T)
    W1T = np.ascontiguousarray(np.asarray(inputs["L0_ffn_W1"]).transpose(0, 2, 1))
    W2T = np.ascontiguousarray(np.asarray(inputs["L0_ffn_W2"]).transpose(0, 2, 1))
    W3T = np.ascontiguousarray(np.asarray(inputs["L0_ffn_W3"]).transpose(0, 2, 1))
    COS, SIN = _dft_consts()
    a_params = np.asarray(inputs["L1_attn_a_params"])
    apos = np.zeros(1024, np.float32); apos[1:512] = a_params[:511]
    arev = np.zeros(1024, np.float32); arev[513:] = a_params[:511][::-1]
    apos_h = apos.reshape(NC_D, 128).T.copy()
    arev_h = arev.reshape(NC_D, 128).T.copy()
    B_w = np.ascontiguousarray(np.asarray(inputs["L1_attn_B_w"]))
    C_wT = np.ascontiguousarray(np.asarray(inputs["L1_attn_C_w"]).T)
    Dv = np.asarray(inputs["L1_attn_D"]).reshape(NC_D, 128).T.copy()
    gwT = np.ascontiguousarray(np.asarray(inputs["L1_ffn_gate_w"]).T)
    eb = np.asarray(inputs["L1_ffn_expert_bias"])[None, :].copy()
    w1 = np.ascontiguousarray(np.asarray(inputs["L1_ffn_w1"]))
    w2 = np.ascontiguousarray(np.asarray(inputs["L1_ffn_w2"]))
    w3 = np.ascontiguousarray(np.asarray(inputs["L1_ffn_w3"]))
    WqkvT = np.ascontiguousarray(np.asarray(inputs["L2_attn_Wqkv"]).T)
    WprojT = np.ascontiguousarray(np.asarray(inputs["L2_attn_Wproj"]).T)
    bprojc = np.asarray(inputs["L2_attn_bproj"]).reshape(NC_D, 128).T.copy()
    sigs = np.ascontiguousarray(np.asarray(inputs["L2_ffn_tile_sigs"]))
    WtT = np.ascontiguousarray(np.asarray(inputs["L2_ffn_W"]).transpose(0, 2, 1))
    bt = np.ascontiguousarray(np.asarray(inputs["L2_ffn_b"]))
    pe = _pe_table(S)
    Wq = np.ascontiguousarray(np.asarray(inputs["L3_attn_weight_q"]))
    Wk = np.ascontiguousarray(np.asarray(inputs["L3_attn_weight_k"]))
    Wv = np.ascontiguousarray(np.asarray(inputs["L3_attn_weight_v"]))
    CH = min(512, TOK)
    jj = np.arange(TOK)
    gcols = (GAMMA ** ((CH - 1) - (jj % CH))).astype(np.float32)
    gcols = gcols.reshape(NTT, 128).T.copy()
    ii = np.arange(CH)
    gqrow = (GAMMA ** (ii + 1)).astype(np.float32)[None, :].copy()
    maskT = np.where(ii[None, :] >= ii[:, None],
                     np.float32(GAMMA) ** (ii[None, :] - ii[:, None]).astype(np.float32),
                     np.float32(0.0)).astype(np.float32)
    Wg3 = np.ascontiguousarray(np.asarray(inputs["L3_ffn_Wg"]))
    bg = np.asarray(inputs["L3_ffn_bg"])[None, :].copy()
    W1_3 = np.ascontiguousarray(np.asarray(inputs["L3_ffn_W1"]))
    b1T = np.ascontiguousarray(np.asarray(inputs["L3_ffn_b1"]).T)
    W2_3 = np.ascontiguousarray(np.asarray(inputs["L3_ffn_W2"]))
    b2 = np.ascontiguousarray(np.asarray(inputs["L3_ffn_b2"]))
    selmat = np.zeros((8, 1024), np.float32)
    for t in range(8):
        selmat[t, t * 128:(t + 1) * 128] = 1.0
    ident = np.eye(128, dtype=np.float32)
    ones512 = np.ones((1, 512), np.float32)
    maps = []
    for c in range(NCORES):
        b, p = c // 4, c % 4
        qc, kc, vc = _heads_cols(4 * p)
        selw = np.zeros((4, 4), np.float32)
        for k in range(4):
            i2 = p - 1 - k
            if i2 >= 0:
                selw[i2, k] = 1.0
        cvecb = np.zeros((128, 4), np.float32)
        for i2 in range(p):
            cv = float(np.float64(GAMMA) ** (TOK * (p - 1 - i2)))
            cvecb[:, i2] = np.float32(cv)
        maps.append({
            "x": np.ascontiguousarray(x[b, p * TOK:(p + 1) * TOK, :].T),
            "wqk": np.ascontiguousarray(WcT[:, np.concatenate([qc, kc])]),
            "wv": np.ascontiguousarray(WcT[:, vc]),
            "bqk": bcv[np.concatenate([qc, kc])][None, :].copy(),
            "bv": bcv[vc][None, :].copy(),
            "onesr": ones512, "ident": ident,
            "onesc": np.ones((128, 1), np.float32),
            "woT": WoT, "woTs": np.ascontiguousarray(WoT[vc - 2048, :])
            if False else np.ascontiguousarray(WoT[np.concatenate([vc]) - 2048, :]),
            "bo": bo, "wgT": WgT,
            "w1T": W1T, "w2T": W2T, "w3T": W3T, "selmat": selmat,
            "apos": apos_h, "arev": arev_h, "B_w": B_w,
            "cosF": COS, "sinF": SIN,
            "ut128": np.triu(np.ones((128, 128), np.float32)),
            "selw": selw, "C_wT": C_wT, "Dcol": Dv, "gwT": gwT, "ebias": eb,
            "w1": w1, "w2": w2, "w3": w3,
            "WqkvT": WqkvT, "WprojT": WprojT, "bprojc": bprojc,
            "tile_sigs": sigs,
            "peT": np.ascontiguousarray(pe[p * TOK:(p + 1) * TOK].T),
            "WtT": WtT, "bt": bt,
            "Wq": Wq, "Wk": Wk, "Wv": Wv, "cvecb": cvecb,
            "gcols": gcols, "gqrow": gqrow, "maskT": maskT,
            "Wg3": Wg3, "bg": bg, "W1": W1_3, "b1T": b1T, "W2": W2_3,
            "b2": b2, **CONST,
        })
    return maps


def kernel(**inputs):
    inputs = {k: np.asarray(v) for k, v in inputs.items()}
    x = inputs["L0_attn_representations"]
    S = x.shape[1]
    TOK = S // 4
    nc, in_names = build_fused(TOK)
    maps = prep_fused(inputs, TOK)
    for m in maps:
        missing = set(in_names) - set(m)
        extra = set(m) - set(in_names)
        assert not missing, f"missing inputs {missing}"
        for k in extra:
            del m[k]
    res = _fused_runner(nc, maps)
    out = np.empty((2, S, 1024), np.float32)
    for c in range(NCORES):
        b, p = c // 4, c % 4
        out[b, p * TOK:(p + 1) * TOK, :] = np.asarray(res[c]["x4"]).T
    return out


def _fused_runner(nc, maps):
    return run_bass_kernel_spmd(nc, maps, core_ids=list(range(NCORES))).results



_GROUP_CACHE = {}

_GROUPS = {
    "g2": (("B", "C"), {"qk": ([512, None], "BF16"), "v": ([None, 2048], "BF16")}),
}


def build_group(TOK, key, parts, internals):
    ck = (TOK, key)
    if ck in _GROUP_CACHE:
        return _GROUP_CACHE[ck]
    nc = bass.Bass("TRN2", target_bir_lowering=False)
    es0 = contextlib.ExitStack()
    with es0:
        kb = KB(nc, es0)
        reg = {}
        for name, (shape, dt) in internals.items():
            reg[name] = nc.dram_tensor(name, shape, dt, kind="Internal").ap()
        ctx = FusedCtx(nc, es0, kb, reg)
        first = True
        for fn in parts:
            if not first:
                kb.barrier()
            fn(TOK, ctx)
            first = False
        kb.finalize()
    _GROUP_CACHE[ck] = (nc, sorted(_REG_INPUTS.get(id(reg), set())))
    return _GROUP_CACHE[ck]


def kernel(**inputs):
    inputs = {k: np.asarray(v) for k, v in inputs.items()}
    x = inputs["L0_attn_representations"]
    S = x.shape[1]
    TOK = S // 4
    TB = 4 * TOK
    NTT = TOK // 128
    runner = _fused_runner

    def run_group(key, parts, internals, maps):
        nc, in_names = build_group(TOK, key, tuple(parts), internals)
        for m in maps:
            missing = set(in_names) - set(m)
            assert not missing, f"group {key} missing {missing}"
            for k in list(m):
                if k not in in_names:
                    del m[k]
        return runner(nc, maps)

    CONST = _const_inputs()
    ident = np.eye(128, dtype=np.float32)
    ones512 = np.ones((1, 512), np.float32)
    maps = []
    for c in range(NCORES):
        b, p = c // 4, c % 4
        maps.append({"x": np.ascontiguousarray(x[b, p * TOK:(p + 1) * TOK, :].T),
                     **CONST})
    r1 = run_group("g1", [build_A], {}, maps)
    xn_sh = [np.asarray(r["xn"]) for r in r1]
    WcT = np.ascontiguousarray(np.asarray(inputs["L0_attn_Wc"]).T)
    bcv = np.asarray(inputs["L0_attn_bc"])
    maps = []
    for c in range(NCORES):
        b, p = c // 4, c % 4
        qc, kc, vc = _heads_cols(4 * p)
        xnf = np.concatenate([xn_sh[b * 4 + j] for j in range(4)], axis=1)
        maps.append({
            "xnf": np.ascontiguousarray(xnf),
            "wqk": np.ascontiguousarray(WcT[:, np.concatenate([qc, kc])]),
            "wv": np.ascontiguousarray(WcT[:, vc]),
            "bqk": bcv[np.concatenate([qc, kc])][None, :].copy(),
            "bv": bcv[vc][None, :].copy(),
            "onesr": ones512, "ident": ident,
            "onesc": np.ones((128, 1), np.float32),
        })
    r2 = run_group("g2", [build_B, build_C],
                   {"qk": ([512, TB], BF16), "v": ([TB, 2048], BF16)}, maps)
    att = [np.asarray(r["att"]) for r in r2]
    WoT = np.ascontiguousarray(np.asarray(inputs["L0_attn_Wo"]).T)
    bo = np.asarray(inputs["L0_attn_bo"]).reshape(NC_D, 128).T.copy()
    WgT = np.ascontiguousarray(np.asarray(inputs["L0_ffn_Wg"]).T)
    W1T = np.ascontiguousarray(np.asarray(inputs["L0_ffn_W1"]).transpose(0, 2, 1))
    W2T = np.ascontiguousarray(np.asarray(inputs["L0_ffn_W2"]).transpose(0, 2, 1))
    W3T = np.ascontiguousarray(np.asarray(inputs["L0_ffn_W3"]).transpose(0, 2, 1))
    COS, SIN = _dft_consts()
    a_params = np.asarray(inputs["L1_attn_a_params"])
    apos = np.zeros(1024, np.float32); apos[1:512] = a_params[:511]
    arev = np.zeros(1024, np.float32); arev[513:] = a_params[:511][::-1]
    apos_h = apos.reshape(NC_D, 128).T.copy()
    arev_h = arev.reshape(NC_D, 128).T.copy()
    selmat = np.zeros((8, 1024), np.float32)
    for t in range(8):
        selmat[t, t * 128:(t + 1) * 128] = 1.0
    maps = []
    for c in range(NCORES):
        b, p = c // 4, c % 4
        rng = slice(p * TOK, (p + 1) * TOK)
        o_all = np.concatenate(
            [att[b * 4 + j][:, rng] for j in range(4)], axis=0)
        maps.append({
            "o_all": np.ascontiguousarray(o_all),
            "woT": WoT, "bo": bo,
            "x": np.ascontiguousarray(x[b, rng, :].T),
            "xn": xn_sh[c],
            "wgT": WgT, "w1T": W1T, "w2T": W2T, "w3T": W3T,
            "ident": ident, "selmat": selmat,
            "apos": apos_h, "arev": arev_h,
            "B_w": np.ascontiguousarray(np.asarray(inputs["L1_attn_B_w"])),
            "cosF": COS, "sinF": SIN,
            "ut128": np.triu(np.ones((128, 128), np.float32)),
            **CONST,
        })
    r3 = run_group("g3", [build_D, build_E], {}, maps)
    x1 = [np.asarray(r["x1"]) for r in r3]
    xn1 = [np.asarray(r["xn1"]) for r in r3]
    Hs = [np.asarray(r["hrow"]).reshape(2048) for r in r3]
    C_wT = np.ascontiguousarray(np.asarray(inputs["L1_attn_C_w"]).T)
    Dv = np.asarray(inputs["L1_attn_D"]).reshape(NC_D, 128).T.copy()
    gwT = np.ascontiguousarray(np.asarray(inputs["L1_ffn_gate_w"]).T)
    eb = np.asarray(inputs["L1_ffn_expert_bias"])[None, :].copy()
    pe = _pe_table(S)
    CH = min(512, TOK)
    jj = np.arange(TOK)
    gcols = (GAMMA ** ((CH - 1) - (jj % CH))).astype(np.float32)
    gcols = gcols.reshape(NTT, 128).T.copy()
    maps = []
    for c in range(NCORES):
        b, p = c // 4, c % 4
        hsel = np.zeros((1, 8192), np.float32)
        for k in range(4):
            i2 = p - 1 - k
            if i2 >= 0:
                hsel[0, k * 2048:(k + 1) * 2048] = Hs[b * 4 + i2]
        maps.append({
            "zcum": np.asarray(r3[c]["zcum"]),
            "cloc": np.asarray(r3[c]["cloc"]),
            "Hsel": hsel, "apos": apos_h, "arev": arev_h,
            "cosF": COS, "sinF": SIN, "C_wT": C_wT, "Dcol": Dv,
            "xn1": xn1[c], "x1": x1[c], "gwT": gwT, "ebias": eb,
            "w1": np.ascontiguousarray(np.asarray(inputs["L1_ffn_w1"])),
            "w2": np.ascontiguousarray(np.asarray(inputs["L1_ffn_w2"])),
            "w3": np.ascontiguousarray(np.asarray(inputs["L1_ffn_w3"])),
            "ident": ident, "selmat": selmat,
            "WqkvT": np.ascontiguousarray(np.asarray(inputs["L2_attn_Wqkv"]).T),
            "WprojT": np.ascontiguousarray(np.asarray(inputs["L2_attn_Wproj"]).T),
            "bprojc": np.asarray(inputs["L2_attn_bproj"]).reshape(NC_D, 128).T.copy(),
            "tile_sigs": np.ascontiguousarray(np.asarray(inputs["L2_ffn_tile_sigs"])),
            "peT": np.ascontiguousarray(pe[p * TOK:(p + 1) * TOK].T),
            "WtT": np.ascontiguousarray(np.asarray(inputs["L2_ffn_W"]).transpose(0, 2, 1)),
            "bt": np.ascontiguousarray(np.asarray(inputs["L2_ffn_b"])),
            "Wk": np.ascontiguousarray(np.asarray(inputs["L3_attn_weight_k"])),
            "Wv": np.ascontiguousarray(np.asarray(inputs["L3_attn_weight_v"])),
            "gcols": gcols,
            **CONST,
        })
    r4 = run_group("g4", [build_F, build_G, build_H],
                   {"x2": ([D, TOK], F32), "xn2": ([D, TOK], BF16)}, maps)
    x3 = [np.asarray(r["x3"]) for r in r4]
    xn3 = [np.asarray(r["xn3"]) for r in r4]
    Ts = [np.asarray(r["Tcore"]) for r in r4]
    ii = np.arange(CH)
    gqrow = (GAMMA ** (ii + 1)).astype(np.float32)[None, :].copy()
    maskT = np.where(ii[None, :] >= ii[:, None],
                     np.float32(GAMMA) ** (ii[None, :] - ii[:, None]).astype(np.float32),
                     np.float32(0.0)).astype(np.float32)
    maps = []
    for c in range(NCORES):
        b, p = c // 4, c % 4
        tsel = np.zeros((3, D, D), np.float32)
        for k in range(3):
            i2 = p - 1 - k
            if i2 >= 0:
                tsel[k] = Ts[b * 4 + i2]
        maps.append({
            "xn3": xn3[c], "x3": x3[c],
            "Wq": np.ascontiguousarray(np.asarray(inputs["L3_attn_weight_q"])),
            "Wk": np.ascontiguousarray(np.asarray(inputs["L3_attn_weight_k"])),
            "Wv": np.ascontiguousarray(np.asarray(inputs["L3_attn_weight_v"])),
            "Tsel": tsel, "gcols": gcols, "gqrow": gqrow, "maskT": maskT,
            "Wg3": np.ascontiguousarray(np.asarray(inputs["L3_ffn_Wg"])),
            "bg": np.asarray(inputs["L3_ffn_bg"])[None, :].copy(),
            "W1": np.ascontiguousarray(np.asarray(inputs["L3_ffn_W1"])),
            "b1T": np.ascontiguousarray(np.asarray(inputs["L3_ffn_b1"]).T),
            "W2": np.ascontiguousarray(np.asarray(inputs["L3_ffn_W2"])),
            "b2": np.ascontiguousarray(np.asarray(inputs["L3_ffn_b2"])),
            "selmat": selmat, "ident": ident,
            **CONST,
        })
    r5 = run_group("g5", [build_I], {}, maps)
    out = np.empty((2, S, 1024), np.float32)
    for c in range(NCORES):
        b, p = c // 4, c % 4
        out[b, p * TOK:(p + 1) * TOK, :] = np.asarray(r5[c]["x4"]).T
    return out
